# version 01
# baseline (speedup 1.0000x reference)
import contextlib
import numpy as np
import concourse.bass as bass
import concourse.mybir as mybir
from concourse.bass_utils import run_bass_kernel_spmd

F32 = mybir.dt.float32
F16 = mybir.dt.float16
AF = mybir.ActivationFunctionType
ALU = mybir.AluOpType

L = 2048
H = 1024
NST = 64
NF = 4096
T2 = 16
M2 = 32
DH = 4096

CH_SL = 64
N_SL = H // CH_SL
KSUB = 16
TGT = 512
N_TG = 2 * L // TGT

MAGIC = float(1.5 * 2 ** 23)
INV2PI = float(1.0 / (2 * np.pi))
CW1 = 6.28125
CW2 = 0.0019353628158569336
CW3 = float(2 * np.pi - CW1 - CW2)
HALFPI = float(np.pi / 2)

DEBUG = False
PHASES = "ALL"
_dbg = {}


def _host_tables():
    f16 = np.float16
    W32 = np.exp(-2j * np.pi / 32)
    B1 = np.zeros((128, 256), np.float32)
    for s in range(4):
        for t2 in range(T2):
            for m2 in range(M2):
                w = W32 ** (t2 * m2)
                B1[s * 32 + t2 * 2 + 0, s * 64 + m2 * 2 + 0] = w.real
                B1[s * 32 + t2 * 2 + 0, s * 64 + m2 * 2 + 1] = w.imag
                B1[s * 32 + t2 * 2 + 1, s * 64 + m2 * 2 + 0] = -w.imag
                B1[s * 32 + t2 * 2 + 1, s * 64 + m2 * 2 + 1] = w.real
    B1R = np.zeros((128, 512), np.float32)
    for s in range(8):
        for t2 in range(T2):
            for m2 in range(M2):
                w = W32 ** (t2 * m2)
                B1R[s * 16 + t2, s * 64 + m2 * 2 + 0] = w.real
                B1R[s * 16 + t2, s * 64 + m2 * 2 + 1] = w.imag
    Wn = np.exp(-2j * np.pi / NF)
    W128 = np.exp(-2j * np.pi / 128)
    t1 = np.arange(128)[:, None]
    m1 = np.arange(128)[None, :]
    Om = np.zeros((M2, 128, 128), complex)
    for m2 in range(M2):
        Om[m2] = (Wn ** (t1 * m2)) * (W128 ** (t1 * m1))
    V = np.conj(Wn)
    V128 = np.exp(+2j * np.pi / 128)
    Ph = np.zeros((M2, 128, 128), complex)
    for m2 in range(M2):
        Ph[m2] = (V128 ** (m1.T * t1.T)) * (V ** (t1.T * m2)) / NF
    V32 = np.exp(+2j * np.pi / 32)
    Th = np.zeros((2, 128, 128), np.float32)
    for hssub in range(4):
        for m2 in range(M2):
            k = hssub * 32 + m2
            for t2 in range(T2):
                v = V32 ** (m2 * t2)
                Th[0, k, hssub * 32 + t2 * 2 + 0] = v.real
                Th[0, k, hssub * 32 + t2 * 2 + 1] = v.imag
                Th[1, k, hssub * 32 + t2 * 2 + 0] = -v.imag
                Th[1, k, hssub * 32 + t2 * 2 + 1] = v.real
    stk = lambda a: np.ascontiguousarray(a.reshape(-1, a.shape[-1]).astype(f16))
    return dict(
        cB1=B1.astype(f16), cB1R=B1R.astype(f16),
        cOmRe=stk(Om.real), cOmIm=stk(Om.imag), cOmImN=stk(-Om.imag),
        cPhRe=stk(Ph.real), cPhIm=stk(Ph.imag), cPhImN=stk(-Ph.imag),
        cTh0=Th[0].astype(f16), cTh1=Th[1].astype(f16),
        cIdent=np.eye(128, dtype=f16),
    )


def build_nc():
    nc = bass.Bass("TRN2", target_bir_lowering=False)
    dram = nc.dram_tensor

    x_in = dram("x", [2, L, H], F32, kind="ExternalInput").ap()
    log_dt = dram("log_dt", [1, H], F32, kind="ExternalInput").ap()
    arlT = dram("arlT", [NST, H], F32, kind="ExternalInput").ap()
    aimT = dram("aimT", [NST, H], F32, kind="ExternalInput").ap()
    creT = dram("creT", [NST, H], F32, kind="ExternalInput").ap()
    cimT = dram("cimT", [NST, H], F32, kind="ExternalInput").ap()
    d_in = dram("d_vec", [1, H], F32, kind="ExternalInput").ap()
    wout = dram("wout", [H, 2 * H], F32, kind="ExternalInput").ap()
    bout = dram("bout", [1, 2 * H], F32, kind="ExternalInput").ap()
    w1T = dram("w1T", [H, DH], F32, kind="ExternalInput").ap()
    w2T = dram("w2T", [H, DH], F32, kind="ExternalInput").ap()
    h1T = dram("h1T", [DH, DH], F32, kind="ExternalInput").ap()
    h2T = dram("h2T", [DH, DH], F32, kind="ExternalInput").ap()
    w3T = dram("w3T", [DH, H], F32, kind="ExternalInput").ap()
    s1_in = dram("s1", [1, DH], F32, kind="ExternalInput").ap()
    s2_in = dram("s2", [1, DH], F32, kind="ExternalInput").ap()
    sh1_in = dram("sh1", [1, DH], F32, kind="ExternalInput").ap()
    sh2_in = dram("sh2", [1, DH], F32, kind="ExternalInput").ap()
    s3_in = dram("s3", [1, H], F32, kind="ExternalInput").ap()
    consts = {}
    for nm, shp in (("cB1", [128, 256]), ("cB1R", [128, 512]),
                    ("cOmRe", [M2 * 128, 128]), ("cOmIm", [M2 * 128, 128]),
                    ("cOmImN", [M2 * 128, 128]), ("cPhRe", [M2 * 128, 128]),
                    ("cPhIm", [M2 * 128, 128]), ("cPhImN", [M2 * 128, 128]),
                    ("cTh0", [128, 128]), ("cTh1", [128, 128]),
                    ("cIdent", [128, 128])):
        consts[nm] = dram(nm, shp, F16, kind="ExternalInput").ap()

    y_out = dram("y", [2, L, H], F32, kind="ExternalOutput").ap()

    xf16 = dram("xf16", [2 * L, H], F16, kind="Internal").ap()
    s_xnh = dram("s_xnh", [CH_SL, NF], F16, kind="Internal").ap()
    s_kern = dram("s_kern", [CH_SL, L], F16, kind="Internal").ap()
    wout16 = dram("wout16", [H, 2 * H], F16, kind="Internal").ap()
    w116 = dram("w116", [H, DH], F16, kind="Internal").ap()
    w216 = dram("w216", [H, DH], F16, kind="Internal").ap()
    h116 = dram("h116", [DH, DH], F16, kind="Internal").ap()
    h216 = dram("h216", [DH, DH], F16, kind="Internal").ap()
    w316 = dram("w316", [DH, H], F16, kind="Internal").ap()
    if DEBUG:
        dbg_xn = dram("dbg_xn", [128, 32 * 1024], F16, kind="ExternalOutput").ap()
        dbg_y2 = dram("dbg_y2", [128, 32 * 1024], F16, kind="ExternalOutput").ap()
        dbg_x1 = dram("dbg_x1", [128, 32 * 1024], F16, kind="ExternalOutput").ap()

    cnt = {}

    def tinc(instr, sem, name, n=1):
        instr.then_inc(sem, n)
        cnt[name] = cnt.get(name, 0) + n

    def inc(e, sem, name, n=1):
        try:
            en = str(e.engine)
        except Exception:
            en = ""
        if ("DVE" in en) or ("Activation" in en) or ("Pool" in en):
            e.drain()
        e.sem_inc(sem, n)
        cnt[name] = cnt.get(name, 0) + n

    st_main = contextlib.ExitStack()
    with st_main:
        TB1 = st_main.enter_context(nc.sbuf_tensor("TB1", [128, 32 * 1024], F16))
        ident = st_main.enter_context(nc.sbuf_tensor("ident", [128, 128], F16))
        b1sb = st_main.enter_context(nc.sbuf_tensor("b1sb", [128, 256], F16))
        b1rsb = st_main.enter_context(nc.sbuf_tensor("b1rsb", [128, 512], F16))
        th0sb = st_main.enter_context(nc.sbuf_tensor("th0sb", [128, 128], F16))
        th1sb = st_main.enter_context(nc.sbuf_tensor("th1sb", [128, 128], F16))
        dbc = st_main.enter_context(nc.sbuf_tensor("dbc", [128, H], F32))
        dtbc = st_main.enter_context(nc.sbuf_tensor("dtbc", [128, H], F32))
        prmA = st_main.enter_context(nc.sbuf_tensor("prmA", [128, H], F32))
        prmI = st_main.enter_context(nc.sbuf_tensor("prmI", [128, H], F32))
        prmCr = st_main.enter_context(nc.sbuf_tensor("prmCr", [128, H], F16))
        prmCi = st_main.enter_context(nc.sbuf_tensor("prmCi", [128, H], F16))
        bosb = st_main.enter_context(nc.sbuf_tensor("bosb", [128, 16], F32))
        s1sb = st_main.enter_context(nc.sbuf_tensor("s1sb", [128, 32], F32))
        s2sb = st_main.enter_context(nc.sbuf_tensor("s2sb", [128, 32], F32))
        sh1sb = st_main.enter_context(nc.sbuf_tensor("sh1sb", [128, 32], F32))
        sh2sb = st_main.enter_context(nc.sbuf_tensor("sh2sb", [128, 32], F32))
        s3sb = st_main.enter_context(nc.sbuf_tensor("s3sb", [128, 8], F32))
        row1 = st_main.enter_context(nc.sbuf_tensor("row1", [1, 2 * H], F32))
        epsc = st_main.enter_context(nc.sbuf_tensor("epsc", [128, 1], F32))
        ones1 = st_main.enter_context(nc.sbuf_tensor("ones1", [1, 128], F32))
        psA = st_main.enter_context(nc.psum_tensor("psA", [128, 512], F32))
        psB = st_main.enter_context(nc.psum_tensor("psB", [128, 512], F32))
        psC = st_main.enter_context(nc.psum_tensor("psC", [128, 512], F32))
        psD = st_main.enter_context(nc.psum_tensor("psD", [128, 512], F32))
        psE = st_main.enter_context(nc.psum_tensor("psE", [128, 512], F32))
        psF = st_main.enter_context(nc.psum_tensor("psF", [128, 512], F32))
        psG = st_main.enter_context(nc.psum_tensor("psG", [128, 512], F32))
        psH = st_main.enter_context(nc.psum_tensor("psH", [128, 512], F32))
        qS = st_main.enter_context(nc.semaphore("qS"))
        qG = st_main.enter_context(nc.semaphore("qG"))
        G = st_main.enter_context(nc.semaphore("G"))
        pP = st_main.enter_context(nc.semaphore("pP"))
        pV = st_main.enter_context(nc.semaphore("pV"))
        pS = st_main.enter_context(nc.semaphore("pS"))
        pG2 = st_main.enter_context(nc.semaphore("pG2"))
        wA = st_main.enter_context(nc.semaphore("wA"))
        wB = st_main.enter_context(nc.semaphore("wB"))
        cW = st_main.enter_context(nc.semaphore("cW"))
        xA = st_main.enter_context(nc.semaphore("xA"))
        xB = st_main.enter_context(nc.semaphore("xB"))
        blk = st_main.enter_context(nc.Block())
        class _DrainProxy:
            def __init__(self, e):
                self._e = e
            def __getattr__(self, name):
                f = getattr(self._e, name)
                if name in ("wait_ge", "sem_inc", "drain", "engine", "dma_start"):
                    return f
                if not callable(f):
                    return f
                def g(*a, **k):
                    r = f(*a, **k)
                    self._e.drain()
                    return r
                return g

        class _BlkWrap:
            def __init__(self, b):
                self._b = b
            def __getattr__(self, engname):
                dec = getattr(self._b, engname)
                if engname.startswith("_"):
                    return dec
                def mydec(fn):
                    def fn2(e):
                        if engname in ("vector", "scalar", "gpsimd"):
                            return fn(_DrainProxy(e))
                        return fn(e)
                    return dec(fn2)
                return mydec

        blk = _BlkWrap(blk)

        banks = [psA, psB, psC, psD, psE, psF, psG, psH]

        def dma(e, sem, name, out, in_, **kw):
            e.dma_start(out=out, in_=in_, **kw).then_inc(sem, 16)
            cnt[name] = cnt.get(name, 0) + 16
            return cnt[name]

        def bar(engines_inc):
            for ename in engines_inc:
                def f(e):
                    inc(e, G, "G")
                getattr(blk, ename)(f)
            tgt = cnt.get("G", 0)
            for ename in ("sync", "gpsimd", "tensor", "vector", "scalar"):
                def f(e, _t=tgt):
                    e.wait_ge(G, _t)
                getattr(blk, ename)(f)

        def cmb(e, dre, dim_, are, aim, bre, bim, tmp):
            e.tensor_tensor(tmp, aim, bim, ALU.mult)
            e.tensor_tensor(dre, are, bre, ALU.mult)
            e.tensor_tensor(dre, dre, tmp, ALU.subtract)
            e.tensor_tensor(tmp, aim, bre, ALU.mult)
            e.tensor_tensor(dim_, are, bim, ALU.mult)
            e.tensor_tensor(dim_, dim_, tmp, ALU.add)

        wcast_tgt = {}

        @blk.gpsimd
        def _(e):
            for dst, srcp in ((prmCr, creT), (prmCi, cimT)):
                dma(e, qG, "qG", dst.ap()[0:64], srcp)
                dma(e, qG, "qG", dst.ap()[64:128], srcp)
            e.wait_ge(qG, cnt["qG"])
            inc(e, pG2, "pG2")
            cnt["prm_ready"] = cnt["pG2"]
            dma(e, qG, "qG", xf16, x_in.rearrange("b t h -> (b t) h"), max_dma_last_dim=4096)
            e.wait_ge(qG, cnt["qG"])
            inc(e, pG2, "pG2")
            cnt["xf16_ready"] = cnt["pG2"]
            wlist = (("wout", wout, wout16), ("w1", w1T, w116),
                     ("w2", w2T, w216), ("w3", w3T, w316),
                     ("h1", h1T, h116), ("h2", h2T, h216))
            if PHASES == "Cpre":
                wlist = wlist[:1]
            elif PHASES in ("C6a", "C6b", "C6c"):
                pass
            elif PHASES != "ALL":
                wlist = ()
            for nm, srcw, dst in wlist:
                dma(e, qG, "qG", dst, srcw, max_dma_last_dim=8192)
                e.wait_ge(qG, cnt["qG"])
                inc(e, pG2, "pG2")
                wcast_tgt[nm] = cnt["pG2"]

        @blk.sync
        def _(e):
            dma(e, qS, "qS", ident[:], consts["cIdent"])
            dma(e, qS, "qS", b1sb[:], consts["cB1"])
            dma(e, qS, "qS", b1rsb[:], consts["cB1R"])
            dma(e, qS, "qS", th0sb[:], consts["cTh0"])
            dma(e, qS, "qS", th1sb[:], consts["cTh1"])
            for dst, src in ((prmA, arlT), (prmI, aimT)):
                dma(e, qS, "qS", dst.ap()[0:64], src)
                dma(e, qS, "qS", dst.ap()[64:128], src)
            e.wait_ge(pG2, cnt["prm_ready"])
            dma(e, qS, "qS", row1[:1, 0:H], log_dt)
            with nc.allow_non_contiguous_dma(reason="tiny one-time scale loads"):
                for dst, src, n in ((s1sb, s1_in, 32), (s2sb, s2_in, 32),
                                    (sh1sb, sh1_in, 32), (sh2sb, sh2_in, 32),
                                    (s3sb, s3_in, 8), (bosb, bout, 16)):
                    dma(e, qS, "qS", dst.ap().rearrange("p (c o) -> p c o", c=n, o=1),
                        src.rearrange("o (c p) -> p c o", c=n, p=128))
            e.wait_ge(qS, cnt["qS"])
            inc(e, G, "G")
        g_a = cnt["G"]

        @blk.vector
        def _(e):
            e.memset(epsc[:], 1e-5)
            e.memset(ones1[:1, :], 1.0)
            e.drain()

        @blk.scalar
        def _(e):
            e.wait_ge(G, g_a)
            e.activation(row1[:1, 0:H], row1[:1, 0:H], AF.Exp)
            e.activation(prmA[:], prmA[:], AF.Exp)
            inc(e, pS, "pS")

        @blk.vector
        def _(e):
            e.wait_ge(pS, cnt["pS"])
            e.tensor_scalar(prmA[:], prmA[:], -1.0, None, ALU.mult, ALU.bypass)
            inc(e, pV, "pV")

        @blk.sync
        def _(e):
            e.wait_ge(pS, cnt["pS"])
            dma(e, qS, "qS", dtbc.ap()[0:1, 0:H], row1[:1, 0:H])
            e.wait_ge(qS, cnt["qS"])
            w = 1
            while w < 128:
                dma(e, qS, "qS", dtbc.ap()[w:2 * w, 0:H], dtbc.ap()[0:w, 0:H])
                e.wait_ge(qS, cnt["qS"])
                w *= 2
            dma(e, qS, "qS", row1[:1, 0:H], d_in)
            e.wait_ge(qS, cnt["qS"])
            dma(e, qS, "qS", dbc.ap()[0:1, 0:H], row1[:1, 0:H])
            e.wait_ge(qS, cnt["qS"])
            w = 1
            while w < 128:
                dma(e, qS, "qS", dbc.ap()[w:2 * w, 0:H], dbc.ap()[0:w, 0:H])
                e.wait_ge(qS, cnt["qS"])
                w *= 2
            inc(e, G, "G")
        bar(["sync"])

        with contextlib.ExitStack() as st_a:
            xa = st_a.enter_context(nc.sbuf_tensor("xa", [128, H], F16))
            xb = st_a.enter_context(nc.sbuf_tensor("xb", [128, H], F16))
            bn6 = st_a.enter_context(nc.sbuf_tensor("bn6", [128, 12], F32))
            bn2 = st_a.enter_context(nc.sbuf_tensor("bn2", [128, 2], F32))
            mrr = st_a.enter_context(nc.sbuf_tensor("mrr", [128, 2], F32))
            xv = x_in.rearrange("b (t2 t1) h -> (b t2) t1 h", t1=128)
            for ti in range(32):
                buf = xa if ti % 2 == 0 else xb

                xsem = xA if ti % 2 == 0 else xB
                xnm = "xA" if ti % 2 == 0 else "xB"

                @blk.gpsimd
                def _(e, _ti=ti, _buf=buf, _xs=xsem, _xn=xnm):
                    if _ti >= 2:
                        e.wait_ge(pV, cnt.get("pV", 0) - 1)
                    dma(e, _xs, _xn, _buf[:], xv[_ti])
                qg_t = cnt[xnm]

                @blk.vector
                def _(e, _buf=buf, _qt=qg_t, _xs=xsem):
                    e.wait_ge(_xs, _qt)
                    e.bn_stats(bn6[:, 0:6], _buf[:, 0:512])
                    e.bn_stats(bn6[:, 6:12], _buf[:, 512:1024])
                    e.drain()
                    e.bn_aggr(bn2[:], bn6[:])
                    inc(e, pV, "pV")

                @blk.scalar
                def _(e):
                    e.wait_ge(pV, cnt["pV"])
                    e.activation(mrr[:, 1:2], bn2[:, 1:2], AF.Ln, bias=epsc[:])
                    e.drain()
                    e.activation(mrr[:, 0:1], mrr[:, 1:2], AF.Exp, scale=-0.5)
                    inc(e, pS, "pS")

                @blk.vector
                def _(e, _ti=ti, _buf=buf):
                    e.wait_ge(pS, cnt["pS"])
                    e.drain()
                    e.tensor_scalar(TB1[:, _ti * 1024:(_ti + 1) * 1024], _buf[:],
                                    bn2[:, 0:1], mrr[:, 0:1], ALU.subtract, ALU.mult)
                    inc(e, pV, "pV")
            bar(["vector"])
        if DEBUG:
            @blk.sync
            def _(e):
                dma(e, qS, "qS", dbg_xn, TB1[:])
                e.wait_ge(qS, cnt["qS"])
            bar(["sync"])

        def finish_dump(extra=None, nc_extra=4096):
            @blk.gpsimd
            def _(e):
                e.wait_ge(G, cnt.get("G", 0))
                e.wait_ge(pV, cnt.get("pV", 0))
                e.wait_ge(pS, cnt.get("pS", 0))
                dma(e, qG, "qG",
                    y_out.rearrange("b (t2 t1) h -> t1 b t2 h", t1=128),
                    TB1.ap().rearrange("p (b t2 h) -> p b t2 h", b=2, t2=16, h=1024),
                    max_dma_last_dim=4096)
                if extra is not None:
                    e.wait_ge(qG, cnt["qG"])
                    yv = y_out.rearrange("b (tq p t1q) h -> b tq p (t1q h)",
                                         p=128, t1q=1)
                    for j in range(nc_extra // 1024):
                        dma(e, qG, "qG", yv[1, j],
                            extra.ap()[:, j * 1024:(j + 1) * 1024])
                e.wait_ge(qG, cnt["qG"])

            @blk.sync
            def _(e):
                e.wait_ge(qS, cnt["qS"])

        if PHASES == "A":
            finish_dump()
            return nc

        NHG = CH_SL // 4
        NHGP = CH_SL // 8
        NHSG = CH_SL // 4
        NPL = M2 * CH_SL

        with contextlib.ExitStack() as st_b:
            constb = st_b.enter_context(nc.sbuf_tensor("constb", [128, M2 * 3 * 128], F16))
            xnh = st_b.enter_context(nc.sbuf_tensor("xnh", [128, NF], F16))
            zst = st_b.enter_context(nc.sbuf_tensor("zst", [128, NHG * 128], F16))
            Ssl = st_b.enter_context(nc.sbuf_tensor("Ssl", [128, 2 * NPL], F16))
            Zsl = st_b.enter_context(nc.sbuf_tensor("Zsl", [128, 2 * NPL], F16))
            KFsl = st_b.enter_context(nc.sbuf_tensor("KFsl", [128, 2 * NPL], F16))
            WmT = st_b.enter_context(nc.sbuf_tensor("WmT", [128, 2 * NPL], F16))
            kstr = st_b.enter_context(nc.sbuf_tensor("kstr", [128, NHGP * 128], F16))
            qre = st_b.enter_context(nc.sbuf_tensor("qre", [128, KSUB * 128], F32))
            qim = st_b.enter_context(nc.sbuf_tensor("qim", [128, KSUB * 128], F32))
            pre = st_b.enter_context(nc.sbuf_tensor("pre", [128, KSUB * 16], F32))
            pim = st_b.enter_context(nc.sbuf_tensor("pim", [128, KSUB * 16], F32))
            pcmb = st_b.enter_context(nc.sbuf_tensor("pcmb", [128, KSUB * 16], F16))
            qcmb = st_b.enter_context(nc.sbuf_tensor("qcmb", [128, KSUB * 128], F16))
            lre = st_b.enter_context(nc.sbuf_tensor("lre", [128, KSUB], F32))
            lim = st_b.enter_context(nc.sbuf_tensor("lim", [128, KSUB], F32))
            pwr = st_b.enter_context(nc.sbuf_tensor("pwr", [128, KSUB], F32))
            pwi = st_b.enter_context(nc.sbuf_tensor("pwi", [128, KSUB], F32))
            t0 = st_b.enter_context(nc.sbuf_tensor("t0", [128, KSUB], F32))
            t1b = st_b.enter_context(nc.sbuf_tensor("t1b", [128, KSUB], F32))
            t2b = st_b.enter_context(nc.sbuf_tensor("t2b", [128, KSUB], F32))
            t3b = st_b.enter_context(nc.sbuf_tensor("t3b", [128, KSUB], F32))
            ctre = st_b.enter_context(nc.sbuf_tensor("ctre", [128, KSUB], F32))
            ctim = st_b.enter_context(nc.sbuf_tensor("ctim", [128, KSUB], F32))
            big32 = st_b.enter_context(nc.sbuf_tensor("big32", [128, 1024], F32))
            ust0 = st_b.enter_context(nc.sbuf_tensor("ust0", [128, 128], F16))
            ust1 = st_b.enter_context(nc.sbuf_tensor("ust1", [128, 128], F16))
            gtmp = st_b.enter_context(nc.sbuf_tensor("gtmp", [128, 512], F32))
            xnds = st_b.enter_context(nc.sbuf_tensor("xnds", [128, 32 * CH_SL], F16))
            def om(pl, m2):
                return constb[:, (pl * M2 + m2) * 128:(pl * M2 + m2) * 128 + 128]

            def load_tables(pre_fix):
                @blk.sync
                def _(e):
                    for i, nm in enumerate(("Re", "Im", "ImN")):
                        dma(e, qS, "qS",
                            constb[:, i * M2 * 128:(i + 1) * M2 * 128].rearrange(
                                "p (m k) -> p m k", m=M2, k=128),
                            consts["c" + pre_fix + nm].rearrange(
                                "(m p) k -> p m k", m=M2))
                    e.wait_ge(qS, cnt["qS"])
                    inc(e, G, "G")
                return cnt["G"]

            def stage2(src, dstbuf, tgt_qs):
                @blk.tensor
                def _(e, _t=tgt_qs):
                    e.wait_ge(G, _t)
                for m2 in range(M2):
                    bkr = banks[4 + (m2 % 2) * 2]
                    bki = banks[5 + (m2 % 2) * 2]

                    @blk.tensor
                    def _(e, _m2=m2, _r=bkr, _i=bki):
                        if _m2 >= 2:
                            e.wait_ge(pS, cnt["pS"] - 1)
                        sre = src[:, _m2 * CH_SL:(_m2 + 1) * CH_SL]
                        sim = src[:, NPL + _m2 * CH_SL:NPL + (_m2 + 1) * CH_SL]
                        e.matmul(_r[:, 0:CH_SL], om(0, _m2), sre, start=True, stop=False)
                        mm1 = e.matmul(_r[:, 0:CH_SL], om(2, _m2), sim, start=False, stop=True)
                        tinc(mm1, pP, "pP")
                        e.matmul(_i[:, 0:CH_SL], om(1, _m2), sre, start=True, stop=False)
                        mm2 = e.matmul(_i[:, 0:CH_SL], om(0, _m2), sim, start=False, stop=True)
                        tinc(mm2, pP, "pP")

                    @blk.scalar
                    def _(e, _m2=m2, _r=bkr, _i=bki):
                        e.wait_ge(pP, cnt["pP"])
                        e.copy(dstbuf[:, _m2 * CH_SL:(_m2 + 1) * CH_SL], _r[:, 0:CH_SL])
                        e.copy(dstbuf[:, NPL + _m2 * CH_SL:NPL + (_m2 + 1) * CH_SL],
                               _i[:, 0:CH_SL])
                        inc(e, pS, "pS")
                bar(["scalar"])

            for sl in range(1 if PHASES == "B1" else N_SL):
                h0 = sl * CH_SL
                om_ld = load_tables("Om")

                for ti in range(32):
                    bk = banks[ti % 2]

                    @blk.tensor
                    def _(e, _ti=ti, _bk=bk):
                        if _ti >= 2 or sl > 0:
                            e.wait_ge(pS, cnt["pS"] - 1)
                        mm = e.matmul(_bk[0:CH_SL, 0:128],
                                 TB1[:, _ti * 1024 + h0:_ti * 1024 + h0 + CH_SL],
                                 ident[:], start=True, stop=True)
                        tinc(mm, pP, "pP")

                    @blk.scalar
                    def _(e, _ti=ti, _bk=bk):
                        e.wait_ge(pP, cnt["pP"])
                        e.copy(xnh[0:CH_SL, _ti * 128:(_ti + 1) * 128],
                               _bk[0:CH_SL, 0:128])
                        inc(e, pS, "pS")
                bar(["scalar"])

                if PHASES == "B0":
                    finish_dump(xnh)
                    return nc
                @blk.sync
                def _(e):
                    dma(e, qS, "qS", s_xnh, xnh[0:CH_SL, :])
                    e.wait_ge(qS, cnt["qS"])
                    srcv = s_xnh.rearrange("(hg s) (c t2 t1) -> s c t2 hg t1",
                                           hg=NHG, s=4, c=2, t2=16, t1=128)
                    for s in range(4):
                        for c in range(2):
                            dstp = zst.ap()[s * 32 + c:s * 32 + c + 31:2]
                            dma(e, qS, "qS",
                                dstp.rearrange("t2 (hg t1) -> t2 hg t1",
                                               hg=NHG, t1=128),
                                srcv[s, c])
                    e.wait_ge(qS, cnt["qS"])
                    inc(e, G, "G")
                g0 = cnt["G"]

                if PHASES == "B1s":
                    finish_dump(zst, 2048)
                    return nc
                for hg in range(NHG):
                    bk = banks[2 + hg % 2]

                    @blk.tensor
                    def _(e, _hg=hg, _bk=bk, _g0=g0):
                        if _hg == 0:
                            e.wait_ge(G, _g0)
                        if _hg >= 2:
                            e.wait_ge(pV, cnt["pV"] - 1)
                        mm = e.matmul(_bk[:, 0:256], zst[:, _hg * 128:(_hg + 1) * 128],
                                 b1sb[:], start=True, stop=True)
                        tinc(mm, pP, "pP")

                    @blk.scalar
                    def _(e, _hg=hg, _bk=bk):
                        e.wait_ge(pP, cnt["pP"])
                        dstv = Ssl.ap().rearrange(
                            "p (cp m2 hg s) -> p hg s m2 cp",
                            cp=2, m2=M2, hg=NHG, s=4)[:, _hg]
                        e.copy(dstv, _bk[:, 0:256].rearrange(
                            "p (s m2 cp) -> p s m2 cp", s=4, m2=M2, cp=2))
                        inc(e, pV, "pV")
                bar(["scalar"])

                if PHASES == "B2":
                    finish_dump(Ssl)
                    return nc
                stage2(Ssl, Zsl, om_ld)

                if PHASES == "B3":
                    finish_dump(Zsl)
                    return nc
                for sub in range(CH_SL // KSUB):
                    cs = slice(h0 + sub * KSUB, h0 + sub * KSUB + KSUB)

                    @blk.vector
                    def _(e):
                        e.tensor_copy(t0[:], prmA[:, cs])
                        e.tensor_tensor(t1b[:], t0[:], dtbc[:, cs], ALU.mult)
                        e.tensor_tensor(t2b[:], prmI[:, cs], dtbc[:, cs], ALU.mult)
                        e.tensor_scalar(t3b[:], t2b[:], INV2PI, None, ALU.mult, ALU.bypass)
                        e.tensor_scalar(t3b[:], t3b[:], MAGIC, MAGIC, ALU.add, ALU.subtract)
                        e.scalar_tensor_tensor(pwr[:], t3b[:], -CW1, t2b[:], ALU.mult, ALU.add)
                        e.scalar_tensor_tensor(pwr[:], t3b[:], -CW2, pwr[:], ALU.mult, ALU.add)
                        e.scalar_tensor_tensor(pwr[:], t3b[:], -CW3, pwr[:], ALU.mult, ALU.add)
                        e.tensor_scalar(t2b[:], t2b[:], HALFPI, None, ALU.add, ALU.bypass)
                        e.tensor_scalar(t3b[:], t2b[:], INV2PI, None, ALU.mult, ALU.bypass)
                        e.tensor_scalar(t3b[:], t3b[:], MAGIC, MAGIC, ALU.add, ALU.subtract)
                        e.scalar_tensor_tensor(pwi[:], t3b[:], -CW1, t2b[:], ALU.mult, ALU.add)
                        e.scalar_tensor_tensor(pwi[:], t3b[:], -CW2, pwi[:], ALU.mult, ALU.add)
                        e.scalar_tensor_tensor(pwi[:], t3b[:], -CW3, pwi[:], ALU.mult, ALU.add)
                        inc(e, pV, "pV")

                    @blk.scalar
                    def _(e):
                        e.wait_ge(pV, cnt["pV"])
                        e.activation(t1b[:], t1b[:], AF.Exp)
                        e.activation(pwr[:], pwr[:], AF.Sin)
                        e.activation(pwi[:], pwi[:], AF.Sin)
                        inc(e, pS, "pS")

                    @blk.vector
                    def _(e):
                        e.wait_ge(pS, cnt["pS"])
                        e.tensor_tensor(lre[:], t1b[:], pwi[:], ALU.mult)
                        e.tensor_tensor(lim[:], t1b[:], pwr[:], ALU.mult)
                        inc(e, pV, "pV")
                    if PHASES == "B4l" and sub == 0:
                        @blk.vector
                        def _(e):
                            kd = xnh
                            e.tensor_copy(kd[:, 0:16], lre[:])
                            e.tensor_copy(kd[:, 16:32], lim[:])
                            e.tensor_copy(kd[:, 32:48], t1b[:])
                            e.tensor_copy(kd[:, 48:64], pwr[:])
                            e.tensor_copy(kd[:, 64:80], pwi[:])
                            e.tensor_copy(kd[:, 80:96], dtbc[:, cs])
                            e.tensor_copy(kd[:, 96:112], prmI[:, cs])
                            e.tensor_copy(kd[:, 112:128], prmA[:, cs])
                            inc(e, pV, "pV")
                        bar(["vector"])
                        finish_dump(xnh, 1024)
                        return nc

                    @blk.scalar
                    def _(e):
                        e.wait_ge(pV, cnt["pV"])
                        e.activation(t1b[:], t0[:], AF.Square)
                        e.activation(t2b[:], prmI[:, cs], AF.Square)
                        inc(e, pS, "pS")

                    @blk.vector
                    def _(e):
                        e.wait_ge(pS, cnt["pS"])
                        e.tensor_tensor(t1b[:], t1b[:], t2b[:], ALU.add)
                        inc(e, pV, "pV")

                    @blk.scalar
                    def _(e):
                        e.wait_ge(pV, cnt["pV"])
                        e.activation(t3b[:], t1b[:], AF.Ln)
                        e.drain()
                        e.activation(t1b[:], t3b[:], AF.Exp, scale=-1.0)
                        inc(e, pS, "pS")

                    @blk.vector
                    def _(e):
                        e.wait_ge(pS, cnt["pS"])
                        e.tensor_tensor(t2b[:], t0[:], t1b[:], ALU.mult)
                        e.scalar_tensor_tensor(t3b[:], prmI[:, cs], -1.0, t1b[:],
                                               ALU.mult, ALU.mult)
                        e.tensor_scalar(t0[:], lre[:], -1.0, None, ALU.add, ALU.bypass)
                        cmb(e, ctre[:], ctim[:], t0[:], lim[:], t2b[:], t3b[:], t1b[:])
                        cmb(e, t2b[:], t3b[:], ctre[:], ctim[:],
                            prmCr[:, cs], prmCi[:, cs], t1b[:])
                        e.tensor_copy(ctre[:], t2b[:])
                        e.tensor_copy(ctim[:], t3b[:])
                        e.tensor_copy(pwr[:], lre[:])
                        e.tensor_copy(pwi[:], lim[:])
                        for _ in range(7):
                            e.tensor_tensor(t0[:], pwr[:], pwr[:], ALU.mult)
                            e.tensor_tensor(t1b[:], pwi[:], pwi[:], ALU.mult)
                            e.tensor_tensor(t2b[:], pwr[:], pwi[:], ALU.mult)
                            e.tensor_tensor(pwr[:], t0[:], t1b[:], ALU.subtract)
                            e.tensor_scalar(pwi[:], t2b[:], 2.0, None, ALU.mult, ALU.bypass)
                        pr = pre.ap().rearrange("p (ch l) -> p l ch", ch=KSUB, l=16)
                        pi = pim.ap().rearrange("p (ch l) -> p l ch", ch=KSUB, l=16)
                        e.tensor_copy(pr[:, 0], ctre[:])
                        e.tensor_copy(pi[:, 0], ctim[:])
                        inc(e, pV, "pV")

                    for kstep in range(4):
                        w = 2 ** kstep

                        @blk.vector
                        def _(e, _w=w):
                            pr = pre.ap().rearrange("p (ch l) -> p l ch", ch=KSUB, l=16)
                            pi = pim.ap().rearrange("p (ch l) -> p l ch", ch=KSUB, l=16)
                            are, aim = pr[:, 0:_w], pi[:, 0:_w]
                            dre, dim_ = pr[:, _w:2 * _w], pi[:, _w:2 * _w]
                            bre = pwr.ap().rearrange("p (u ch) -> p u ch", u=1, ch=KSUB)
                            bim = pwi.ap().rearrange("p (u ch) -> p u ch", u=1, ch=KSUB)
                            bre2, are2 = bass.broadcast_tensor_aps(bre, are)
                            bim2, aim2 = bass.broadcast_tensor_aps(bim, aim)
                            tmp = big32[:, 0:_w * KSUB].rearrange(
                                "p (l c) -> p l c", l=_w, c=KSUB)
                            cmb(e, dre, dim_, are2, aim2, bre2, bim2, tmp)
                            if _w < 8:
                                e.tensor_tensor(t0[:], pwr[:], pwr[:], ALU.mult)
                                e.tensor_tensor(t1b[:], pwi[:], pwi[:], ALU.mult)
                                e.tensor_tensor(t2b[:], pwr[:], pwi[:], ALU.mult)
                                e.tensor_tensor(pwr[:], t0[:], t1b[:], ALU.subtract)
                                e.tensor_scalar(pwi[:], t2b[:], 2.0, None,
                                                ALU.mult, ALU.bypass)
                            inc(e, pV, "pV")

                    @blk.vector
                    def _(e):
                        qr = qre.ap().rearrange("p (ch l) -> p l ch", ch=KSUB, l=128)
                        qi = qim.ap().rearrange("p (ch l) -> p l ch", ch=KSUB, l=128)
                        e.memset(qr[:, 0], 1.0)
                        e.memset(qi[:, 0], 0.0)
                        e.tensor_copy(pwr[:], lre[:])
                        e.tensor_copy(pwi[:], lim[:])
                        inc(e, pV, "pV")
                    for kstep in range(7):
                        w = 2 ** kstep

                        @blk.vector
                        def _(e, _w=w):
                            qr = qre.ap().rearrange("p (ch l) -> p l ch", ch=KSUB, l=128)
                            qi = qim.ap().rearrange("p (ch l) -> p l ch", ch=KSUB, l=128)
                            are, aim = qr[:, 0:_w], qi[:, 0:_w]
                            dre, dim_ = qr[:, _w:2 * _w], qi[:, _w:2 * _w]
                            bre = pwr.ap().rearrange("p (u ch) -> p u ch", u=1, ch=KSUB)
                            bim = pwi.ap().rearrange("p (u ch) -> p u ch", u=1, ch=KSUB)
                            bre2, are2 = bass.broadcast_tensor_aps(bre, are)
                            bim2, aim2 = bass.broadcast_tensor_aps(bim, aim)
                            tmp = big32[:, 0:_w * KSUB].rearrange(
                                "p (l c) -> p l c", l=_w, c=KSUB)
                            cmb(e, dre, dim_, are2, aim2, bre2, bim2, tmp)
                            if _w < 64:
                                e.tensor_tensor(t0[:], pwr[:], pwr[:], ALU.mult)
                                e.tensor_tensor(t1b[:], pwi[:], pwi[:], ALU.mult)
                                e.tensor_tensor(t2b[:], pwr[:], pwi[:], ALU.mult)
                                e.tensor_tensor(pwr[:], t0[:], t1b[:], ALU.subtract)
                                e.tensor_scalar(pwi[:], t2b[:], 2.0, None,
                                                ALU.mult, ALU.bypass)
                            inc(e, pV, "pV")

                    @blk.vector
                    def _(e):
                        e.tensor_scalar(pcmb.ap()[0:64], pre.ap()[0:64], 2.0, None,
                                        ALU.mult, ALU.bypass)
                        e.tensor_scalar(pcmb.ap()[64:128], pim.ap()[64:128], -2.0, None,
                                        ALU.mult, ALU.bypass)
                        e.tensor_copy(qcmb.ap()[0:64], qre.ap()[0:64])
                        e.tensor_copy(qcmb.ap()[64:128], qim.ap()[64:128])
                        inc(e, pV, "pV")
                    gv = cnt["pV"]
                    if PHASES == "B4q" and sub == 0:
                        bar(["vector"])
                        finish_dump(qcmb, 2048)
                        return nc
                    if PHASES == "B4p" and sub == 0:
                        bar(["vector"])
                        finish_dump(pcmb, 256)
                        return nc

                    kstg = xnh
                    for ch in range(KSUB):
                        bk = banks[6 + (ch // 4) % 2]
                        fo = (ch % 4) * 128

                        @blk.tensor
                        def _(e, _ch=ch, _bk=bk, _fo=fo, _gv=gv):
                            if _ch == 0:
                                e.wait_ge(pV, _gv)
                            if _ch % 4 == 0 and _ch >= 8:
                                e.wait_ge(pS, cnt["pS"] - 1)
                            mm = e.matmul(_bk[0:16, _fo:_fo + 128],
                                     pcmb[:, _ch * 16:(_ch + 1) * 16],
                                     qcmb[:, _ch * 128:(_ch + 1) * 128],
                                     start=True, stop=True)
                            if _ch % 4 == 3:
                                tinc(mm, pP, "pP")

                        if ch % 4 == 3:
                            @blk.scalar
                            def _(e, _ch=ch, _bk=bk):
                                e.wait_ge(pP, cnt["pP"])
                                base = (_ch - 3) * 128
                                e.copy(kstg[0:16, base:base + 512], _bk[0:16, 0:512])
                                inc(e, pS, "pS")
                    bar(["scalar"])

                    @blk.sync
                    def _(e, _sub=sub):
                        src = kstg[0:16, 0:KSUB * 128].rearrange("t2 (ch t1) -> t2 ch t1",
                                                      ch=KSUB, t1=128)
                        dstv = s_kern.rearrange("(sb ch) (t2 t1) -> sb t2 ch t1",
                                                sb=CH_SL // KSUB, ch=KSUB,
                                                t2=16, t1=128)[_sub]
                        dma(e, qS, "qS", dstv, src)
                        e.wait_ge(qS, cnt["qS"])
                        inc(e, G, "G")
                bar(["sync"])

                @blk.sync
                def _(e):
                    for s in range(8):
                        srcv = s_kern.rearrange("(hgp s) (t2 t1) -> s t2 hgp t1",
                                                hgp=NHGP, s=8, t2=16, t1=128)[s]
                        dma(e, qS, "qS",
                            kstr.ap()[16 * s:16 * s + 16].rearrange(
                                "t2 (hgp t1) -> t2 hgp t1", hgp=NHGP, t1=128),
                            srcv)
                    e.wait_ge(qS, cnt["qS"])
                    inc(e, G, "G")
                g1 = cnt["G"]
                if PHASES == "B4w":
                    finish_dump(kstr, 1024)
                    return nc

                for hgp in range(NHGP):
                    bk = banks[2 + hgp % 2]

                    @blk.tensor
                    def _(e, _h=hgp, _bk=bk, _g=g1):
                        if _h == 0:
                            e.wait_ge(G, _g)
                        if _h >= 2:
                            e.wait_ge(pV, cnt["pV"] - 1)
                        mm = e.matmul(_bk[:, 0:512], kstr[:, _h * 128:(_h + 1) * 128],
                                 b1rsb[:], start=True, stop=True)
                        tinc(mm, pP, "pP")

                    @blk.scalar
                    def _(e, _h=hgp, _bk=bk):
                        e.wait_ge(pP, cnt["pP"])
                        dstv = Ssl.ap().rearrange(
                            "p (cp m2 hgp s) -> p hgp s m2 cp",
                            cp=2, m2=M2, hgp=NHGP, s=8)[:, _h]
                        e.copy(dstv, _bk[:, 0:512].rearrange(
                            "p (s m2 cp) -> p s m2 cp", s=8, m2=M2, cp=2))
                        inc(e, pV, "pV")
                bar(["scalar"])

                if PHASES == "B4r":
                    finish_dump(Ssl)
                    return nc
                stage2(Ssl, KFsl, om_ld)

                ph_ld = load_tables("Ph")

                if PHASES == "B4s":
                    finish_dump(KFsl)
                    return nc
                half = NPL // 2

                @blk.vector
                def _(e):
                    zr, zi = Zsl[:, 0:half], Zsl[:, NPL:NPL + half]
                    kr, ki = KFsl[:, 0:half], KFsl[:, NPL:NPL + half]
                    tt = big32[:, 0:half]
                    e.tensor_tensor(tt, zi, ki, ALU.mult)
                    e.tensor_tensor(WmT[:, 0:half], zr, kr, ALU.mult)
                    e.tensor_tensor(WmT[:, 0:half], WmT[:, 0:half], tt, ALU.subtract)
                    e.tensor_tensor(tt, zi, kr, ALU.mult)
                    e.tensor_tensor(WmT[:, NPL:NPL + half], zr, ki, ALU.mult)
                    e.tensor_tensor(WmT[:, NPL:NPL + half], WmT[:, NPL:NPL + half],
                                    tt, ALU.add)
                    inc(e, pV, "pV")

                @blk.gpsimd
                def _(e):
                    zr, zi = Zsl[:, half:NPL], Zsl[:, NPL + half:2 * NPL]
                    kr, ki = KFsl[:, half:NPL], KFsl[:, NPL + half:2 * NPL]
                    tt = qre[:, 0:half]
                    e.tensor_tensor(tt, zi, ki, ALU.mult)
                    e.tensor_tensor(WmT[:, half:NPL], zr, kr, ALU.mult)
                    e.tensor_tensor(WmT[:, half:NPL], WmT[:, half:NPL], tt, ALU.subtract)
                    e.tensor_tensor(tt, zi, kr, ALU.mult)
                    e.tensor_tensor(WmT[:, NPL + half:2 * NPL], zr, ki, ALU.mult)
                    e.tensor_tensor(WmT[:, NPL + half:2 * NPL],
                                    WmT[:, NPL + half:2 * NPL], tt, ALU.add)
                    inc(e, pG2, "pG2")

                for tix in range(32):
                    @blk.gpsimd
                    def _(e, _ti=tix):
                        e.tensor_tensor(
                            xnds[:, _ti * CH_SL:(_ti + 1) * CH_SL],
                            TB1[:, _ti * 1024 + h0:_ti * 1024 + h0 + CH_SL],
                            dbc[:, h0:h0 + CH_SL], ALU.mult)
                        if _ti == 31:
                            inc(e, pG2, "pG2")
                bar(["vector", "gpsimd"])

                if PHASES == "B5":
                    finish_dump(WmT)
                    return nc
                @blk.tensor
                def _(e, _t=ph_ld):
                    e.wait_ge(G, _t)
                for m2 in range(M2):
                    bkr = banks[4 + (m2 % 2) * 2]
                    bki = banks[5 + (m2 % 2) * 2]

                    @blk.tensor
                    def _(e, _m2=m2, _r=bkr, _i=bki):
                        if _m2 >= 2:
                            e.wait_ge(pS, cnt["pS"] - 1)
                        wre = WmT[:, _m2 * CH_SL:(_m2 + 1) * CH_SL]
                        wim = WmT[:, NPL + _m2 * CH_SL:NPL + (_m2 + 1) * CH_SL]
                        e.matmul(_r[:, 0:CH_SL], om(0, _m2), wre, start=True, stop=False)
                        mm1 = e.matmul(_r[:, 0:CH_SL], om(2, _m2), wim, start=False, stop=True)
                        tinc(mm1, pP, "pP")
                        e.matmul(_i[:, 0:CH_SL], om(1, _m2), wre, start=True, stop=False)
                        mm2 = e.matmul(_i[:, 0:CH_SL], om(0, _m2), wim, start=False, stop=True)
                        tinc(mm2, pP, "pP")

                    @blk.scalar
                    def _(e, _m2=m2, _r=bkr, _i=bki):
                        e.wait_ge(pP, cnt["pP"])
                        for c, bk in ((0, _r), (1, _i)):
                            dstv = Zsl.ap().rearrange(
                                "p (c hsg hssub m2) -> p c m2 hsg hssub",
                                c=2, hsg=NHSG, hssub=4, m2=M2)[:, c, _m2]
                            e.copy(dstv, bk[:, 0:CH_SL].rearrange(
                                "p (hsg hssub) -> p hsg hssub", hsg=NHSG, hssub=4))
                        inc(e, pS, "pS")
                bar(["scalar"])

                if PHASES == "B6":
                    finish_dump(Zsl)
                    return nc
                ust_ready = []
                psy_done = []
                add_done = []
                gelu_done = []
                for hsg in range(NHSG):
                    bky = banks[2 + hsg % 2]
                    go = (hsg % 2) * 128
                    go2 = 256 + (hsg % 2) * 128

                    @blk.tensor
                    def _(e, _hsg=hsg):
                        if _hsg >= 1:
                            e.wait_ge(pS, ust_ready[_hsg - 1])
                        e.matmul(psA[:, 0:128], Zsl[:, _hsg * 128:(_hsg + 1) * 128],
                                 ident[:], start=True, stop=True)
                        mm = e.matmul(psA[:, 128:256],
                                 Zsl[:, NPL + _hsg * 128:NPL + (_hsg + 1) * 128],
                                 ident[:], start=True, stop=True)
                        tinc(mm, pP, "pP")

                    @blk.scalar
                    def _(e):
                        e.wait_ge(pP, cnt["pP"])
                        e.copy(ust0[:], psA[:, 0:128])
                        e.copy(ust1[:], psA[:, 128:256])
                        inc(e, pS, "pS")
                    ust_ready.append(cnt["pS"])

                    @blk.tensor
                    def _(e, _y=bky, _hsg=hsg, _gv=ust_ready[-1]):
                        e.wait_ge(pS, _gv)
                        if _hsg >= 2:
                            e.wait_ge(pS, psy_done[_hsg - 2])
                        e.matmul(_y[:, 0:128], ust0[:], th0sb[:], start=True, stop=False)
                        mm = e.matmul(_y[:, 0:128], ust1[:], th1sb[:], start=False, stop=True)
                        tinc(mm, pP, "pP")

                    @blk.scalar
                    def _(e, _hsg=hsg, _y=bky, _go2=go2):
                        e.wait_ge(pP, cnt["pP"])
                        if _hsg >= 2:
                            e.wait_ge(pV, add_done[_hsg - 2])
                        e.copy(gtmp[:, _go2:_go2 + 128], _y[:, 0:128])
                        inc(e, pS, "pS")
                    psy_done.append(cnt["pS"])

                    @blk.vector
                    def _(e, _hsg=hsg, _go=go, _go2=go2, _pd=psy_done[-1]):
                        e.wait_ge(pS, _pd)
                        av = xnds.ap().rearrange(
                            "p (c t2 hsg hssub) -> p hsg hssub t2 c",
                            c=2, t2=16, hsg=NHSG, hssub=4)[:, _hsg]
                        e.tensor_tensor(
                            gtmp[:, _go:_go + 128].rearrange(
                                "p (hssub t2 c) -> p hssub t2 c", hssub=4, t2=16, c=2),
                            gtmp[:, _go2:_go2 + 128].rearrange(
                                "p (hssub t2 c) -> p hssub t2 c", hssub=4, t2=16, c=2),
                            av, ALU.add)
                        inc(e, pV, "pV")
                    add_done.append(cnt["pV"])

                    @blk.scalar
                    def _(e, _hsg=hsg, _go=go, _sl=sl, _ad=add_done[-1]):
                        e.wait_ge(pV, _ad)
                        dstv = TB1.ap().rearrange(
                            "p (c t2 sl hsg hssub) -> p sl hsg hssub t2 c",
                            c=2, t2=16, sl=N_SL, hsg=NHSG, hssub=4)[:, _sl, _hsg]
                        e.activation(dstv,
                                     gtmp[:, _go:_go + 128].rearrange(
                                         "p (hssub t2 c) -> p hssub t2 c",
                                         hssub=4, t2=16, c=2),
                                     AF.Gelu_apprx_tanh)
                        inc(e, pS, "pS")
                    gelu_done.append(cnt["pS"])
                bar(["scalar"])
        if PHASES in ("B1", "B"):
            finish_dump()
            return nc
        if DEBUG:
            @blk.sync
            def _(e):
                dma(e, qS, "qS", dbg_y2, TB1[:])
                e.wait_ge(qS, cnt["qS"])
            bar(["sync"])

        NT_TG = TGT // 128

        def ln_pass(src_ap, stat, dst_fn):
            bn6, bn2, mrr = stat

            @blk.vector
            def _(e):
                e.bn_stats(bn6[:, 0:6], src_ap[:, 0:512])
                e.bn_stats(bn6[:, 6:12], src_ap[:, 512:1024])
                e.drain()
                e.bn_aggr(bn2[:], bn6[:])
                inc(e, pV, "pV")

            @blk.scalar
            def _(e):
                e.wait_ge(pV, cnt["pV"])
                e.activation(mrr[:, 1:2], bn2[:, 1:2], AF.Ln, bias=epsc[:])
                e.drain()
                e.activation(mrr[:, 0:1], mrr[:, 1:2], AF.Exp, scale=-0.5)
                inc(e, pS, "pS")

            @blk.vector
            def _(e):
                e.wait_ge(pS, cnt["pS"])
                e.drain()
                dst_fn(e, bn2[:, 0:1], mrr[:, 0:1])
                inc(e, pV, "pV")

        with contextlib.ExitStack() as st_c:
            y2T = st_c.enter_context(nc.sbuf_tensor("y2T", [128, 8 * TGT], F16))
            attf = st_c.enter_context(nc.sbuf_tensor("attf", [128, 2 * TGT], F16))
            attm = st_c.enter_context(nc.sbuf_tensor("attm", [128, NT_TG * 1024], F16))
            wcA = st_c.enter_context(nc.sbuf_tensor("wcA", [128, 2048], F16))
            wcB = st_c.enter_context(nc.sbuf_tensor("wcB", [128, 2048], F16))
            xtl = st_c.enter_context(nc.sbuf_tensor("xtl", [128, H], F16))
            xtl2 = st_c.enter_context(nc.sbuf_tensor("xtl2", [128, H], F16))
            evt = st_c.enter_context(nc.sbuf_tensor("evt", [128, TGT], F16))
            evt2 = st_c.enter_context(nc.sbuf_tensor("evt2", [128, TGT], F16))
            cb6 = st_c.enter_context(nc.sbuf_tensor("cb6", [128, 12], F32))
            cb2 = st_c.enter_context(nc.sbuf_tensor("cb2", [128, 2], F32))
            cmr = st_c.enter_context(nc.sbuf_tensor("cmr", [128, 2], F32))
            ctt = st_c.enter_context(nc.sbuf_tensor("ctt", [128, H], F16))
            stat = (cb6, cb2, cmr)

            @blk.sync
            def _(e):
                e.wait_ge(pG2, wcast_tgt["wout"])
            woutv = wout16.rearrange("(kc p) o -> p kc o", kc=8, p=128)
            for g in range(N_TG):
                for tt in range(NT_TG):
                    for hc in range(8):
                        bk = banks[(tt * 8 + hc) % 2]
                        tia = g * NT_TG + tt

                        @blk.tensor
                        def _(e, _t=tt, _hc=hc, _bk=bk, _tia=tia, _g=g):
                            if _t + _hc > 0 or _g > 0:
                                e.wait_ge(pS, cnt["pS"] - 1)
                            mm = e.matmul(_bk[:, 0:128],
                                     TB1[:, _tia * 1024 + _hc * 128:
                                         _tia * 1024 + (_hc + 1) * 128],
                                     ident[:], start=True, stop=True)
                            tinc(mm, pP, "pP")

                        @blk.scalar
                        def _(e, _t=tt, _hc=hc, _bk=bk):
                            e.wait_ge(pP, cnt["pP"])
                            e.copy(y2T[:, _hc * TGT + _t * 128:
                                       _hc * TGT + (_t + 1) * 128], _bk[:, 0:128])
                            inc(e, pS, "pS")
                bar(["scalar"])

                for oc in range(8):
                    wb = wcA if oc % 2 == 0 else wcB

                    wsem = wA if oc % 2 == 0 else wB
                    wnm = "wA" if oc % 2 == 0 else "wB"

                    @blk.sync
                    def _(e, _oc=oc, _wb=wb, _ws=wsem, _wn=wnm):
                        if _oc >= 2:
                            e.wait_ge(cW, cnt.get("cW", 0) - 1)
                        dma(e, _ws, _wn, _wb[:, 0:1024].rearrange(
                            "p (kc o) -> p kc o", kc=8, o=128),
                            woutv[:, :, _oc * 128:(_oc + 1) * 128])
                        dma(e, _ws, _wn, _wb[:, 1024:2048].rearrange(
                            "p (kc o) -> p kc o", kc=8, o=128),
                            woutv[:, :, (_oc + 8) * 128:(_oc + 9) * 128])
                    wt = cnt[wnm]

                    @blk.tensor
                    def _(e, _oc=oc, _wb=wb, _wt=wt, _ws=wsem):
                        e.wait_ge(_ws, _wt)
                        if _oc >= 1:
                            e.wait_ge(pV, cnt["pV"] - 0)
                        for kc in range(8):
                            mm = e.matmul(psE[:, 0:TGT], _wb[:, kc * 128:(kc + 1) * 128],
                                     y2T[:, kc * TGT:(kc + 1) * TGT],
                                     start=(kc == 0), stop=(kc == 7))
                        tinc(mm, pP, "pP")
                        for kc in range(8):
                            mm = e.matmul(psF[:, 0:TGT],
                                     _wb[:, 1024 + kc * 128:1024 + (kc + 1) * 128],
                                     y2T[:, kc * TGT:(kc + 1) * TGT],
                                     start=(kc == 0), stop=(kc == 7))
                        tinc(mm, pP, "pP")

                    @blk.scalar
                    def _(e, _oc=oc):
                        e.wait_ge(pP, cnt["pP"])
                        inc(e, cW, "cW")
                        e.activation(evt[:], psE[:, 0:TGT], AF.Identity,
                                     bias=bosb[:, _oc:_oc + 1])
                        e.activation(evt2[:], psF[:, 0:TGT], AF.Sigmoid,
                                     bias=bosb[:, 8 + _oc:9 + _oc])
                        inc(e, pS, "pS")

                    @blk.vector
                    def _(e, _oc=oc):
                        e.wait_ge(pS, cnt["pS"])
                        e.tensor_tensor(attf[:, 0:TGT], evt[:], evt2[:], ALU.mult)
                        inc(e, pV, "pV")
                    gva = cnt["pV"]

                    for tt in range(NT_TG):
                        bk = banks[2 + tt % 2]

                        @blk.tensor
                        def _(e, _oc=oc, _tt=tt, _bk=bk, _gv=gva):
                            e.wait_ge(pV, _gv)
                            if _oc + _tt > 0:
                                e.wait_ge(pS, cnt["pS"] - 1)
                            mm = e.matmul(_bk[:, 0:128], attf[:, _tt * 128:(_tt + 1) * 128],
                                     ident[:], start=True, stop=True)
                            tinc(mm, pP, "pP")

                        @blk.scalar
                        def _(e, _oc=oc, _tt=tt, _bk=bk):
                            e.wait_ge(pP, cnt["pP"])
                            e.copy(attm[:, _tt * 1024 + _oc * 128:
                                        _tt * 1024 + (_oc + 1) * 128], _bk[:, 0:128])
                            inc(e, pS, "pS")
                bar(["scalar"])

                if g == 0:
                    @blk.sync
                    def _(e):
                        e.wait_ge(pG2, cnt['xf16_ready'])
                for tt in range(NT_TG):
                    tia = g * NT_TG + tt
                    xb2 = xtl if tt % 2 == 0 else xtl2
                    xsem = xA if tt % 2 == 0 else xB
                    xnm = "xA" if tt % 2 == 0 else "xB"

                    @blk.sync
                    def _(e, _tia=tia, _xb=xb2, _tt=tt, _xs=xsem, _xn=xnm):
                        if _tt >= 2 or g > 0:
                            e.wait_ge(pV, cnt["pV"] - 2)
                        dma(e, _xs, _xn, _xb[:],
                            xf16.rearrange("(ti t1) h -> ti t1 h", t1=128)[_tia])
                    xt_t = cnt[xnm]

                    @blk.vector
                    def _(e, _t=xt_t, _xs=xsem):
                        e.wait_ge(_xs, _t)

                    src = attm[:, tt * 1024:(tt + 1) * 1024]

                    def dst_fn(e, m_ap, r_ap, _tia=tia, _src=src, _xb=xb2):
                        e.tensor_scalar(ctt[:], _src, m_ap, r_ap,
                                        ALU.subtract, ALU.mult)
                        e.tensor_tensor(TB1[:, _tia * 1024:(_tia + 1) * 1024],
                                        ctt[:], _xb[:], ALU.add)
                    ln_pass(src, stat, dst_fn)
                bar(["vector"])
        if DEBUG:
            @blk.sync
            def _(e):
                dma(e, qS, "qS", dbg_x1, TB1[:])
                e.wait_ge(qS, cnt["qS"])
            bar(["sync"])

        if PHASES == "Cpre":
            finish_dump()
            return nc

        with contextlib.ExitStack() as st_f:
            x1n = st_f.enter_context(nc.sbuf_tensor("x1n", [128, H], F16))
            x1nT = st_f.enter_context(nc.sbuf_tensor("x1nT", [128, 8 * TGT], F16))
            HID = st_f.enter_context(nc.sbuf_tensor("HID", [128, 32 * TGT], F16))
            HID2 = st_f.enter_context(nc.sbuf_tensor("HID2", [128, 32 * TGT], F16))
            f3m = st_f.enter_context(nc.sbuf_tensor("f3m", [128, NT_TG * 1024], F16))
            whA = st_f.enter_context(nc.sbuf_tensor("whA", [128, 4096], F16))
            whB = st_f.enter_context(nc.sbuf_tensor("whB", [128, 4096], F16))
            w12A = st_f.enter_context(nc.sbuf_tensor("w12A", [128, 2048], F16))
            w12B = st_f.enter_context(nc.sbuf_tensor("w12B", [128, 2048], F16))
            outb = st_f.enter_context(nc.sbuf_tensor("outb", [128, H], F32))
            fb6 = st_f.enter_context(nc.sbuf_tensor("fb6", [128, 12], F32))
            fb2 = st_f.enter_context(nc.sbuf_tensor("fb2", [128, 2], F32))
            fmr = st_f.enter_context(nc.sbuf_tensor("fmr", [128, 2], F32))
            stat2 = (fb6, fb2, fmr)
            F3 = x1nT
            ftt = x1n

            @blk.sync
            def _(e):
                e.wait_ge(pG2, wcast_tgt["h2"])
            w1v = w116.rearrange("(kc p) o -> p kc o", kc=8, p=128)
            w2v = w216.rearrange("(kc p) o -> p kc o", kc=8, p=128)
            h1v = h116.rearrange("(kc p) o -> p kc o", kc=32, p=128)
            h2v = h216.rearrange("(kc p) o -> p kc o", kc=32, p=128)
            w3v = w316.rearrange("(kc p) o -> p kc o", kc=32, p=128)
            out_t = 0

            for g in range(N_TG):
                for tt in range(NT_TG):
                    tia = g * NT_TG + tt
                    src = TB1[:, tia * 1024:(tia + 1) * 1024]

                    def dst_fn(e, m_ap, r_ap, _src=src):
                        e.tensor_scalar(x1n[:], _src, m_ap, r_ap,
                                        ALU.subtract, ALU.mult)
                    ln_pass(src, stat2, dst_fn)
                    gvb = cnt["pV"]
                    for hc in range(8):
                        bk = banks[hc % 2]

                        @blk.tensor
                        def _(e, _tt=tt, _hc=hc, _bk=bk, _gv=gvb):
                            e.wait_ge(pV, _gv)
                            if _tt + _hc > 0 or g > 0:
                                e.wait_ge(pS, cnt["pS"] - 1)
                            mm = e.matmul(_bk[:, 0:128], x1n[:, _hc * 128:(_hc + 1) * 128],
                                     ident[:], start=True, stop=True)
                            tinc(mm, pP, "pP")

                        @blk.scalar
                        def _(e, _tt=tt, _hc=hc, _bk=bk):
                            e.wait_ge(pP, cnt["pP"])
                            e.copy(x1nT[:, _hc * TGT + _tt * 128:
                                        _hc * TGT + (_tt + 1) * 128], _bk[:, 0:128])
                            inc(e, pS, "pS")
                bar(["scalar"])

                for ht in range(32):
                    wb = w12A if ht % 2 == 0 else w12B

                    wsem = wA if ht % 2 == 0 else wB
                    wnm = "wA" if ht % 2 == 0 else "wB"

                    @blk.sync
                    def _(e, _ht=ht, _wb=wb, _ws=wsem, _wn=wnm):
                        if _ht >= 2:
                            e.wait_ge(cW, cnt["cW"] - 1)
                        dma(e, _ws, _wn, _wb[:, 0:1024].rearrange(
                            "p (kc o) -> p kc o", kc=8, o=128),
                            w1v[:, :, _ht * 128:(_ht + 1) * 128])
                        dma(e, _ws, _wn, _wb[:, 1024:2048].rearrange(
                            "p (kc o) -> p kc o", kc=8, o=128),
                            w2v[:, :, _ht * 128:(_ht + 1) * 128])
                    wt = cnt[wnm]

                    @blk.tensor
                    def _(e, _ht=ht, _wb=wb, _wt=wt, _ws=wsem):
                        e.wait_ge(_ws, _wt)
                        if _ht >= 1:
                            e.wait_ge(pV, cnt["pV"])
                        for kc in range(8):
                            mm = e.matmul(psE[:, 0:TGT], _wb[:, kc * 128:(kc + 1) * 128],
                                     x1nT[:, kc * TGT:(kc + 1) * TGT],
                                     start=(kc == 0), stop=(kc == 7))
                        tinc(mm, pP, "pP")
                        for kc in range(8):
                            mm = e.matmul(psF[:, 0:TGT],
                                     _wb[:, 1024 + kc * 128:1024 + (kc + 1) * 128],
                                     x1nT[:, kc * TGT:(kc + 1) * TGT],
                                     start=(kc == 0), stop=(kc == 7))
                        tinc(mm, pP, "pP")

                    @blk.scalar
                    def _(e, _ht=ht):
                        e.wait_ge(pP, cnt["pP"])
                        inc(e, cW, "cW")
                        e.activation(f3m[:, 0:512], psE[:, 0:TGT], AF.Silu,
                                     scale=s1sb[:, _ht:_ht + 1])
                        e.activation(f3m[:, 512:1024], psF[:, 0:TGT], AF.Identity,
                                     scale=s2sb[:, _ht:_ht + 1])
                        inc(e, pS, "pS")

                    @blk.vector
                    def _(e, _ht=ht):
                        e.wait_ge(pS, cnt["pS"])
                        e.tensor_tensor(HID[:, _ht * TGT:(_ht + 1) * TGT],
                                        f3m[:, 0:512], f3m[:, 512:1024], ALU.mult)
                        inc(e, pV, "pV")
                bar(["vector"])

                if PHASES == "C6a" and g == 0:
                    finish_dump(HID, 4096)
                    return nc
                for stagei, (wv, scl, srcbuf, dstbuf) in enumerate(
                        ((h1v, sh1sb, HID, HID2), (h2v, sh2sb, HID2, HID))):
                    for ot in range(32):
                        wb = whA if ot % 2 == 0 else whB

                        wsem = wA if ot % 2 == 0 else wB
                        wnm = "wA" if ot % 2 == 0 else "wB"

                        @blk.sync
                        def _(e, _ot=ot, _wb=wb, _wv=wv, _ws=wsem, _wn=wnm):
                            if _ot >= 2 or stagei > 0 or g > 0:
                                e.wait_ge(cW, cnt["cW"] - 1)
                            dma(e, _ws, _wn, _wb[:].rearrange(
                                "p (kc o) -> p kc o", kc=32, o=128),
                                _wv[:, :, _ot * 128:(_ot + 1) * 128])
                        wt = cnt[wnm]

                        pbk = psE if ot % 2 == 0 else psF

                        @blk.tensor
                        def _(e, _ot=ot, _wb=wb, _wt=wt, _src=srcbuf, _pb=pbk,
                              _ws=wsem):
                            e.wait_ge(_ws, _wt)
                            if _ot >= 2:
                                e.wait_ge(pS, cnt["pS"] - 1)
                            for kc in range(32):
                                mm = e.matmul(_pb[:, 0:TGT], _wb[:, kc * 128:(kc + 1) * 128],
                                         _src[:, kc * TGT:(kc + 1) * TGT],
                                         start=(kc == 0), stop=(kc == 31))
                            tinc(mm, pP, "pP")

                        @blk.scalar
                        def _(e, _ot=ot, _dst=dstbuf, _scl=scl, _pb=pbk):
                            e.wait_ge(pP, cnt["pP"])
                            inc(e, cW, "cW")
                            e.activation(_dst[:, _ot * TGT:(_ot + 1) * TGT],
                                         _pb[:, 0:TGT], AF.Gelu,
                                         scale=_scl[:, _ot:_ot + 1])
                            inc(e, pS, "pS")
                    bar(["scalar"])

                if PHASES == "C6b" and g == 0:
                    finish_dump(HID, 4096)
                    return nc
                for dc in range(8):
                    wb = whA if dc % 2 == 0 else whB

                    wsem = wA if dc % 2 == 0 else wB
                    wnm = "wA" if dc % 2 == 0 else "wB"

                    @blk.sync
                    def _(e, _dc=dc, _wb=wb, _ws=wsem, _wn=wnm):
                        e.wait_ge(cW, cnt["cW"] - 1)
                        dma(e, _ws, _wn, _wb[:].rearrange(
                            "p (kc o) -> p kc o", kc=32, o=128),
                            w3v[:, :, _dc * 128:(_dc + 1) * 128])
                    wt = cnt[wnm]

                    pbk = psE if dc % 2 == 0 else psF

                    @blk.tensor
                    def _(e, _dc=dc, _wb=wb, _wt=wt, _pb=pbk, _ws=wsem):
                        e.wait_ge(_ws, _wt)
                        if _dc >= 2:
                            e.wait_ge(pS, cnt["pS"] - 1)
                        for kc in range(32):
                            mm = e.matmul(_pb[:, 0:TGT], _wb[:, kc * 128:(kc + 1) * 128],
                                     HID[:, kc * TGT:(kc + 1) * TGT],
                                     start=(kc == 0), stop=(kc == 31))
                        tinc(mm, pP, "pP")

                    @blk.scalar
                    def _(e, _dc=dc, _pb=pbk):
                        e.wait_ge(pP, cnt["pP"])
                        inc(e, cW, "cW")
                        e.activation(F3[:, _dc * TGT:(_dc + 1) * TGT], _pb[:, 0:TGT],
                                     AF.Identity, scale=s3sb[:, _dc:_dc + 1])
                        inc(e, pS, "pS")
                bar(["scalar"])

                if PHASES == "C6c" and g == 0:
                    finish_dump(F3, 4096)
                    return nc
                for tt in range(NT_TG):
                    for dc in range(8):
                        bk = banks[dc % 2]

                        @blk.tensor
                        def _(e, _tt=tt, _dc=dc, _bk=bk):
                            if _tt + _dc > 0 or g > 0:
                                e.wait_ge(pS, cnt["pS"] - 1)
                            mm = e.matmul(_bk[:, 0:128],
                                     F3[:, _dc * TGT + _tt * 128:
                                        _dc * TGT + (_tt + 1) * 128],
                                     ident[:], start=True, stop=True)
                            tinc(mm, pP, "pP")

                        @blk.scalar
                        def _(e, _tt=tt, _dc=dc, _bk=bk):
                            e.wait_ge(pP, cnt["pP"])
                            e.copy(f3m[:, _tt * 1024 + _dc * 128:
                                       _tt * 1024 + (_dc + 1) * 128], _bk[:, 0:128])
                            inc(e, pS, "pS")
                bar(["scalar"])
                for tt in range(NT_TG):
                    tia = g * NT_TG + tt
                    src = f3m[:, tt * 1024:(tt + 1) * 1024]
                    prev_out = out_t

                    def dst_fn(e, m_ap, r_ap, _src=src, _tia=tia, _po=prev_out):
                        if _po > 0:
                            e.wait_ge(qS, _po)
                        e.tensor_scalar(ftt[:], _src, m_ap, r_ap,
                                        ALU.subtract, ALU.mult)
                        e.tensor_tensor(outb[:], ftt[:],
                                        TB1[:, _tia * 1024:(_tia + 1) * 1024], ALU.add)
                    ln_pass(src, stat2, dst_fn)
                    gvv = cnt["pV"]

                    @blk.sync
                    def _(e, _tia=tia, _gv=gvv):
                        e.wait_ge(pV, _gv)
                        dma(e, qS, "qS", y_out.rearrange(
                            "b (t2 t1) h -> (b t2) t1 h", t1=128)[_tia], outb[:])
                    out_t = cnt["qS"]
            @blk.sync
            def _(e):
                e.wait_ge(qS, cnt["qS"])
            bar(["sync"])
    return nc


_tables = _host_tables()
_nc_cache = {}


def kernel(L0_attn_x, L0_attn_log_dt, L0_attn_A_real_log, L0_attn_A_imag,
           L0_attn_C_real, L0_attn_C_imag, L0_attn_D, L0_attn_W_out,
           L0_attn_b_out, L0_ffn_w1_w, L0_ffn_w1_s, L0_ffn_w2_w, L0_ffn_w2_s,
           L0_ffn_h1_w, L0_ffn_h1_s, L0_ffn_h2_w, L0_ffn_h2_s, L0_ffn_w3_w,
           L0_ffn_w3_s):
    f32 = np.float32
    c = np.ascontiguousarray
    common = dict(
        log_dt=c(np.asarray(L0_attn_log_dt, f32).reshape(1, H)),
        arlT=c(np.asarray(L0_attn_A_real_log, f32).T),
        aimT=c(np.asarray(L0_attn_A_imag, f32).T),
        creT=c(np.asarray(L0_attn_C_real, f32).T),
        cimT=c(np.asarray(L0_attn_C_imag, f32).T),
        d_vec=c(np.asarray(L0_attn_D, f32).reshape(1, H)),
        wout=c(np.asarray(L0_attn_W_out, f32)),
        bout=c(np.asarray(L0_attn_b_out, f32).reshape(1, 2 * H)),
        w1T=c(np.asarray(L0_ffn_w1_w, f32).T),
        w2T=c(np.asarray(L0_ffn_w2_w, f32).T),
        h1T=c(np.asarray(L0_ffn_h1_w, f32).T),
        h2T=c(np.asarray(L0_ffn_h2_w, f32).T),
        w3T=c(np.asarray(L0_ffn_w3_w, f32).T),
        s1=c(np.asarray(L0_ffn_w1_s, f32).reshape(1, DH)),
        s2=c(np.asarray(L0_ffn_w2_s, f32).reshape(1, DH)),
        sh1=c(np.asarray(L0_ffn_h1_s, f32).reshape(1, DH)),
        sh2=c(np.asarray(L0_ffn_h2_s, f32).reshape(1, DH)),
        s3=c(np.asarray(L0_ffn_w3_s, f32).reshape(1, H)),
        **_tables,
    )
    x = np.asarray(L0_attn_x, f32)
    in_maps = []
    for core in range(8):
        m = dict(common)
        m["x"] = c(x[2 * core:2 * core + 2])
        in_maps.append(m)
    if "nc" not in _nc_cache:
        _nc_cache["nc"] = build_nc()
    res = run_bass_kernel_spmd(_nc_cache["nc"], in_maps, core_ids=list(range(8)))
    if DEBUG:
        _dbg.clear()
        _dbg.update({k: v for k, v in res.results[0].items() if k.startswith("dbg")})
    out = np.concatenate([res.results[i]["y"] for i in range(8)], axis=0)
    return out


# version 04
# speedup vs baseline: 1.1844x; 1.1844x over previous
import contextlib
import numpy as np
import concourse.bass as bass
import concourse.mybir as mybir
from concourse.bass_utils import run_bass_kernel_spmd

F32 = mybir.dt.float32
F16 = mybir.dt.float16
AF = mybir.ActivationFunctionType
ALU = mybir.AluOpType

L = 2048
H = 1024
NST = 64
NF = 4096
T2 = 16
M2 = 32
DH = 4096

CH_SL = 64
N_SL = H // CH_SL
KSUB = 16
TGT = 512
N_TG = 2 * L // TGT

MAGIC = float(1.5 * 2 ** 23)
INV2PI = float(1.0 / (2 * np.pi))
CW1 = 6.28125
CW2 = 0.0019353628158569336
CW3 = float(2 * np.pi - CW1 - CW2)
HALFPI = float(np.pi / 2)

DEBUG = False
PHASES = "ALL"
_dbg = {}


def _host_tables():
    f16 = np.float16
    W32 = np.exp(-2j * np.pi / 32)
    B1 = np.zeros((128, 256), np.float32)
    for s in range(4):
        for t2 in range(T2):
            for m2 in range(M2):
                w = W32 ** (t2 * m2)
                B1[s * 32 + t2 * 2 + 0, s * 64 + m2 * 2 + 0] = w.real
                B1[s * 32 + t2 * 2 + 0, s * 64 + m2 * 2 + 1] = w.imag
                B1[s * 32 + t2 * 2 + 1, s * 64 + m2 * 2 + 0] = -w.imag
                B1[s * 32 + t2 * 2 + 1, s * 64 + m2 * 2 + 1] = w.real
    B1R = np.zeros((128, 512), np.float32)
    for s in range(8):
        for t2 in range(T2):
            for m2 in range(M2):
                w = W32 ** (t2 * m2)
                B1R[s * 16 + t2, s * 64 + m2 * 2 + 0] = w.real
                B1R[s * 16 + t2, s * 64 + m2 * 2 + 1] = w.imag
    Wn = np.exp(-2j * np.pi / NF)
    W128 = np.exp(-2j * np.pi / 128)
    t1 = np.arange(128)[:, None]
    m1 = np.arange(128)[None, :]
    Om = np.zeros((M2, 128, 128), complex)
    for m2 in range(M2):
        Om[m2] = (Wn ** (t1 * m2)) * (W128 ** (t1 * m1))
    V = np.conj(Wn)
    V128 = np.exp(+2j * np.pi / 128)
    Ph = np.zeros((M2, 128, 128), complex)
    for m2 in range(M2):
        Ph[m2] = (V128 ** (m1.T * t1.T)) * (V ** (t1.T * m2)) / NF
    V32 = np.exp(+2j * np.pi / 32)
    Th = np.zeros((2, 128, 128), np.float32)
    for hssub in range(4):
        for m2 in range(M2):
            k = hssub * 32 + m2
            for t2 in range(T2):
                v = V32 ** (m2 * t2)
                Th[0, k, hssub * 32 + t2 * 2 + 0] = v.real
                Th[0, k, hssub * 32 + t2 * 2 + 1] = v.imag
                Th[1, k, hssub * 32 + t2 * 2 + 0] = -v.imag
                Th[1, k, hssub * 32 + t2 * 2 + 1] = v.real
    stk = lambda a: np.ascontiguousarray(a.reshape(-1, a.shape[-1]).astype(f16))
    return dict(
        cB1=B1.astype(f16), cB1R=B1R.astype(f16),
        cOmRe=stk(Om.real), cOmIm=stk(Om.imag), cOmImN=stk(-Om.imag),
        cPhRe=stk(Ph.real), cPhIm=stk(Ph.imag), cPhImN=stk(-Ph.imag),
        cTh0=Th[0].astype(f16), cTh1=Th[1].astype(f16),
        cIdent=np.eye(128, dtype=f16),
    )


def build_nc():
    nc = bass.Bass("TRN2", target_bir_lowering=False)
    dram = nc.dram_tensor

    x_in = dram("x", [2, L, H], F32, kind="ExternalInput").ap()
    log_dt = dram("log_dt", [1, H], F32, kind="ExternalInput").ap()
    arlT = dram("arlT", [NST, H], F32, kind="ExternalInput").ap()
    aimT = dram("aimT", [NST, H], F32, kind="ExternalInput").ap()
    creT = dram("creT", [NST, H], F32, kind="ExternalInput").ap()
    cimT = dram("cimT", [NST, H], F32, kind="ExternalInput").ap()
    d_in = dram("d_vec", [1, H], F32, kind="ExternalInput").ap()
    wout = dram("wout", [H, 2 * H], F32, kind="ExternalInput").ap()
    bout = dram("bout", [1, 2 * H], F32, kind="ExternalInput").ap()
    w1T = dram("w1T", [H, DH], F32, kind="ExternalInput").ap()
    w2T = dram("w2T", [H, DH], F32, kind="ExternalInput").ap()
    h1T = dram("h1T", [DH, DH], F32, kind="ExternalInput").ap()
    h2T = dram("h2T", [DH, DH], F32, kind="ExternalInput").ap()
    w3T = dram("w3T", [DH, H], F32, kind="ExternalInput").ap()
    s1_in = dram("s1", [1, DH], F32, kind="ExternalInput").ap()
    s2_in = dram("s2", [1, DH], F32, kind="ExternalInput").ap()
    sh1_in = dram("sh1", [1, DH], F32, kind="ExternalInput").ap()
    sh2_in = dram("sh2", [1, DH], F32, kind="ExternalInput").ap()
    s3_in = dram("s3", [1, H], F32, kind="ExternalInput").ap()
    consts = {}
    for nm, shp in (("cB1", [128, 256]), ("cB1R", [128, 512]),
                    ("cOmRe", [M2 * 128, 128]), ("cOmIm", [M2 * 128, 128]),
                    ("cOmImN", [M2 * 128, 128]), ("cPhRe", [M2 * 128, 128]),
                    ("cPhIm", [M2 * 128, 128]), ("cPhImN", [M2 * 128, 128]),
                    ("cTh0", [128, 128]), ("cTh1", [128, 128]),
                    ("cIdent", [128, 128])):
        consts[nm] = dram(nm, shp, F16, kind="ExternalInput").ap()

    y_out = dram("y", [2, L, H], F32, kind="ExternalOutput").ap()

    xf16 = dram("xf16", [2 * L, H], F16, kind="Internal").ap()
    s_xnh = dram("s_xnh", [CH_SL, NF], F16, kind="Internal").ap()
    s_kern = dram("s_kern", [CH_SL, L], F16, kind="Internal").ap()
    wout16 = dram("wout16", [H, 2 * H], F16, kind="Internal").ap()
    w116 = dram("w116", [H, DH], F16, kind="Internal").ap()
    w216 = dram("w216", [H, DH], F16, kind="Internal").ap()
    h116 = dram("h116", [DH, DH], F16, kind="Internal").ap()
    h216 = dram("h216", [DH, DH], F16, kind="Internal").ap()
    w316 = dram("w316", [DH, H], F16, kind="Internal").ap()
    if DEBUG:
        dbg_xn = dram("dbg_xn", [128, 32 * 1024], F16, kind="ExternalOutput").ap()
        dbg_y2 = dram("dbg_y2", [128, 32 * 1024], F16, kind="ExternalOutput").ap()
        dbg_x1 = dram("dbg_x1", [128, 32 * 1024], F16, kind="ExternalOutput").ap()

    cnt = {}

    def tinc(instr, sem, name, n=1):
        instr.then_inc(sem, n)
        cnt[name] = cnt.get(name, 0) + n

    def inc(e, sem, name, n=1):
        last = getattr(e, "last", None)
        if last is not None:
            last.then_inc(sem, n)
            e.last = None
        else:
            try:
                en = str(e.engine)
            except Exception:
                en = ""
            if ("DVE" in en) or ("Activation" in en) or ("Pool" in en):
                e.drain()
            e.sem_inc(sem, n)
        cnt[name] = cnt.get(name, 0) + n

    st_main = contextlib.ExitStack()
    with st_main:
        TB1 = st_main.enter_context(nc.sbuf_tensor("TB1", [128, 32 * 1024], F16))
        ident = st_main.enter_context(nc.sbuf_tensor("ident", [128, 128], F16))
        b1sb = st_main.enter_context(nc.sbuf_tensor("b1sb", [128, 256], F16))
        b1rsb = st_main.enter_context(nc.sbuf_tensor("b1rsb", [128, 512], F16))
        th0sb = st_main.enter_context(nc.sbuf_tensor("th0sb", [128, 128], F16))
        th1sb = st_main.enter_context(nc.sbuf_tensor("th1sb", [128, 128], F16))
        dbc = st_main.enter_context(nc.sbuf_tensor("dbc", [128, H], F32))
        dtbc = st_main.enter_context(nc.sbuf_tensor("dtbc", [128, H], F32))
        prmA = st_main.enter_context(nc.sbuf_tensor("prmA", [128, H], F32))
        prmI = st_main.enter_context(nc.sbuf_tensor("prmI", [128, H], F32))
        prmCr = st_main.enter_context(nc.sbuf_tensor("prmCr", [128, H], F16))
        prmCi = st_main.enter_context(nc.sbuf_tensor("prmCi", [128, H], F16))
        bosb = st_main.enter_context(nc.sbuf_tensor("bosb", [128, 16], F32))
        s1sb = st_main.enter_context(nc.sbuf_tensor("s1sb", [128, 32], F32))
        s2sb = st_main.enter_context(nc.sbuf_tensor("s2sb", [128, 32], F32))
        sh1sb = st_main.enter_context(nc.sbuf_tensor("sh1sb", [128, 32], F32))
        sh2sb = st_main.enter_context(nc.sbuf_tensor("sh2sb", [128, 32], F32))
        s3sb = st_main.enter_context(nc.sbuf_tensor("s3sb", [128, 8], F32))
        row1 = st_main.enter_context(nc.sbuf_tensor("row1", [1, 2 * H], F32))
        epsc = st_main.enter_context(nc.sbuf_tensor("epsc", [128, 1], F32))
        ones1 = st_main.enter_context(nc.sbuf_tensor("ones1", [1, 128], F32))
        negone = st_main.enter_context(nc.sbuf_tensor("negone", [128, 64], F32))
        nhalf = st_main.enter_context(nc.sbuf_tensor("nhalf", [128, 1], F32))
        psA = st_main.enter_context(nc.psum_tensor("psA", [128, 512], F32))
        psB = st_main.enter_context(nc.psum_tensor("psB", [128, 512], F32))
        psC = st_main.enter_context(nc.psum_tensor("psC", [128, 512], F32))
        psD = st_main.enter_context(nc.psum_tensor("psD", [128, 512], F32))
        psE = st_main.enter_context(nc.psum_tensor("psE", [128, 512], F32))
        psF = st_main.enter_context(nc.psum_tensor("psF", [128, 512], F32))
        psG = st_main.enter_context(nc.psum_tensor("psG", [128, 512], F32))
        psH = st_main.enter_context(nc.psum_tensor("psH", [128, 512], F32))
        qS = st_main.enter_context(nc.semaphore("qS"))
        qG = st_main.enter_context(nc.semaphore("qG"))
        G = st_main.enter_context(nc.semaphore("G"))
        pP = st_main.enter_context(nc.semaphore("pP"))
        pV = st_main.enter_context(nc.semaphore("pV"))
        pS = st_main.enter_context(nc.semaphore("pS"))
        pG2 = st_main.enter_context(nc.semaphore("pG2"))
        wA = st_main.enter_context(nc.semaphore("wA"))
        wB = st_main.enter_context(nc.semaphore("wB"))
        cW = st_main.enter_context(nc.semaphore("cW"))
        xA = st_main.enter_context(nc.semaphore("xA"))
        xB = st_main.enter_context(nc.semaphore("xB"))
        blk = st_main.enter_context(nc.Block())
        class _Rec:
            def __init__(self, e):
                self._e = e
                self.last = None
            def __getattr__(self, name):
                f = getattr(self._e, name)
                if name in ("wait_ge", "sem_inc", "drain", "engine", "dma_start"):
                    return f
                if not callable(f):
                    return f
                def g(*a, **k):
                    r = f(*a, **k)
                    self.last = r
                    return r
                return g

        class _BlkWrap:
            def __init__(self, b):
                self._b = b
            def __getattr__(self, engname):
                dec = getattr(self._b, engname)
                if engname.startswith("_"):
                    return dec
                def mydec(fn):
                    def fn2(e):
                        if engname in ("vector", "scalar", "gpsimd"):
                            return fn(_Rec(e))
                        return fn(e)
                    return dec(fn2)
                return mydec

        blk = _BlkWrap(blk)

        banks = [psA, psB, psC, psD, psE, psF, psG, psH]

        def dma(e, sem, name, out, in_, **kw):
            e.dma_start(out=out, in_=in_, **kw).then_inc(sem, 16)
            cnt[name] = cnt.get(name, 0) + 16
            return cnt[name]

        def bar(engines_inc):
            for ename in engines_inc:
                def f(e):
                    inc(e, G, "G")
                getattr(blk, ename)(f)
            tgt = cnt.get("G", 0)
            for ename in ("sync", "gpsimd", "tensor", "vector", "scalar"):
                def f(e, _t=tgt):
                    e.wait_ge(G, _t)
                getattr(blk, ename)(f)

        def cmb(e, dre, dim_, are, aim, bre, bim, tmp):
            e.tensor_tensor(tmp, aim, bim, ALU.mult)
            e.tensor_tensor(dre, are, bre, ALU.mult)
            e.drain()
            e.tensor_tensor(dre, dre, tmp, ALU.subtract)
            e.tensor_tensor(tmp, aim, bre, ALU.mult)
            e.tensor_tensor(dim_, are, bim, ALU.mult)
            e.drain()
            e.tensor_tensor(dim_, dim_, tmp, ALU.add)

        wcast_tgt = {}

        @blk.gpsimd
        def _(e):
            for dst, srcp in ((prmCr, creT), (prmCi, cimT)):
                dma(e, qG, "qG", dst.ap()[0:64], srcp)
                dma(e, qG, "qG", dst.ap()[64:128], srcp)
            e.wait_ge(qG, cnt["qG"])
            inc(e, pG2, "pG2")
            cnt["prm_ready"] = cnt["pG2"]
            dma(e, qG, "qG", xf16, x_in.rearrange("b t h -> (b t) h"), max_dma_last_dim=4096)
            e.wait_ge(qG, cnt["qG"])
            inc(e, pG2, "pG2")
            cnt["xf16_ready"] = cnt["pG2"]
            wlist = (("wout", wout, wout16), ("w1", w1T, w116),
                     ("w2", w2T, w216), ("w3", w3T, w316),
                     ("h1", h1T, h116), ("h2", h2T, h216))
            if PHASES == "Cpre":
                wlist = wlist[:1]
            elif PHASES in ("C6a", "C6b", "C6c", "C45"):
                pass
            elif PHASES != "ALL":
                wlist = ()
            for nm, srcw, dst in wlist:
                dma(e, qG, "qG", dst, srcw, max_dma_last_dim=8192)
                e.wait_ge(qG, cnt["qG"])
                inc(e, pG2, "pG2")
                wcast_tgt[nm] = cnt["pG2"]

        @blk.sync
        def _(e):
            dma(e, qS, "qS", ident[:], consts["cIdent"])
            dma(e, qS, "qS", b1sb[:], consts["cB1"])
            dma(e, qS, "qS", b1rsb[:], consts["cB1R"])
            dma(e, qS, "qS", th0sb[:], consts["cTh0"])
            dma(e, qS, "qS", th1sb[:], consts["cTh1"])
            for dst, src in ((prmA, arlT), (prmI, aimT)):
                dma(e, qS, "qS", dst.ap()[0:64], src)
                dma(e, qS, "qS", dst.ap()[64:128], src)
            e.wait_ge(pG2, cnt["prm_ready"])
            dma(e, qS, "qS", row1[:1, 0:H], log_dt)
            with nc.allow_non_contiguous_dma(reason="tiny one-time scale loads"):
                for dst, src, n in ((s1sb, s1_in, 32), (s2sb, s2_in, 32),
                                    (sh1sb, sh1_in, 32), (sh2sb, sh2_in, 32),
                                    (s3sb, s3_in, 8), (bosb, bout, 16)):
                    dma(e, qS, "qS", dst.ap().rearrange("p (c o) -> p c o", c=n, o=1),
                        src.rearrange("o (c p) -> p c o", c=n, p=128))
            e.wait_ge(qS, cnt["qS"])
            inc(e, G, "G")
        g_a = cnt["G"]

        @blk.vector
        def _(e):
            e.memset(epsc[:], 1e-5)
            e.memset(ones1[:1, :], 1.0)
            e.memset(negone[:], -1.0)
            e.memset(nhalf[:], -0.5)
            e.drain()

        @blk.scalar
        def _(e):
            e.wait_ge(G, g_a)
            e.activation(row1[:1, 0:H], row1[:1, 0:H], AF.Exp)
            e.activation(prmA[:], prmA[:], AF.Exp)
            inc(e, pS, "pS")

        @blk.vector
        def _(e):
            e.wait_ge(pS, cnt["pS"])
            e.tensor_scalar(prmA[:], prmA[:], -1.0, None, ALU.mult, ALU.bypass)
            inc(e, pV, "pV")

        @blk.sync
        def _(e):
            e.wait_ge(pS, cnt["pS"])
            dma(e, qS, "qS", dtbc.ap()[0:1, 0:H], row1[:1, 0:H])
            e.wait_ge(qS, cnt["qS"])
            w = 1
            while w < 128:
                dma(e, qS, "qS", dtbc.ap()[w:2 * w, 0:H], dtbc.ap()[0:w, 0:H])
                e.wait_ge(qS, cnt["qS"])
                w *= 2
            dma(e, qS, "qS", row1[:1, 0:H], d_in)
            e.wait_ge(qS, cnt["qS"])
            dma(e, qS, "qS", dbc.ap()[0:1, 0:H], row1[:1, 0:H])
            e.wait_ge(qS, cnt["qS"])
            w = 1
            while w < 128:
                dma(e, qS, "qS", dbc.ap()[w:2 * w, 0:H], dbc.ap()[0:w, 0:H])
                e.wait_ge(qS, cnt["qS"])
                w *= 2
            inc(e, G, "G")
        bar(["sync"])

        with contextlib.ExitStack() as st_a:
            xa = st_a.enter_context(nc.sbuf_tensor("xa", [128, H], F16))
            xb = st_a.enter_context(nc.sbuf_tensor("xb", [128, H], F16))
            bn6 = st_a.enter_context(nc.sbuf_tensor("bn6", [128, 12], F32))
            bn2 = st_a.enter_context(nc.sbuf_tensor("bn2", [128, 2], F32))
            mrr = st_a.enter_context(nc.sbuf_tensor("mrr", [128, 2], F32))
            xv = x_in.rearrange("b (t2 t1) h -> (b t2) t1 h", t1=128)
            for ti in range(32):
                buf = xa if ti % 2 == 0 else xb

                xsem = xA if ti % 2 == 0 else xB
                xnm = "xA" if ti % 2 == 0 else "xB"

                @blk.gpsimd
                def _(e, _ti=ti, _buf=buf, _xs=xsem, _xn=xnm):
                    if _ti >= 2:
                        e.wait_ge(pV, cnt.get("pV", 0) - 1)
                    dma(e, _xs, _xn, _buf[:], xv[_ti])
                qg_t = cnt[xnm]

                @blk.vector
                def _(e, _buf=buf, _qt=qg_t, _xs=xsem):
                    e.wait_ge(_xs, _qt)
                    e.bn_stats(bn6[:, 0:6], _buf[:, 0:512])
                    e.bn_stats(bn6[:, 6:12], _buf[:, 512:1024])
                    e.drain()
                    e.bn_aggr(bn2[:], bn6[:])
                    inc(e, pV, "pV")

                @blk.gpsimd
                def _(e):
                    e.wait_ge(pV, cnt["pV"])
                    e.tensor_scalar(mrr[:, 1:2], bn2[:, 1:2], 1e-5, None, ALU.add, ALU.bypass)
                    e.drain()
                    e.tensor_tensor(mrr[:, 0:1], mrr[:, 1:2], nhalf[:, 0:1], ALU.pow)
                    inc(e, pG2, "pG2")

                @blk.vector
                def _(e, _ti=ti, _buf=buf):
                    e.wait_ge(pG2, cnt["pG2"])
                    e.tensor_scalar(TB1[:, _ti * 1024:(_ti + 1) * 1024], _buf[:],
                                    bn2[:, 0:1], mrr[:, 0:1], ALU.subtract, ALU.mult)
                    inc(e, pV, "pV")
            bar(["vector"])
        if DEBUG:
            @blk.sync
            def _(e):
                dma(e, qS, "qS", dbg_xn, TB1[:])
                e.wait_ge(qS, cnt["qS"])
            bar(["sync"])

        def finish_dump(extra=None, nc_extra=4096):
            @blk.gpsimd
            def _(e):
                e.wait_ge(G, cnt.get("G", 0))
                e.wait_ge(pV, cnt.get("pV", 0))
                e.wait_ge(pS, cnt.get("pS", 0))
                dma(e, qG, "qG",
                    y_out.rearrange("b (t2 t1) h -> t1 b t2 h", t1=128),
                    TB1.ap().rearrange("p (b t2 h) -> p b t2 h", b=2, t2=16, h=1024),
                    max_dma_last_dim=4096)
                if extra is not None:
                    e.wait_ge(qG, cnt["qG"])
                    yv = y_out.rearrange("b (tq p t1q) h -> b tq p (t1q h)",
                                         p=128, t1q=1)
                    for j in range(nc_extra // 1024):
                        dma(e, qG, "qG", yv[1, j],
                            extra.ap()[:, j * 1024:(j + 1) * 1024])
                e.wait_ge(qG, cnt["qG"])

            @blk.sync
            def _(e):
                e.wait_ge(qS, cnt["qS"])

        if PHASES == "A":
            finish_dump()
            return nc

        NHG = CH_SL // 4
        NHGP = CH_SL // 8
        NHSG = CH_SL // 4
        NPL = M2 * CH_SL

        with contextlib.ExitStack() as st_b:
            constb = st_b.enter_context(nc.sbuf_tensor("constb", [128, M2 * 3 * 128], F16))
            xnh = st_b.enter_context(nc.sbuf_tensor("xnh", [128, NF], F16))
            zst = st_b.enter_context(nc.sbuf_tensor("zst", [128, NHG * 128], F16))
            Ssl = st_b.enter_context(nc.sbuf_tensor("Ssl", [128, 2 * NPL], F16))
            Zsl = st_b.enter_context(nc.sbuf_tensor("Zsl", [128, 2 * NPL], F16))
            KFsl = st_b.enter_context(nc.sbuf_tensor("KFsl", [128, 2 * NPL], F16))
            WmT = st_b.enter_context(nc.sbuf_tensor("WmT", [128, 2 * NPL], F16))
            kstr = st_b.enter_context(nc.sbuf_tensor("kstr", [128, NHGP * 128], F16))
            qre = st_b.enter_context(nc.sbuf_tensor("qre", [128, KSUB * 128], F32))
            qim = st_b.enter_context(nc.sbuf_tensor("qim", [128, KSUB * 128], F32))
            pre = st_b.enter_context(nc.sbuf_tensor("pre", [128, KSUB * 16], F32))
            pim = st_b.enter_context(nc.sbuf_tensor("pim", [128, KSUB * 16], F32))
            pcmb = st_b.enter_context(nc.sbuf_tensor("pcmb", [128, KSUB * 16], F16))
            qcmb = st_b.enter_context(nc.sbuf_tensor("qcmb", [128, KSUB * 128], F16))
            lre = st_b.enter_context(nc.sbuf_tensor("lre", [128, KSUB], F32))
            lim = st_b.enter_context(nc.sbuf_tensor("lim", [128, KSUB], F32))
            pwr = st_b.enter_context(nc.sbuf_tensor("pwr", [128, KSUB], F32))
            pwi = st_b.enter_context(nc.sbuf_tensor("pwi", [128, KSUB], F32))
            t0 = st_b.enter_context(nc.sbuf_tensor("t0", [128, KSUB], F32))
            t1b = st_b.enter_context(nc.sbuf_tensor("t1b", [128, KSUB], F32))
            t2b = st_b.enter_context(nc.sbuf_tensor("t2b", [128, KSUB], F32))
            t3b = st_b.enter_context(nc.sbuf_tensor("t3b", [128, KSUB], F32))
            ctre = st_b.enter_context(nc.sbuf_tensor("ctre", [128, KSUB], F32))
            ctim = st_b.enter_context(nc.sbuf_tensor("ctim", [128, KSUB], F32))
            big32 = st_b.enter_context(nc.sbuf_tensor("big32", [128, 1024], F32))
            ust0 = st_b.enter_context(nc.sbuf_tensor("ust0", [128, 128], F16))
            ust1 = st_b.enter_context(nc.sbuf_tensor("ust1", [128, 128], F16))
            gtmp = st_b.enter_context(nc.sbuf_tensor("gtmp", [128, 512], F32))
            xnds = st_b.enter_context(nc.sbuf_tensor("xnds", [128, 32 * CH_SL], F16))
            def om(pl, m2):
                return constb[:, (pl * M2 + m2) * 128:(pl * M2 + m2) * 128 + 128]

            def load_tables(pre_fix):
                @blk.sync
                def _(e):
                    for i, nm in enumerate(("Re", "Im", "ImN")):
                        dma(e, qS, "qS",
                            constb[:, i * M2 * 128:(i + 1) * M2 * 128].rearrange(
                                "p (m k) -> p m k", m=M2, k=128),
                            consts["c" + pre_fix + nm].rearrange(
                                "(m p) k -> p m k", m=M2))
                    e.wait_ge(qS, cnt["qS"])
                    inc(e, G, "G")
                return cnt["G"]

            def stage2(src, dstbuf, tgt_qs):
                @blk.tensor
                def _(e, _t=tgt_qs):
                    e.wait_ge(G, _t)
                for m2 in range(M2):
                    bkr = banks[4 + (m2 % 2) * 2]
                    bki = banks[5 + (m2 % 2) * 2]

                    @blk.tensor
                    def _(e, _m2=m2, _r=bkr, _i=bki):
                        if _m2 >= 2:
                            e.wait_ge(pS, cnt["pS"] - 1)
                        sre = src[:, _m2 * CH_SL:(_m2 + 1) * CH_SL]
                        sim = src[:, NPL + _m2 * CH_SL:NPL + (_m2 + 1) * CH_SL]
                        e.matmul(_r[:, 0:CH_SL], om(0, _m2), sre, start=True, stop=False)
                        mm1 = e.matmul(_r[:, 0:CH_SL], om(2, _m2), sim, start=False, stop=True)
                        tinc(mm1, pP, "pP")
                        e.matmul(_i[:, 0:CH_SL], om(1, _m2), sre, start=True, stop=False)
                        mm2 = e.matmul(_i[:, 0:CH_SL], om(0, _m2), sim, start=False, stop=True)
                        tinc(mm2, pP, "pP")

                    @blk.scalar
                    def _(e, _m2=m2, _r=bkr, _i=bki):
                        e.wait_ge(pP, cnt["pP"])
                        e.copy(dstbuf[:, _m2 * CH_SL:(_m2 + 1) * CH_SL], _r[:, 0:CH_SL])
                        e.copy(dstbuf[:, NPL + _m2 * CH_SL:NPL + (_m2 + 1) * CH_SL],
                               _i[:, 0:CH_SL])
                        inc(e, pS, "pS")
                bar(["scalar"])

            for sl in range(1 if PHASES == "B1" else N_SL):
                h0 = sl * CH_SL
                om_ld = load_tables("Om")

                for ti in range(32):
                    bk = banks[ti % 2]

                    @blk.tensor
                    def _(e, _ti=ti, _bk=bk):
                        if _ti >= 2 or sl > 0:
                            e.wait_ge(pS, cnt["pS"] - 1)
                        mm = e.matmul(_bk[0:CH_SL, 0:128],
                                 TB1[:, _ti * 1024 + h0:_ti * 1024 + h0 + CH_SL],
                                 ident[:], start=True, stop=True)
                        tinc(mm, pP, "pP")

                    @blk.scalar
                    def _(e, _ti=ti, _bk=bk):
                        e.wait_ge(pP, cnt["pP"])
                        e.copy(xnh[0:CH_SL, _ti * 128:(_ti + 1) * 128],
                               _bk[0:CH_SL, 0:128])
                        inc(e, pS, "pS")
                bar(["scalar"])

                if PHASES == "B0":
                    finish_dump(xnh)
                    return nc
                @blk.sync
                def _(e):
                    dma(e, qS, "qS", s_xnh, xnh[0:CH_SL, :])
                    e.wait_ge(qS, cnt["qS"])
                    srcv = s_xnh.rearrange("(hg s) (c t2 t1) -> s c t2 hg t1",
                                           hg=NHG, s=4, c=2, t2=16, t1=128)
                    for s in range(4):
                        for c in range(2):
                            dstp = zst.ap()[s * 32 + c:s * 32 + c + 31:2]
                            dma(e, qS, "qS",
                                dstp.rearrange("t2 (hg t1) -> t2 hg t1",
                                               hg=NHG, t1=128),
                                srcv[s, c])
                    e.wait_ge(qS, cnt["qS"])
                    inc(e, G, "G")
                g0 = cnt["G"]

                if PHASES == "B1s":
                    finish_dump(zst, 2048)
                    return nc
                for hg in range(NHG):
                    bk = banks[2 + hg % 2]

                    @blk.tensor
                    def _(e, _hg=hg, _bk=bk, _g0=g0):
                        if _hg == 0:
                            e.wait_ge(G, _g0)
                        if _hg >= 2:
                            e.wait_ge(pV, cnt["pV"] - 1)
                        mm = e.matmul(_bk[:, 0:256], zst[:, _hg * 128:(_hg + 1) * 128],
                                 b1sb[:], start=True, stop=True)
                        tinc(mm, pP, "pP")

                    @blk.scalar
                    def _(e, _hg=hg, _bk=bk):
                        e.wait_ge(pP, cnt["pP"])
                        dstv = Ssl.ap().rearrange(
                            "p (cp m2 hg s) -> p hg s m2 cp",
                            cp=2, m2=M2, hg=NHG, s=4)[:, _hg]
                        e.copy(dstv, _bk[:, 0:256].rearrange(
                            "p (s m2 cp) -> p s m2 cp", s=4, m2=M2, cp=2))
                        inc(e, pV, "pV")
                bar(["scalar"])

                if PHASES == "B2":
                    finish_dump(Ssl)
                    return nc
                stage2(Ssl, Zsl, om_ld)

                if PHASES == "B3":
                    finish_dump(Zsl)
                    return nc
                for sub in range(CH_SL // KSUB):
                    cs = slice(h0 + sub * KSUB, h0 + sub * KSUB + KSUB)

                    @blk.vector
                    def _(e):
                        e.tensor_copy(t0[:], prmA[:, cs])
                        e.drain()
                        e.tensor_tensor(t1b[:], t0[:], dtbc[:, cs], ALU.mult)
                        e.tensor_tensor(t2b[:], prmI[:, cs], dtbc[:, cs], ALU.mult)
                        e.drain()
                        e.tensor_scalar(t3b[:], t2b[:], INV2PI, None, ALU.mult, ALU.bypass)
                        e.drain()
                        e.tensor_scalar(t3b[:], t3b[:], MAGIC, MAGIC, ALU.add, ALU.subtract)
                        e.drain()
                        e.scalar_tensor_tensor(pwr[:], t3b[:], -CW1, t2b[:], ALU.mult, ALU.add)
                        e.drain()
                        e.scalar_tensor_tensor(pwr[:], t3b[:], -CW2, pwr[:], ALU.mult, ALU.add)
                        e.drain()
                        e.scalar_tensor_tensor(pwr[:], t3b[:], -CW3, pwr[:], ALU.mult, ALU.add)
                        e.tensor_scalar(t2b[:], t2b[:], HALFPI, None, ALU.add, ALU.bypass)
                        e.drain()
                        e.tensor_scalar(t3b[:], t2b[:], INV2PI, None, ALU.mult, ALU.bypass)
                        e.drain()
                        e.tensor_scalar(t3b[:], t3b[:], MAGIC, MAGIC, ALU.add, ALU.subtract)
                        e.drain()
                        e.scalar_tensor_tensor(pwi[:], t3b[:], -CW1, t2b[:], ALU.mult, ALU.add)
                        e.drain()
                        e.scalar_tensor_tensor(pwi[:], t3b[:], -CW2, pwi[:], ALU.mult, ALU.add)
                        e.drain()
                        e.scalar_tensor_tensor(pwi[:], t3b[:], -CW3, pwi[:], ALU.mult, ALU.add)
                        inc(e, pV, "pV")

                    @blk.scalar
                    def _(e):
                        e.wait_ge(pV, cnt["pV"])
                        e.activation(t1b[:], t1b[:], AF.Exp)
                        e.activation(pwr[:], pwr[:], AF.Sin)
                        e.activation(pwi[:], pwi[:], AF.Sin)
                        inc(e, pS, "pS")

                    @blk.vector
                    def _(e):
                        e.wait_ge(pS, cnt["pS"])
                        e.tensor_tensor(lre[:], t1b[:], pwi[:], ALU.mult)
                        e.tensor_tensor(lim[:], t1b[:], pwr[:], ALU.mult)
                        inc(e, pV, "pV")
                    if PHASES == "B4l" and sub == 0:
                        @blk.vector
                        def _(e):
                            kd = xnh
                            e.tensor_copy(kd[:, 0:16], lre[:])
                            e.tensor_copy(kd[:, 16:32], lim[:])
                            e.tensor_copy(kd[:, 32:48], t1b[:])
                            e.tensor_copy(kd[:, 48:64], pwr[:])
                            e.tensor_copy(kd[:, 64:80], pwi[:])
                            e.tensor_copy(kd[:, 80:96], dtbc[:, cs])
                            e.tensor_copy(kd[:, 96:112], prmI[:, cs])
                            e.tensor_copy(kd[:, 112:128], prmA[:, cs])
                            inc(e, pV, "pV")
                        bar(["vector"])
                        finish_dump(xnh, 1024)
                        return nc

                    @blk.scalar
                    def _(e):
                        e.wait_ge(pV, cnt["pV"])
                        e.activation(t1b[:], t0[:], AF.Square)
                        e.activation(t2b[:], prmI[:, cs], AF.Square)
                        inc(e, pS, "pS")

                    @blk.vector
                    def _(e):
                        e.wait_ge(pS, cnt["pS"])
                        e.tensor_tensor(t1b[:], t1b[:], t2b[:], ALU.add)
                        inc(e, pV, "pV")

                    @blk.gpsimd
                    def _(e):
                        e.wait_ge(pV, cnt["pV"])
                        e.tensor_tensor(t1b[:], t1b[:], negone[:, 0:KSUB], ALU.pow)
                        inc(e, pG2, "pG2")

                    @blk.vector
                    def _(e):
                        e.wait_ge(pG2, cnt["pG2"])
                        e.tensor_tensor(t2b[:], t0[:], t1b[:], ALU.mult)
                        e.scalar_tensor_tensor(t3b[:], prmI[:, cs], -1.0, t1b[:],
                                               ALU.mult, ALU.mult)
                        e.drain()
                        e.tensor_scalar(t0[:], lre[:], -1.0, None, ALU.add, ALU.bypass)
                        e.drain()
                        cmb(e, ctre[:], ctim[:], t0[:], lim[:], t2b[:], t3b[:], t1b[:])
                        e.drain()
                        cmb(e, t2b[:], t3b[:], ctre[:], ctim[:],
                            prmCr[:, cs], prmCi[:, cs], t1b[:])
                        e.drain()
                        e.tensor_copy(ctre[:], t2b[:])
                        e.tensor_copy(ctim[:], t3b[:])
                        e.tensor_copy(pwr[:], lre[:])
                        e.tensor_copy(pwi[:], lim[:])
                        e.drain()
                        for _ in range(7):
                            e.tensor_tensor(t0[:], pwr[:], pwr[:], ALU.mult)
                            e.tensor_tensor(t1b[:], pwi[:], pwi[:], ALU.mult)
                            e.tensor_tensor(t2b[:], pwr[:], pwi[:], ALU.mult)
                            e.drain()
                            e.tensor_tensor(pwr[:], t0[:], t1b[:], ALU.subtract)
                            e.tensor_scalar(pwi[:], t2b[:], 2.0, None, ALU.mult, ALU.bypass)
                            e.drain()
                        pr = pre.ap().rearrange("p (ch l) -> p l ch", ch=KSUB, l=16)
                        pi = pim.ap().rearrange("p (ch l) -> p l ch", ch=KSUB, l=16)
                        e.tensor_copy(pr[:, 0], ctre[:])
                        e.tensor_copy(pi[:, 0], ctim[:])
                        inc(e, pV, "pV")

                    for kstep in range(4):
                        w = 2 ** kstep

                        @blk.vector
                        def _(e, _w=w):
                            pr = pre.ap().rearrange("p (ch l) -> p l ch", ch=KSUB, l=16)
                            pi = pim.ap().rearrange("p (ch l) -> p l ch", ch=KSUB, l=16)
                            are, aim = pr[:, 0:_w], pi[:, 0:_w]
                            dre, dim_ = pr[:, _w:2 * _w], pi[:, _w:2 * _w]
                            bre = pwr.ap().rearrange("p (u ch) -> p u ch", u=1, ch=KSUB)
                            bim = pwi.ap().rearrange("p (u ch) -> p u ch", u=1, ch=KSUB)
                            bre2, are2 = bass.broadcast_tensor_aps(bre, are)
                            bim2, aim2 = bass.broadcast_tensor_aps(bim, aim)
                            tmp = big32[:, 0:_w * KSUB].rearrange(
                                "p (l c) -> p l c", l=_w, c=KSUB)
                            cmb(e, dre, dim_, are2, aim2, bre2, bim2, tmp)
                            if _w < 8:
                                e.drain()
                                e.tensor_tensor(t0[:], pwr[:], pwr[:], ALU.mult)
                                e.tensor_tensor(t1b[:], pwi[:], pwi[:], ALU.mult)
                                e.tensor_tensor(t2b[:], pwr[:], pwi[:], ALU.mult)
                                e.drain()
                                e.tensor_tensor(pwr[:], t0[:], t1b[:], ALU.subtract)
                                e.tensor_scalar(pwi[:], t2b[:], 2.0, None,
                                                ALU.mult, ALU.bypass)
                            inc(e, pV, "pV")

                    @blk.vector
                    def _(e):
                        qr = qre.ap().rearrange("p (ch l) -> p l ch", ch=KSUB, l=128)
                        qi = qim.ap().rearrange("p (ch l) -> p l ch", ch=KSUB, l=128)
                        e.memset(qr[:, 0], 1.0)
                        e.memset(qi[:, 0], 0.0)
                        e.tensor_copy(pwr[:], lre[:])
                        e.tensor_copy(pwi[:], lim[:])
                        e.drain()
                        inc(e, pV, "pV")
                    for kstep in range(7):
                        w = 2 ** kstep

                        @blk.vector
                        def _(e, _w=w):
                            qr = qre.ap().rearrange("p (ch l) -> p l ch", ch=KSUB, l=128)
                            qi = qim.ap().rearrange("p (ch l) -> p l ch", ch=KSUB, l=128)
                            are, aim = qr[:, 0:_w], qi[:, 0:_w]
                            dre, dim_ = qr[:, _w:2 * _w], qi[:, _w:2 * _w]
                            bre = pwr.ap().rearrange("p (u ch) -> p u ch", u=1, ch=KSUB)
                            bim = pwi.ap().rearrange("p (u ch) -> p u ch", u=1, ch=KSUB)
                            bre2, are2 = bass.broadcast_tensor_aps(bre, are)
                            bim2, aim2 = bass.broadcast_tensor_aps(bim, aim)
                            tmp = big32[:, 0:_w * KSUB].rearrange(
                                "p (l c) -> p l c", l=_w, c=KSUB)
                            cmb(e, dre, dim_, are2, aim2, bre2, bim2, tmp)
                            if _w < 64:
                                e.drain()
                                e.tensor_tensor(t0[:], pwr[:], pwr[:], ALU.mult)
                                e.tensor_tensor(t1b[:], pwi[:], pwi[:], ALU.mult)
                                e.tensor_tensor(t2b[:], pwr[:], pwi[:], ALU.mult)
                                e.drain()
                                e.tensor_tensor(pwr[:], t0[:], t1b[:], ALU.subtract)
                                e.tensor_scalar(pwi[:], t2b[:], 2.0, None,
                                                ALU.mult, ALU.bypass)
                            inc(e, pV, "pV")

                    @blk.vector
                    def _(e):
                        e.drain()
                        e.tensor_scalar(pcmb.ap()[0:64], pre.ap()[0:64], 2.0, None,
                                        ALU.mult, ALU.bypass)
                        e.tensor_scalar(pcmb.ap()[64:128], pim.ap()[64:128], -2.0, None,
                                        ALU.mult, ALU.bypass)
                        e.tensor_copy(qcmb.ap()[0:64], qre.ap()[0:64])
                        e.tensor_copy(qcmb.ap()[64:128], qim.ap()[64:128])
                        inc(e, pV, "pV")
                    gv = cnt["pV"]
                    if PHASES == "B4q" and sub == 0:
                        bar(["vector"])
                        finish_dump(qcmb, 2048)
                        return nc
                    if PHASES == "B4p" and sub == 0:
                        bar(["vector"])
                        finish_dump(pcmb, 256)
                        return nc

                    kstg = xnh
                    for ch in range(KSUB):
                        bk = banks[6 + (ch // 4) % 2]
                        fo = (ch % 4) * 128

                        @blk.tensor
                        def _(e, _ch=ch, _bk=bk, _fo=fo, _gv=gv):
                            if _ch == 0:
                                e.wait_ge(pV, _gv)
                            if _ch % 4 == 0 and _ch >= 8:
                                e.wait_ge(pS, cnt["pS"] - 1)
                            mm = e.matmul(_bk[0:16, _fo:_fo + 128],
                                     pcmb[:, _ch * 16:(_ch + 1) * 16],
                                     qcmb[:, _ch * 128:(_ch + 1) * 128],
                                     start=True, stop=True)
                            if _ch % 4 == 3:
                                tinc(mm, pP, "pP")

                        if ch % 4 == 3:
                            @blk.scalar
                            def _(e, _ch=ch, _bk=bk):
                                e.wait_ge(pP, cnt["pP"])
                                base = (_ch - 3) * 128
                                e.copy(kstg[0:16, base:base + 512], _bk[0:16, 0:512])
                                inc(e, pS, "pS")
                    bar(["scalar"])

                    @blk.sync
                    def _(e, _sub=sub):
                        src = kstg[0:16, 0:KSUB * 128].rearrange("t2 (ch t1) -> t2 ch t1",
                                                      ch=KSUB, t1=128)
                        dstv = s_kern.rearrange("(sb ch) (t2 t1) -> sb t2 ch t1",
                                                sb=CH_SL // KSUB, ch=KSUB,
                                                t2=16, t1=128)[_sub]
                        dma(e, qS, "qS", dstv, src)
                        e.wait_ge(qS, cnt["qS"])
                        inc(e, G, "G")
                bar(["sync"])

                @blk.sync
                def _(e):
                    for s in range(8):
                        srcv = s_kern.rearrange("(hgp s) (t2 t1) -> s t2 hgp t1",
                                                hgp=NHGP, s=8, t2=16, t1=128)[s]
                        dma(e, qS, "qS",
                            kstr.ap()[16 * s:16 * s + 16].rearrange(
                                "t2 (hgp t1) -> t2 hgp t1", hgp=NHGP, t1=128),
                            srcv)
                    e.wait_ge(qS, cnt["qS"])
                    inc(e, G, "G")
                g1 = cnt["G"]
                if PHASES == "B4w":
                    finish_dump(kstr, 1024)
                    return nc

                for hgp in range(NHGP):
                    bk = banks[2 + hgp % 2]

                    @blk.tensor
                    def _(e, _h=hgp, _bk=bk, _g=g1):
                        if _h == 0:
                            e.wait_ge(G, _g)
                        if _h >= 2:
                            e.wait_ge(pV, cnt["pV"] - 1)
                        mm = e.matmul(_bk[:, 0:512], kstr[:, _h * 128:(_h + 1) * 128],
                                 b1rsb[:], start=True, stop=True)
                        tinc(mm, pP, "pP")

                    @blk.scalar
                    def _(e, _h=hgp, _bk=bk):
                        e.wait_ge(pP, cnt["pP"])
                        dstv = Ssl.ap().rearrange(
                            "p (cp m2 hgp s) -> p hgp s m2 cp",
                            cp=2, m2=M2, hgp=NHGP, s=8)[:, _h]
                        e.copy(dstv, _bk[:, 0:512].rearrange(
                            "p (s m2 cp) -> p s m2 cp", s=8, m2=M2, cp=2))
                        inc(e, pV, "pV")
                bar(["scalar"])

                if PHASES == "B4r":
                    finish_dump(Ssl)
                    return nc
                stage2(Ssl, KFsl, om_ld)

                ph_ld = load_tables("Ph")

                if PHASES == "B4s":
                    finish_dump(KFsl)
                    return nc
                half = NPL // 2

                @blk.vector
                def _(e):
                    zr, zi = Zsl[:, 0:half], Zsl[:, NPL:NPL + half]
                    kr, ki = KFsl[:, 0:half], KFsl[:, NPL:NPL + half]
                    tt = big32[:, 0:half]
                    e.tensor_tensor(tt, zi, ki, ALU.mult)
                    e.tensor_tensor(WmT[:, 0:half], zr, kr, ALU.mult)
                    e.drain()
                    e.tensor_tensor(WmT[:, 0:half], WmT[:, 0:half], tt, ALU.subtract)
                    e.drain()
                    e.tensor_tensor(tt, zi, kr, ALU.mult)
                    e.tensor_tensor(WmT[:, NPL:NPL + half], zr, ki, ALU.mult)
                    e.drain()
                    e.tensor_tensor(WmT[:, NPL:NPL + half], WmT[:, NPL:NPL + half],
                                    tt, ALU.add)
                    inc(e, pV, "pV")

                @blk.gpsimd
                def _(e):
                    zr, zi = Zsl[:, half:NPL], Zsl[:, NPL + half:2 * NPL]
                    kr, ki = KFsl[:, half:NPL], KFsl[:, NPL + half:2 * NPL]
                    tt = qre[:, 0:half]
                    e.tensor_tensor(tt, zi, ki, ALU.mult)
                    e.tensor_tensor(WmT[:, half:NPL], zr, kr, ALU.mult)
                    e.drain()
                    e.tensor_tensor(WmT[:, half:NPL], WmT[:, half:NPL], tt, ALU.subtract)
                    e.drain()
                    e.tensor_tensor(tt, zi, kr, ALU.mult)
                    e.tensor_tensor(WmT[:, NPL + half:2 * NPL], zr, ki, ALU.mult)
                    e.drain()
                    e.tensor_tensor(WmT[:, NPL + half:2 * NPL],
                                    WmT[:, NPL + half:2 * NPL], tt, ALU.add)
                    inc(e, pG2, "pG2")

                for tix in range(32):
                    @blk.gpsimd
                    def _(e, _ti=tix):
                        e.tensor_tensor(
                            xnds[:, _ti * CH_SL:(_ti + 1) * CH_SL],
                            TB1[:, _ti * 1024 + h0:_ti * 1024 + h0 + CH_SL],
                            dbc[:, h0:h0 + CH_SL], ALU.mult)
                        if _ti == 31:
                            inc(e, pG2, "pG2")
                bar(["vector", "gpsimd"])

                if PHASES == "B5":
                    finish_dump(WmT)
                    return nc
                @blk.tensor
                def _(e, _t=ph_ld):
                    e.wait_ge(G, _t)
                for m2 in range(M2):
                    bkr = banks[4 + (m2 % 2) * 2]
                    bki = banks[5 + (m2 % 2) * 2]

                    @blk.tensor
                    def _(e, _m2=m2, _r=bkr, _i=bki):
                        if _m2 >= 2:
                            e.wait_ge(pS, cnt["pS"] - 1)
                        wre = WmT[:, _m2 * CH_SL:(_m2 + 1) * CH_SL]
                        wim = WmT[:, NPL + _m2 * CH_SL:NPL + (_m2 + 1) * CH_SL]
                        e.matmul(_r[:, 0:CH_SL], om(0, _m2), wre, start=True, stop=False)
                        mm1 = e.matmul(_r[:, 0:CH_SL], om(2, _m2), wim, start=False, stop=True)
                        tinc(mm1, pP, "pP")
                        e.matmul(_i[:, 0:CH_SL], om(1, _m2), wre, start=True, stop=False)
                        mm2 = e.matmul(_i[:, 0:CH_SL], om(0, _m2), wim, start=False, stop=True)
                        tinc(mm2, pP, "pP")

                    @blk.scalar
                    def _(e, _m2=m2, _r=bkr, _i=bki):
                        e.wait_ge(pP, cnt["pP"])
                        for c, bk in ((0, _r), (1, _i)):
                            dstv = Zsl.ap().rearrange(
                                "p (c hsg hssub m2) -> p c m2 hsg hssub",
                                c=2, hsg=NHSG, hssub=4, m2=M2)[:, c, _m2]
                            e.copy(dstv, bk[:, 0:CH_SL].rearrange(
                                "p (hsg hssub) -> p hsg hssub", hsg=NHSG, hssub=4))
                        inc(e, pS, "pS")
                bar(["scalar"])

                if PHASES == "B6":
                    finish_dump(Zsl)
                    return nc
                ust_ready = []
                psy_done = []
                add_done = []
                gelu_done = []
                for hsg in range(NHSG):
                    bky = banks[2 + hsg % 2]
                    go = (hsg % 2) * 128
                    go2 = 256 + (hsg % 2) * 128

                    @blk.tensor
                    def _(e, _hsg=hsg):
                        if _hsg >= 1:
                            e.wait_ge(pS, ust_ready[_hsg - 1])
                        e.matmul(psA[:, 0:128], Zsl[:, _hsg * 128:(_hsg + 1) * 128],
                                 ident[:], start=True, stop=True)
                        mm = e.matmul(psA[:, 128:256],
                                 Zsl[:, NPL + _hsg * 128:NPL + (_hsg + 1) * 128],
                                 ident[:], start=True, stop=True)
                        tinc(mm, pP, "pP")

                    @blk.scalar
                    def _(e):
                        e.wait_ge(pP, cnt["pP"])
                        e.copy(ust0[:], psA[:, 0:128])
                        e.copy(ust1[:], psA[:, 128:256])
                        inc(e, pS, "pS")
                    ust_ready.append(cnt["pS"])

                    @blk.tensor
                    def _(e, _y=bky, _hsg=hsg, _gv=ust_ready[-1]):
                        e.wait_ge(pS, _gv)
                        if _hsg >= 2:
                            e.wait_ge(pS, psy_done[_hsg - 2])
                        e.matmul(_y[:, 0:128], ust0[:], th0sb[:], start=True, stop=False)
                        mm = e.matmul(_y[:, 0:128], ust1[:], th1sb[:], start=False, stop=True)
                        tinc(mm, pP, "pP")

                    @blk.scalar
                    def _(e, _hsg=hsg, _y=bky, _go2=go2):
                        e.wait_ge(pP, cnt["pP"])
                        if _hsg >= 2:
                            e.wait_ge(pV, add_done[_hsg - 2])
                        e.copy(gtmp[:, _go2:_go2 + 128], _y[:, 0:128])
                        inc(e, pS, "pS")
                    psy_done.append(cnt["pS"])

                    @blk.vector
                    def _(e, _hsg=hsg, _go=go, _go2=go2, _pd=psy_done[-1]):
                        e.wait_ge(pS, _pd)
                        av = xnds.ap().rearrange(
                            "p (c t2 hsg hssub) -> p hsg hssub t2 c",
                            c=2, t2=16, hsg=NHSG, hssub=4)[:, _hsg]
                        e.tensor_tensor(
                            gtmp[:, _go:_go + 128].rearrange(
                                "p (hssub t2 c) -> p hssub t2 c", hssub=4, t2=16, c=2),
                            gtmp[:, _go2:_go2 + 128].rearrange(
                                "p (hssub t2 c) -> p hssub t2 c", hssub=4, t2=16, c=2),
                            av, ALU.add)
                        inc(e, pV, "pV")
                    add_done.append(cnt["pV"])

                    @blk.scalar
                    def _(e, _hsg=hsg, _go=go, _sl=sl, _ad=add_done[-1]):
                        e.wait_ge(pV, _ad)
                        dstv = TB1.ap().rearrange(
                            "p (c t2 sl hsg hssub) -> p sl hsg hssub t2 c",
                            c=2, t2=16, sl=N_SL, hsg=NHSG, hssub=4)[:, _sl, _hsg]
                        e.activation(dstv,
                                     gtmp[:, _go:_go + 128].rearrange(
                                         "p (hssub t2 c) -> p hssub t2 c",
                                         hssub=4, t2=16, c=2),
                                     AF.Gelu_apprx_tanh)
                        inc(e, pS, "pS")
                    gelu_done.append(cnt["pS"])
                bar(["scalar"])
        if PHASES in ("B1", "B"):
            finish_dump()
            return nc
        if DEBUG:
            @blk.sync
            def _(e):
                dma(e, qS, "qS", dbg_y2, TB1[:])
                e.wait_ge(qS, cnt["qS"])
            bar(["sync"])

        NT_TG = TGT // 128

        def ln_pass(src_ap, stat, dst_fn, wait_pp=None):
            bn6, bn2, mrr = stat

            @blk.vector
            def _(e):
                e.bn_stats(bn6[:, 0:6], src_ap[:, 0:512])
                e.bn_stats(bn6[:, 6:12], src_ap[:, 512:1024])
                e.drain()
                e.bn_aggr(bn2[:], bn6[:])
                inc(e, pV, "pV")

            @blk.gpsimd
            def _(e):
                e.wait_ge(pV, cnt["pV"])
                e.tensor_scalar(mrr[:, 1:2], bn2[:, 1:2], 1e-5, None, ALU.add, ALU.bypass)
                e.drain()
                e.tensor_tensor(mrr[:, 0:1], mrr[:, 1:2], nhalf[:, 0:1], ALU.pow)
                inc(e, pG2, "pG2")

            @blk.vector
            def _(e):
                e.wait_ge(pG2, cnt["pG2"])
                if wait_pp is not None:
                    e.wait_ge(pP, wait_pp)
                dst_fn(e, bn2[:, 0:1], mrr[:, 0:1])
                inc(e, pV, "pV")

        with contextlib.ExitStack() as st_c:
            y2T = st_c.enter_context(nc.sbuf_tensor("y2T", [128, 8 * TGT], F16))
            attf = st_c.enter_context(nc.sbuf_tensor("attf", [128, 2 * TGT], F16))
            attm = st_c.enter_context(nc.sbuf_tensor("attm", [128, NT_TG * 1024], F16))
            wcA = st_c.enter_context(nc.sbuf_tensor("wcA", [128, 2048], F16))
            wcB = st_c.enter_context(nc.sbuf_tensor("wcB", [128, 2048], F16))
            xtl = st_c.enter_context(nc.sbuf_tensor("xtl", [128, H], F16))
            xtl2 = st_c.enter_context(nc.sbuf_tensor("xtl2", [128, H], F16))
            evt = st_c.enter_context(nc.sbuf_tensor("evt", [128, TGT], F16))
            evt2 = st_c.enter_context(nc.sbuf_tensor("evt2", [128, TGT], F16))
            cb6 = st_c.enter_context(nc.sbuf_tensor("cb6", [128, 12], F32))
            cb2 = st_c.enter_context(nc.sbuf_tensor("cb2", [128, 2], F32))
            cmr = st_c.enter_context(nc.sbuf_tensor("cmr", [128, 2], F32))
            ctt = st_c.enter_context(nc.sbuf_tensor("ctt", [128, H], F16))
            stat = (cb6, cb2, cmr)

            @blk.sync
            def _(e):
                e.wait_ge(pG2, wcast_tgt["wout"])
            woutv = wout16.rearrange("(kc p) o -> p kc o", kc=8, p=128)
            for g in range(N_TG):
                for tt in range(NT_TG):
                    for hc in range(8):
                        bk = banks[(tt * 8 + hc) % 2]
                        tia = g * NT_TG + tt

                        @blk.tensor
                        def _(e, _t=tt, _hc=hc, _bk=bk, _tia=tia, _g=g):
                            if _t + _hc > 0 or _g > 0:
                                e.wait_ge(pS, cnt["pS"] - 1)
                            mm = e.matmul(_bk[:, 0:128],
                                     TB1[:, _tia * 1024 + _hc * 128:
                                         _tia * 1024 + (_hc + 1) * 128],
                                     ident[:], start=True, stop=True)
                            tinc(mm, pP, "pP")

                        @blk.scalar
                        def _(e, _t=tt, _hc=hc, _bk=bk):
                            e.wait_ge(pP, cnt["pP"])
                            e.copy(y2T[:, _hc * TGT + _t * 128:
                                       _hc * TGT + (_t + 1) * 128], _bk[:, 0:128])
                            inc(e, pS, "pS")
                bar(["scalar"])

                for oc in range(8):
                    wb = wcA if oc % 2 == 0 else wcB

                    wsem = wA if oc % 2 == 0 else wB
                    wnm = "wA" if oc % 2 == 0 else "wB"

                    @blk.sync
                    def _(e, _oc=oc, _wb=wb, _ws=wsem, _wn=wnm):
                        if _oc >= 2:
                            e.wait_ge(cW, cnt.get("cW", 0) - 1)
                        dma(e, _ws, _wn, _wb[:, 0:1024].rearrange(
                            "p (kc o) -> p kc o", kc=8, o=128),
                            woutv[:, :, _oc * 128:(_oc + 1) * 128])
                        dma(e, _ws, _wn, _wb[:, 1024:2048].rearrange(
                            "p (kc o) -> p kc o", kc=8, o=128),
                            woutv[:, :, (_oc + 8) * 128:(_oc + 9) * 128])
                    wt = cnt[wnm]

                    @blk.tensor
                    def _(e, _oc=oc, _wb=wb, _wt=wt, _ws=wsem):
                        e.wait_ge(_ws, _wt)
                        if _oc >= 1:
                            e.wait_ge(pV, cnt["pV"] - 0)
                        for kc in range(8):
                            mm = e.matmul(psE[:, 0:TGT], _wb[:, kc * 128:(kc + 1) * 128],
                                     y2T[:, kc * TGT:(kc + 1) * TGT],
                                     start=(kc == 0), stop=(kc == 7))
                        tinc(mm, pP, "pP")
                        for kc in range(8):
                            mm = e.matmul(psF[:, 0:TGT],
                                     _wb[:, 1024 + kc * 128:1024 + (kc + 1) * 128],
                                     y2T[:, kc * TGT:(kc + 1) * TGT],
                                     start=(kc == 0), stop=(kc == 7))
                        tinc(mm, pP, "pP")

                    @blk.scalar
                    def _(e, _oc=oc):
                        e.wait_ge(pP, cnt["pP"])
                        inc(e, cW, "cW")
                        e.activation(evt[:], psE[:, 0:TGT], AF.Identity,
                                     bias=bosb[:, _oc:_oc + 1])
                        e.activation(evt2[:], psF[:, 0:TGT], AF.Sigmoid,
                                     bias=bosb[:, 8 + _oc:9 + _oc])
                        inc(e, pS, "pS")

                    @blk.vector
                    def _(e, _oc=oc):
                        e.wait_ge(pS, cnt["pS"])
                        e.tensor_tensor(attf[:, 0:TGT], evt[:], evt2[:], ALU.mult)
                        inc(e, pV, "pV")
                    gva = cnt["pV"]

                    for tt in range(NT_TG):
                        bk = banks[2 + tt % 2]

                        @blk.tensor
                        def _(e, _oc=oc, _tt=tt, _bk=bk, _gv=gva):
                            e.wait_ge(pV, _gv)
                            if _oc + _tt > 0:
                                e.wait_ge(pS, cnt["pS"] - 1)
                            mm = e.matmul(_bk[:, 0:128], attf[:, _tt * 128:(_tt + 1) * 128],
                                     ident[:], start=True, stop=True)
                            tinc(mm, pP, "pP")

                        @blk.scalar
                        def _(e, _oc=oc, _tt=tt, _bk=bk):
                            e.wait_ge(pP, cnt["pP"])
                            e.copy(attm[:, _tt * 1024 + _oc * 128:
                                        _tt * 1024 + (_oc + 1) * 128], _bk[:, 0:128])
                            inc(e, pS, "pS")
                bar(["scalar"])

                if g == 0:
                    @blk.sync
                    def _(e):
                        e.wait_ge(pG2, cnt['xf16_ready'])
                for tt in range(NT_TG):
                    tia = g * NT_TG + tt
                    xb2 = xtl if tt % 2 == 0 else xtl2
                    xsem = xA if tt % 2 == 0 else xB
                    xnm = "xA" if tt % 2 == 0 else "xB"

                    @blk.sync
                    def _(e, _tia=tia, _xb=xb2, _tt=tt, _xs=xsem, _xn=xnm):
                        if _tt >= 2 or g > 0:
                            e.wait_ge(pV, cnt["pV"] - 2)
                        dma(e, _xs, _xn, _xb[:],
                            xf16.rearrange("(ti t1) h -> ti t1 h", t1=128)[_tia])
                    xt_t = cnt[xnm]

                    @blk.vector
                    def _(e, _t=xt_t, _xs=xsem):
                        e.wait_ge(_xs, _t)

                    src = attm[:, tt * 1024:(tt + 1) * 1024]

                    def dst_fn(e, m_ap, r_ap, _tia=tia, _src=src, _xb=xb2):
                        e.tensor_scalar(ctt[:], _src, m_ap, r_ap,
                                        ALU.subtract, ALU.mult)
                        e.drain()
                        e.tensor_tensor(TB1[:, _tia * 1024:(_tia + 1) * 1024],
                                        ctt[:], _xb[:], ALU.add)
                    ln_pass(src, stat, dst_fn)
                bar(["vector"])
        if DEBUG:
            @blk.sync
            def _(e):
                dma(e, qS, "qS", dbg_x1, TB1[:])
                e.wait_ge(qS, cnt["qS"])
            bar(["sync"])

        if PHASES == "Cpre":
            finish_dump()
            return nc

        with contextlib.ExitStack() as st_f:
            x1n = st_f.enter_context(nc.sbuf_tensor("x1n", [128, H], F16))
            x1nT = st_f.enter_context(nc.sbuf_tensor("x1nT", [128, 8 * TGT], F16))
            HID = st_f.enter_context(nc.sbuf_tensor("HID", [128, 32 * TGT], F16))
            HID2 = st_f.enter_context(nc.sbuf_tensor("HID2", [128, 32 * TGT], F16))
            f3m = st_f.enter_context(nc.sbuf_tensor("f3m", [128, NT_TG * 1024], F16))
            whA = st_f.enter_context(nc.sbuf_tensor("whA", [128, 4096], F16))
            whB = st_f.enter_context(nc.sbuf_tensor("whB", [128, 4096], F16))
            w12A = st_f.enter_context(nc.sbuf_tensor("w12A", [128, 2048], F16))
            w12B = st_f.enter_context(nc.sbuf_tensor("w12B", [128, 2048], F16))
            outb = st_f.enter_context(nc.sbuf_tensor("outb", [128, H], F32))
            fb6 = st_f.enter_context(nc.sbuf_tensor("fb6", [128, 12], F32))
            fb2 = st_f.enter_context(nc.sbuf_tensor("fb2", [128, 2], F32))
            fmr = st_f.enter_context(nc.sbuf_tensor("fmr", [128, 2], F32))
            stat2 = (fb6, fb2, fmr)
            F3 = x1nT
            ftt = x1n

            @blk.sync
            def _(e):
                e.wait_ge(pG2, wcast_tgt["h2"])
            w1v = w116.rearrange("(kc p) o -> p kc o", kc=8, p=128)
            w2v = w216.rearrange("(kc p) o -> p kc o", kc=8, p=128)
            h1v = h116.rearrange("(kc p) o -> p kc o", kc=32, p=128)
            h2v = h216.rearrange("(kc p) o -> p kc o", kc=32, p=128)
            w3v = w316.rearrange("(kc p) o -> p kc o", kc=32, p=128)
            out_t = 0

            for g in range(N_TG):
                for tt in range(NT_TG):
                    tia = g * NT_TG + tt
                    src = TB1[:, tia * 1024:(tia + 1) * 1024]

                    def dst_fn(e, m_ap, r_ap, _src=src):
                        e.tensor_scalar(x1n[:], _src, m_ap, r_ap,
                                        ALU.subtract, ALU.mult)
                    ln_pass(src, stat2, dst_fn,
                            wait_pp=(cnt["pP"] if (tt > 0 or g > 0) else None))
                    gvb = cnt["pV"]
                    for hc in range(8):
                        bk = banks[hc % 2]

                        @blk.tensor
                        def _(e, _tt=tt, _hc=hc, _bk=bk, _gv=gvb):
                            e.wait_ge(pV, _gv)
                            if _tt + _hc > 0 or g > 0:
                                e.wait_ge(pS, cnt["pS"] - 1)
                            mm = e.matmul(_bk[:, 0:128], x1n[:, _hc * 128:(_hc + 1) * 128],
                                     ident[:], start=True, stop=True)
                            tinc(mm, pP, "pP")

                        @blk.scalar
                        def _(e, _tt=tt, _hc=hc, _bk=bk):
                            e.wait_ge(pP, cnt["pP"])
                            e.copy(x1nT[:, _hc * TGT + _tt * 128:
                                        _hc * TGT + (_tt + 1) * 128], _bk[:, 0:128])
                            inc(e, pS, "pS")
                bar(["scalar"])

                if PHASES == "C45" and g == 0:
                    finish_dump(x1nT, 4096)
                    return nc
                for ht in range(32):
                    wb = w12A if ht % 2 == 0 else w12B

                    wsem = wA if ht % 2 == 0 else wB
                    wnm = "wA" if ht % 2 == 0 else "wB"

                    @blk.sync
                    def _(e, _ht=ht, _wb=wb, _ws=wsem, _wn=wnm):
                        if _ht >= 2:
                            e.wait_ge(cW, cnt["cW"] - 1)
                        dma(e, _ws, _wn, _wb[:, 0:1024].rearrange(
                            "p (kc o) -> p kc o", kc=8, o=128),
                            w1v[:, :, _ht * 128:(_ht + 1) * 128])
                        dma(e, _ws, _wn, _wb[:, 1024:2048].rearrange(
                            "p (kc o) -> p kc o", kc=8, o=128),
                            w2v[:, :, _ht * 128:(_ht + 1) * 128])
                    wt = cnt[wnm]

                    @blk.tensor
                    def _(e, _ht=ht, _wb=wb, _wt=wt, _ws=wsem):
                        e.wait_ge(_ws, _wt)
                        if _ht >= 1:
                            e.wait_ge(pV, cnt["pV"])
                        for kc in range(8):
                            mm = e.matmul(psE[:, 0:TGT], _wb[:, kc * 128:(kc + 1) * 128],
                                     x1nT[:, kc * TGT:(kc + 1) * TGT],
                                     start=(kc == 0), stop=(kc == 7))
                        tinc(mm, pP, "pP")
                        for kc in range(8):
                            mm = e.matmul(psF[:, 0:TGT],
                                     _wb[:, 1024 + kc * 128:1024 + (kc + 1) * 128],
                                     x1nT[:, kc * TGT:(kc + 1) * TGT],
                                     start=(kc == 0), stop=(kc == 7))
                        tinc(mm, pP, "pP")

                    @blk.scalar
                    def _(e, _ht=ht):
                        e.wait_ge(pP, cnt["pP"])
                        inc(e, cW, "cW")
                        e.activation(f3m[:, 0:512], psE[:, 0:TGT], AF.Silu,
                                     scale=s1sb[:, _ht:_ht + 1])
                        e.activation(f3m[:, 512:1024], psF[:, 0:TGT], AF.Identity,
                                     scale=s2sb[:, _ht:_ht + 1])
                        inc(e, pS, "pS")

                    @blk.vector
                    def _(e, _ht=ht):
                        e.wait_ge(pS, cnt["pS"])
                        e.tensor_tensor(HID[:, _ht * TGT:(_ht + 1) * TGT],
                                        f3m[:, 0:512], f3m[:, 512:1024], ALU.mult)
                        inc(e, pV, "pV")
                bar(["vector"])

                if PHASES == "C6a" and g == 0:
                    finish_dump(HID, 4096)
                    return nc
                for stagei, (wv, scl, srcbuf, dstbuf) in enumerate(
                        ((h1v, sh1sb, HID, HID2), (h2v, sh2sb, HID2, HID))):
                    for ot in range(32):
                        wb = whA if ot % 2 == 0 else whB

                        wsem = wA if ot % 2 == 0 else wB
                        wnm = "wA" if ot % 2 == 0 else "wB"

                        @blk.sync
                        def _(e, _ot=ot, _wb=wb, _wv=wv, _ws=wsem, _wn=wnm):
                            if _ot >= 2 or stagei > 0 or g > 0:
                                e.wait_ge(cW, cnt["cW"] - 1)
                            dma(e, _ws, _wn, _wb[:].rearrange(
                                "p (kc o) -> p kc o", kc=32, o=128),
                                _wv[:, :, _ot * 128:(_ot + 1) * 128])
                        wt = cnt[wnm]

                        pbk = psE if ot % 2 == 0 else psF

                        @blk.tensor
                        def _(e, _ot=ot, _wb=wb, _wt=wt, _src=srcbuf, _pb=pbk,
                              _ws=wsem):
                            e.wait_ge(_ws, _wt)
                            if _ot >= 2:
                                e.wait_ge(pS, cnt["pS"] - 1)
                            for kc in range(32):
                                mm = e.matmul(_pb[:, 0:TGT], _wb[:, kc * 128:(kc + 1) * 128],
                                         _src[:, kc * TGT:(kc + 1) * TGT],
                                         start=(kc == 0), stop=(kc == 31))
                            tinc(mm, pP, "pP")

                        @blk.scalar
                        def _(e, _ot=ot, _dst=dstbuf, _scl=scl, _pb=pbk):
                            e.wait_ge(pP, cnt["pP"])
                            inc(e, cW, "cW")
                            e.activation(_dst[:, _ot * TGT:(_ot + 1) * TGT],
                                         _pb[:, 0:TGT], AF.Gelu,
                                         scale=_scl[:, _ot:_ot + 1])
                            inc(e, pS, "pS")
                    bar(["scalar"])

                if PHASES == "C6b" and g == 0:
                    finish_dump(HID, 4096)
                    return nc
                for dc in range(8):
                    wb = whA if dc % 2 == 0 else whB

                    wsem = wA if dc % 2 == 0 else wB
                    wnm = "wA" if dc % 2 == 0 else "wB"

                    @blk.sync
                    def _(e, _dc=dc, _wb=wb, _ws=wsem, _wn=wnm):
                        e.wait_ge(cW, cnt["cW"] - 1)
                        dma(e, _ws, _wn, _wb[:].rearrange(
                            "p (kc o) -> p kc o", kc=32, o=128),
                            w3v[:, :, _dc * 128:(_dc + 1) * 128])
                    wt = cnt[wnm]

                    pbk = psE if dc % 2 == 0 else psF

                    @blk.tensor
                    def _(e, _dc=dc, _wb=wb, _wt=wt, _pb=pbk, _ws=wsem):
                        e.wait_ge(_ws, _wt)
                        if _dc >= 2:
                            e.wait_ge(pS, cnt["pS"] - 1)
                        for kc in range(32):
                            mm = e.matmul(_pb[:, 0:TGT], _wb[:, kc * 128:(kc + 1) * 128],
                                     HID[:, kc * TGT:(kc + 1) * TGT],
                                     start=(kc == 0), stop=(kc == 31))
                        tinc(mm, pP, "pP")

                    @blk.scalar
                    def _(e, _dc=dc, _pb=pbk):
                        e.wait_ge(pP, cnt["pP"])
                        inc(e, cW, "cW")
                        e.activation(F3[:, _dc * TGT:(_dc + 1) * TGT], _pb[:, 0:TGT],
                                     AF.Identity, scale=s3sb[:, _dc:_dc + 1])
                        inc(e, pS, "pS")
                bar(["scalar"])

                if PHASES == "C6c" and g == 0:
                    finish_dump(F3, 4096)
                    return nc
                for tt in range(NT_TG):
                    for dc in range(8):
                        bk = banks[dc % 2]

                        @blk.tensor
                        def _(e, _tt=tt, _dc=dc, _bk=bk):
                            if _tt + _dc > 0 or g > 0:
                                e.wait_ge(pS, cnt["pS"] - 1)
                            mm = e.matmul(_bk[:, 0:128],
                                     F3[:, _dc * TGT + _tt * 128:
                                        _dc * TGT + (_tt + 1) * 128],
                                     ident[:], start=True, stop=True)
                            tinc(mm, pP, "pP")

                        @blk.scalar
                        def _(e, _tt=tt, _dc=dc, _bk=bk):
                            e.wait_ge(pP, cnt["pP"])
                            e.copy(f3m[:, _tt * 1024 + _dc * 128:
                                       _tt * 1024 + (_dc + 1) * 128], _bk[:, 0:128])
                            inc(e, pS, "pS")
                bar(["scalar"])
                for tt in range(NT_TG):
                    tia = g * NT_TG + tt
                    src = f3m[:, tt * 1024:(tt + 1) * 1024]
                    prev_out = out_t

                    def dst_fn(e, m_ap, r_ap, _src=src, _tia=tia, _po=prev_out):
                        if _po > 0:
                            e.wait_ge(qS, _po)
                        e.tensor_scalar(ftt[:], _src, m_ap, r_ap,
                                        ALU.subtract, ALU.mult)
                        e.drain()
                        e.tensor_tensor(outb[:], ftt[:],
                                        TB1[:, _tia * 1024:(_tia + 1) * 1024], ALU.add)
                    ln_pass(src, stat2, dst_fn)
                    gvv = cnt["pV"]

                    @blk.sync
                    def _(e, _tia=tia, _gv=gvv):
                        e.wait_ge(pV, _gv)
                        dma(e, qS, "qS", y_out.rearrange(
                            "b (t2 t1) h -> (b t2) t1 h", t1=128)[_tia], outb[:])
                    out_t = cnt["qS"]
            @blk.sync
            def _(e):
                e.wait_ge(qS, cnt["qS"])
            bar(["sync"])
    return nc


_tables = _host_tables()
_nc_cache = {}


def kernel(L0_attn_x, L0_attn_log_dt, L0_attn_A_real_log, L0_attn_A_imag,
           L0_attn_C_real, L0_attn_C_imag, L0_attn_D, L0_attn_W_out,
           L0_attn_b_out, L0_ffn_w1_w, L0_ffn_w1_s, L0_ffn_w2_w, L0_ffn_w2_s,
           L0_ffn_h1_w, L0_ffn_h1_s, L0_ffn_h2_w, L0_ffn_h2_s, L0_ffn_w3_w,
           L0_ffn_w3_s):
    f32 = np.float32
    c = np.ascontiguousarray
    common = dict(
        log_dt=c(np.asarray(L0_attn_log_dt, f32).reshape(1, H)),
        arlT=c(np.asarray(L0_attn_A_real_log, f32).T),
        aimT=c(np.asarray(L0_attn_A_imag, f32).T),
        creT=c(np.asarray(L0_attn_C_real, f32).T),
        cimT=c(np.asarray(L0_attn_C_imag, f32).T),
        d_vec=c(np.asarray(L0_attn_D, f32).reshape(1, H)),
        wout=c(np.asarray(L0_attn_W_out, f32)),
        bout=c(np.asarray(L0_attn_b_out, f32).reshape(1, 2 * H)),
        w1T=c(np.asarray(L0_ffn_w1_w, f32).T),
        w2T=c(np.asarray(L0_ffn_w2_w, f32).T),
        h1T=c(np.asarray(L0_ffn_h1_w, f32).T),
        h2T=c(np.asarray(L0_ffn_h2_w, f32).T),
        w3T=c(np.asarray(L0_ffn_w3_w, f32).T),
        s1=c(np.asarray(L0_ffn_w1_s, f32).reshape(1, DH)),
        s2=c(np.asarray(L0_ffn_w2_s, f32).reshape(1, DH)),
        sh1=c(np.asarray(L0_ffn_h1_s, f32).reshape(1, DH)),
        sh2=c(np.asarray(L0_ffn_h2_s, f32).reshape(1, DH)),
        s3=c(np.asarray(L0_ffn_w3_s, f32).reshape(1, H)),
        **_tables,
    )
    x = np.asarray(L0_attn_x, f32)
    in_maps = []
    for core in range(8):
        m = dict(common)
        m["x"] = c(x[2 * core:2 * core + 2])
        in_maps.append(m)
    if "nc" not in _nc_cache:
        _nc_cache["nc"] = build_nc()
    res = run_bass_kernel_spmd(_nc_cache["nc"], in_maps, core_ids=list(range(8)))
    if DEBUG:
        _dbg.clear()
        _dbg.update({k: v for k, v in res.results[0].items() if k.startswith("dbg")})
    out = np.concatenate([res.results[i]["y"] for i in range(8)], axis=0)
    return out


# version 05 (final)
# speedup vs baseline: 1.5167x; 1.2806x over previous
import contextlib
import numpy as np
import concourse.bass as bass
import concourse.mybir as mybir
from concourse.bass_utils import run_bass_kernel_spmd

F32 = mybir.dt.float32
F16 = mybir.dt.float16
AF = mybir.ActivationFunctionType
ALU = mybir.AluOpType

L = 2048
H = 1024
NST = 64
NF = 4096
T2 = 16
M2 = 32
DH = 4096

CH_SL = 64
N_SL = H // CH_SL
KSUB = 16
TGT = 512
N_TG = 2 * L // TGT

MAGIC = float(1.5 * 2 ** 23)
INV2PI = float(1.0 / (2 * np.pi))
CW1 = 6.28125
CW2 = 0.0019353628158569336
CW3 = float(2 * np.pi - CW1 - CW2)
HALFPI = float(np.pi / 2)

DEBUG = False
PHASES = "ALL"
_dbg = {}


def _host_tables():
    f16 = np.float16
    W32 = np.exp(-2j * np.pi / 32)
    B1 = np.zeros((128, 256), np.float32)
    for s in range(4):
        for t2 in range(T2):
            for m2 in range(M2):
                w = W32 ** (t2 * m2)
                B1[s * 32 + t2 * 2 + 0, s * 64 + m2 * 2 + 0] = w.real
                B1[s * 32 + t2 * 2 + 0, s * 64 + m2 * 2 + 1] = w.imag
                B1[s * 32 + t2 * 2 + 1, s * 64 + m2 * 2 + 0] = -w.imag
                B1[s * 32 + t2 * 2 + 1, s * 64 + m2 * 2 + 1] = w.real
    B1R = np.zeros((128, 512), np.float32)
    for s in range(8):
        for t2 in range(T2):
            for m2 in range(M2):
                w = W32 ** (t2 * m2)
                B1R[s * 16 + t2, s * 64 + m2 * 2 + 0] = w.real
                B1R[s * 16 + t2, s * 64 + m2 * 2 + 1] = w.imag
    Wn = np.exp(-2j * np.pi / NF)
    W128 = np.exp(-2j * np.pi / 128)
    t1 = np.arange(128)[:, None]
    m1 = np.arange(128)[None, :]
    Om = np.zeros((M2, 128, 128), complex)
    for m2 in range(M2):
        Om[m2] = (Wn ** (t1 * m2)) * (W128 ** (t1 * m1))
    V = np.conj(Wn)
    V128 = np.exp(+2j * np.pi / 128)
    Ph = np.zeros((M2, 128, 128), complex)
    for m2 in range(M2):
        Ph[m2] = (V128 ** (m1.T * t1.T)) * (V ** (t1.T * m2)) / NF
    V32 = np.exp(+2j * np.pi / 32)
    Th = np.zeros((2, 128, 128), np.float32)
    for hssub in range(4):
        for m2 in range(M2):
            k = hssub * 32 + m2
            for t2 in range(T2):
                v = V32 ** (m2 * t2)
                Th[0, k, hssub * 32 + t2 * 2 + 0] = v.real
                Th[0, k, hssub * 32 + t2 * 2 + 1] = v.imag
                Th[1, k, hssub * 32 + t2 * 2 + 0] = -v.imag
                Th[1, k, hssub * 32 + t2 * 2 + 1] = v.real
    stk = lambda a: np.ascontiguousarray(a.reshape(-1, a.shape[-1]).astype(f16))
    return dict(
        cB1=B1.astype(f16), cB1R=B1R.astype(f16),
        cOmRe=stk(Om.real), cOmIm=stk(Om.imag), cOmImN=stk(-Om.imag),
        cPhRe=stk(Ph.real), cPhIm=stk(Ph.imag), cPhImN=stk(-Ph.imag),
        cTh0=Th[0].astype(f16), cTh1=Th[1].astype(f16),
        cIdent=np.eye(128, dtype=f16),
    )


def build_nc():
    nc = bass.Bass("TRN2", target_bir_lowering=False)
    dram = nc.dram_tensor

    x_in = dram("x", [2, L, H], F32, kind="ExternalInput").ap()
    log_dt = dram("log_dt", [1, H], F32, kind="ExternalInput").ap()
    arlT = dram("arlT", [NST, H], F32, kind="ExternalInput").ap()
    aimT = dram("aimT", [NST, H], F32, kind="ExternalInput").ap()
    creT = dram("creT", [NST, H], F32, kind="ExternalInput").ap()
    cimT = dram("cimT", [NST, H], F32, kind="ExternalInput").ap()
    d_in = dram("d_vec", [1, H], F32, kind="ExternalInput").ap()
    wout = dram("wout", [H, 2 * H], F32, kind="ExternalInput").ap()
    bout = dram("bout", [1, 2 * H], F32, kind="ExternalInput").ap()
    w1T = dram("w1T", [H, DH], F32, kind="ExternalInput").ap()
    w2T = dram("w2T", [H, DH], F32, kind="ExternalInput").ap()
    h1T = dram("h1T", [DH, DH], F32, kind="ExternalInput").ap()
    h2T = dram("h2T", [DH, DH], F32, kind="ExternalInput").ap()
    w3T = dram("w3T", [DH, H], F32, kind="ExternalInput").ap()
    s1_in = dram("s1", [1, DH], F32, kind="ExternalInput").ap()
    s2_in = dram("s2", [1, DH], F32, kind="ExternalInput").ap()
    sh1_in = dram("sh1", [1, DH], F32, kind="ExternalInput").ap()
    sh2_in = dram("sh2", [1, DH], F32, kind="ExternalInput").ap()
    s3_in = dram("s3", [1, H], F32, kind="ExternalInput").ap()
    consts = {}
    for nm, shp in (("cB1", [128, 256]), ("cB1R", [128, 512]),
                    ("cOmRe", [M2 * 128, 128]), ("cOmIm", [M2 * 128, 128]),
                    ("cOmImN", [M2 * 128, 128]), ("cPhRe", [M2 * 128, 128]),
                    ("cPhIm", [M2 * 128, 128]), ("cPhImN", [M2 * 128, 128]),
                    ("cTh0", [128, 128]), ("cTh1", [128, 128]),
                    ("cIdent", [128, 128])):
        consts[nm] = dram(nm, shp, F16, kind="ExternalInput").ap()

    y_out = dram("y", [2, L, H], F32, kind="ExternalOutput").ap()

    xf16 = dram("xf16", [2 * L, H], F16, kind="Internal").ap()
    s_xnh = dram("s_xnh", [CH_SL, NF], F16, kind="Internal").ap()
    s_kern = dram("s_kern", [CH_SL, L], F16, kind="Internal").ap()
    wout16 = dram("wout16", [H, 2 * H], F16, kind="Internal").ap()
    w116 = dram("w116", [H, DH], F16, kind="Internal").ap()
    w216 = dram("w216", [H, DH], F16, kind="Internal").ap()
    h116 = dram("h116", [DH, DH], F16, kind="Internal").ap()
    h216 = dram("h216", [DH, DH], F16, kind="Internal").ap()
    w316 = dram("w316", [DH, H], F16, kind="Internal").ap()
    if DEBUG:
        dbg_xn = dram("dbg_xn", [128, 32 * 1024], F16, kind="ExternalOutput").ap()
        dbg_y2 = dram("dbg_y2", [128, 32 * 1024], F16, kind="ExternalOutput").ap()
        dbg_x1 = dram("dbg_x1", [128, 32 * 1024], F16, kind="ExternalOutput").ap()

    cnt = {}

    def tinc(instr, sem, name, n=1):
        instr.then_inc(sem, n)
        cnt[name] = cnt.get(name, 0) + n

    def inc(e, sem, name, n=1):
        last = getattr(e, "last", None)
        if last is not None:
            last.then_inc(sem, n)
            e.last = None
        else:
            try:
                en = str(e.engine)
            except Exception:
                en = ""
            if ("DVE" in en) or ("Activation" in en) or ("Pool" in en):
                e.drain()
            e.sem_inc(sem, n)
        cnt[name] = cnt.get(name, 0) + n

    st_main = contextlib.ExitStack()
    with st_main:
        TB1 = st_main.enter_context(nc.sbuf_tensor("TB1", [128, 32 * 1024], F16))
        ident = st_main.enter_context(nc.sbuf_tensor("ident", [128, 128], F16))
        b1sb = st_main.enter_context(nc.sbuf_tensor("b1sb", [128, 256], F16))
        b1rsb = st_main.enter_context(nc.sbuf_tensor("b1rsb", [128, 512], F16))
        th0sb = st_main.enter_context(nc.sbuf_tensor("th0sb", [128, 128], F16))
        th1sb = st_main.enter_context(nc.sbuf_tensor("th1sb", [128, 128], F16))
        dbc = st_main.enter_context(nc.sbuf_tensor("dbc", [128, H], F32))
        dtbc = st_main.enter_context(nc.sbuf_tensor("dtbc", [128, H], F32))
        prmA = st_main.enter_context(nc.sbuf_tensor("prmA", [128, H], F32))
        prmI = st_main.enter_context(nc.sbuf_tensor("prmI", [128, H], F32))
        prmCr = st_main.enter_context(nc.sbuf_tensor("prmCr", [128, H], F16))
        prmCi = st_main.enter_context(nc.sbuf_tensor("prmCi", [128, H], F16))
        bosb = st_main.enter_context(nc.sbuf_tensor("bosb", [128, 16], F32))
        s1sb = st_main.enter_context(nc.sbuf_tensor("s1sb", [128, 32], F32))
        s2sb = st_main.enter_context(nc.sbuf_tensor("s2sb", [128, 32], F32))
        sh1sb = st_main.enter_context(nc.sbuf_tensor("sh1sb", [128, 32], F32))
        sh2sb = st_main.enter_context(nc.sbuf_tensor("sh2sb", [128, 32], F32))
        s3sb = st_main.enter_context(nc.sbuf_tensor("s3sb", [128, 8], F32))
        row1 = st_main.enter_context(nc.sbuf_tensor("row1", [1, 2 * H], F32))
        epsc = st_main.enter_context(nc.sbuf_tensor("epsc", [128, 1], F32))
        ones1 = st_main.enter_context(nc.sbuf_tensor("ones1", [1, 128], F32))
        negone = st_main.enter_context(nc.sbuf_tensor("negone", [128, 64], F32))
        nhalf = st_main.enter_context(nc.sbuf_tensor("nhalf", [128, 1], F32))
        psA = st_main.enter_context(nc.psum_tensor("psA", [128, 512], F32))
        psB = st_main.enter_context(nc.psum_tensor("psB", [128, 512], F32))
        psC = st_main.enter_context(nc.psum_tensor("psC", [128, 512], F32))
        psD = st_main.enter_context(nc.psum_tensor("psD", [128, 512], F32))
        psE = st_main.enter_context(nc.psum_tensor("psE", [128, 512], F32))
        psF = st_main.enter_context(nc.psum_tensor("psF", [128, 512], F32))
        psG = st_main.enter_context(nc.psum_tensor("psG", [128, 512], F32))
        psH = st_main.enter_context(nc.psum_tensor("psH", [128, 512], F32))
        qS = st_main.enter_context(nc.semaphore("qS"))
        qG = st_main.enter_context(nc.semaphore("qG"))
        G = st_main.enter_context(nc.semaphore("G"))
        pP = st_main.enter_context(nc.semaphore("pP"))
        pV = st_main.enter_context(nc.semaphore("pV"))
        pS = st_main.enter_context(nc.semaphore("pS"))
        pG2 = st_main.enter_context(nc.semaphore("pG2"))
        wA = st_main.enter_context(nc.semaphore("wA"))
        wB = st_main.enter_context(nc.semaphore("wB"))
        cW = st_main.enter_context(nc.semaphore("cW"))
        xA = st_main.enter_context(nc.semaphore("xA"))
        xB = st_main.enter_context(nc.semaphore("xB"))
        blk = st_main.enter_context(nc.Block())
        class _Rec:
            def __init__(self, e):
                self._e = e
                self.last = None
            def __getattr__(self, name):
                f = getattr(self._e, name)
                if name in ("wait_ge", "sem_inc", "drain", "engine", "dma_start"):
                    return f
                if not callable(f):
                    return f
                def g(*a, **k):
                    r = f(*a, **k)
                    self.last = r
                    return r
                return g

        class _BlkWrap:
            def __init__(self, b):
                self._b = b
            def __getattr__(self, engname):
                dec = getattr(self._b, engname)
                if engname.startswith("_"):
                    return dec
                def mydec(fn):
                    def fn2(e):
                        if engname in ("vector", "scalar", "gpsimd"):
                            return fn(_Rec(e))
                        return fn(e)
                    return dec(fn2)
                return mydec

        blk = _BlkWrap(blk)

        banks = [psA, psB, psC, psD, psE, psF, psG, psH]

        def dma(e, sem, name, out, in_, **kw):
            e.dma_start(out=out, in_=in_, **kw).then_inc(sem, 16)
            cnt[name] = cnt.get(name, 0) + 16
            return cnt[name]

        def bar(engines_inc):
            for ename in engines_inc:
                def f(e):
                    inc(e, G, "G")
                getattr(blk, ename)(f)
            tgt = cnt.get("G", 0)
            for ename in ("sync", "gpsimd", "tensor", "vector", "scalar"):
                def f(e, _t=tgt):
                    e.wait_ge(G, _t)
                getattr(blk, ename)(f)

        def cmb(e, dre, dim_, are, aim, bre, bim, tmp):
            e.tensor_tensor(tmp, aim, bim, ALU.mult)
            e.tensor_tensor(dre, are, bre, ALU.mult)
            e.drain()
            e.tensor_tensor(dre, dre, tmp, ALU.subtract)
            e.tensor_tensor(tmp, aim, bre, ALU.mult)
            e.tensor_tensor(dim_, are, bim, ALU.mult)
            e.drain()
            e.tensor_tensor(dim_, dim_, tmp, ALU.add)

        wcast_tgt = {}

        @blk.gpsimd
        def _(e):
            for dst, srcp in ((prmCr, creT), (prmCi, cimT)):
                dma(e, qG, "qG", dst.ap()[0:64], srcp)
                dma(e, qG, "qG", dst.ap()[64:128], srcp)
            e.wait_ge(qG, cnt["qG"])
            inc(e, pG2, "pG2")
            cnt["prm_ready"] = cnt["pG2"]
            dma(e, qG, "qG", xf16, x_in.rearrange("b t h -> (b t) h"), max_dma_last_dim=4096)
            e.wait_ge(qG, cnt["qG"])
            inc(e, pG2, "pG2")
            cnt["xf16_ready"] = cnt["pG2"]
            wlist = (("wout", wout, wout16), ("w1", w1T, w116),
                     ("w2", w2T, w216), ("w3", w3T, w316),
                     ("h1", h1T, h116), ("h2", h2T, h216))
            if PHASES == "Cpre":
                wlist = wlist[:1]
            elif PHASES in ("C6a", "C6b", "C6c", "C45"):
                pass
            elif PHASES != "ALL":
                wlist = ()
            for nm, srcw, dst in wlist:
                dma(e, qG, "qG", dst, srcw, max_dma_last_dim=8192)
                e.wait_ge(qG, cnt["qG"])
                inc(e, pG2, "pG2")
                wcast_tgt[nm] = cnt["pG2"]

        @blk.sync
        def _(e):
            dma(e, qS, "qS", ident[:], consts["cIdent"])
            dma(e, qS, "qS", b1sb[:], consts["cB1"])
            dma(e, qS, "qS", b1rsb[:], consts["cB1R"])
            dma(e, qS, "qS", th0sb[:], consts["cTh0"])
            dma(e, qS, "qS", th1sb[:], consts["cTh1"])
            for dst, src in ((prmA, arlT), (prmI, aimT)):
                dma(e, qS, "qS", dst.ap()[0:64], src)
                dma(e, qS, "qS", dst.ap()[64:128], src)
            e.wait_ge(pG2, cnt["prm_ready"])
            dma(e, qS, "qS", row1[:1, 0:H], log_dt)
            with nc.allow_non_contiguous_dma(reason="tiny one-time scale loads"):
                for dst, src, n in ((s1sb, s1_in, 32), (s2sb, s2_in, 32),
                                    (sh1sb, sh1_in, 32), (sh2sb, sh2_in, 32),
                                    (s3sb, s3_in, 8), (bosb, bout, 16)):
                    dma(e, qS, "qS", dst.ap().rearrange("p (c o) -> p c o", c=n, o=1),
                        src.rearrange("o (c p) -> p c o", c=n, p=128))
            e.wait_ge(qS, cnt["qS"])
            inc(e, G, "G")
        g_a = cnt["G"]

        @blk.vector
        def _(e):
            e.memset(epsc[:], 1e-5)
            e.memset(ones1[:1, :], 1.0)
            e.memset(negone[:], -1.0)
            e.memset(nhalf[:], -0.5)
            e.drain()

        @blk.scalar
        def _(e):
            e.wait_ge(G, g_a)
            e.activation(row1[:1, 0:H], row1[:1, 0:H], AF.Exp)
            e.activation(prmA[:], prmA[:], AF.Exp)
            inc(e, pS, "pS")

        @blk.vector
        def _(e):
            e.wait_ge(pS, cnt["pS"])
            e.tensor_scalar(prmA[:], prmA[:], -1.0, None, ALU.mult, ALU.bypass)
            inc(e, pV, "pV")

        @blk.sync
        def _(e):
            e.wait_ge(pS, cnt["pS"])
            dma(e, qS, "qS", dtbc.ap()[0:1, 0:H], row1[:1, 0:H])
            e.wait_ge(qS, cnt["qS"])
            w = 1
            while w < 128:
                dma(e, qS, "qS", dtbc.ap()[w:2 * w, 0:H], dtbc.ap()[0:w, 0:H])
                e.wait_ge(qS, cnt["qS"])
                w *= 2
            dma(e, qS, "qS", row1[:1, 0:H], d_in)
            e.wait_ge(qS, cnt["qS"])
            dma(e, qS, "qS", dbc.ap()[0:1, 0:H], row1[:1, 0:H])
            e.wait_ge(qS, cnt["qS"])
            w = 1
            while w < 128:
                dma(e, qS, "qS", dbc.ap()[w:2 * w, 0:H], dbc.ap()[0:w, 0:H])
                e.wait_ge(qS, cnt["qS"])
                w *= 2
            inc(e, G, "G")
        bar(["sync"])

        with contextlib.ExitStack() as st_a:
            xa = st_a.enter_context(nc.sbuf_tensor("xa", [128, H], F16))
            xb = st_a.enter_context(nc.sbuf_tensor("xb", [128, H], F16))
            bn6 = st_a.enter_context(nc.sbuf_tensor("bn6", [128, 12], F32))
            bn2 = st_a.enter_context(nc.sbuf_tensor("bn2", [128, 2], F32))
            mrr = st_a.enter_context(nc.sbuf_tensor("mrr", [128, 2], F32))
            xv = x_in.rearrange("b (t2 t1) h -> (b t2) t1 h", t1=128)
            for ti in range(32):
                buf = xa if ti % 2 == 0 else xb

                xsem = xA if ti % 2 == 0 else xB
                xnm = "xA" if ti % 2 == 0 else "xB"

                @blk.gpsimd
                def _(e, _ti=ti, _buf=buf, _xs=xsem, _xn=xnm):
                    if _ti >= 2:
                        e.wait_ge(pV, cnt.get("pV", 0) - 1)
                    dma(e, _xs, _xn, _buf[:], xv[_ti])
                qg_t = cnt[xnm]

                @blk.vector
                def _(e, _buf=buf, _qt=qg_t, _xs=xsem):
                    e.wait_ge(_xs, _qt)
                    e.bn_stats(bn6[:, 0:6], _buf[:, 0:512])
                    e.bn_stats(bn6[:, 6:12], _buf[:, 512:1024])
                    e.drain()
                    e.bn_aggr(bn2[:], bn6[:])
                    inc(e, pV, "pV")

                @blk.gpsimd
                def _(e):
                    e.wait_ge(pV, cnt["pV"])
                    e.tensor_scalar(mrr[:, 1:2], bn2[:, 1:2], 1e-5, None, ALU.add, ALU.bypass)
                    e.drain()
                    e.tensor_tensor(mrr[:, 0:1], mrr[:, 1:2], nhalf[:, 0:1], ALU.pow)
                    inc(e, pG2, "pG2")

                @blk.vector
                def _(e, _ti=ti, _buf=buf):
                    e.wait_ge(pG2, cnt["pG2"])
                    e.tensor_scalar(TB1[:, _ti * 1024:(_ti + 1) * 1024], _buf[:],
                                    bn2[:, 0:1], mrr[:, 0:1], ALU.subtract, ALU.mult)
                    inc(e, pV, "pV")
            bar(["vector"])
        if DEBUG:
            @blk.sync
            def _(e):
                dma(e, qS, "qS", dbg_xn, TB1[:])
                e.wait_ge(qS, cnt["qS"])
            bar(["sync"])

        def finish_dump(extra=None, nc_extra=4096):
            @blk.gpsimd
            def _(e):
                e.wait_ge(G, cnt.get("G", 0))
                e.wait_ge(pV, cnt.get("pV", 0))
                e.wait_ge(pS, cnt.get("pS", 0))
                dma(e, qG, "qG",
                    y_out.rearrange("b (t2 t1) h -> t1 b t2 h", t1=128),
                    TB1.ap().rearrange("p (b t2 h) -> p b t2 h", b=2, t2=16, h=1024),
                    max_dma_last_dim=4096)
                if extra is not None:
                    e.wait_ge(qG, cnt["qG"])
                    yv = y_out.rearrange("b (tq p t1q) h -> b tq p (t1q h)",
                                         p=128, t1q=1)
                    for j in range(nc_extra // 1024):
                        dma(e, qG, "qG", yv[1, j],
                            extra.ap()[:, j * 1024:(j + 1) * 1024])
                e.wait_ge(qG, cnt["qG"])

            @blk.sync
            def _(e):
                e.wait_ge(qS, cnt["qS"])

        if PHASES == "A":
            finish_dump()
            return nc

        NHG = CH_SL // 4
        NHGP = CH_SL // 8
        NHSG = CH_SL // 4
        NPL = M2 * CH_SL

        with contextlib.ExitStack() as st_b:
            constb = st_b.enter_context(nc.sbuf_tensor("constb", [128, M2 * 3 * 128], F16))
            xnh = st_b.enter_context(nc.sbuf_tensor("xnh", [128, NF], F16))
            zst = st_b.enter_context(nc.sbuf_tensor("zst", [128, NHG * 128], F16))
            Ssl = st_b.enter_context(nc.sbuf_tensor("Ssl", [128, 2 * NPL], F16))
            Zsl = st_b.enter_context(nc.sbuf_tensor("Zsl", [128, 2 * NPL], F16))
            KFsl = st_b.enter_context(nc.sbuf_tensor("KFsl", [128, 2 * NPL], F16))
            WmT = st_b.enter_context(nc.sbuf_tensor("WmT", [128, 2 * NPL], F16))
            kstr = st_b.enter_context(nc.sbuf_tensor("kstr", [128, NHGP * 128], F16))
            qre = st_b.enter_context(nc.sbuf_tensor("qre", [128, KSUB * 128], F32))
            qim = st_b.enter_context(nc.sbuf_tensor("qim", [128, KSUB * 128], F32))
            pre = st_b.enter_context(nc.sbuf_tensor("pre", [128, KSUB * 16], F32))
            pim = st_b.enter_context(nc.sbuf_tensor("pim", [128, KSUB * 16], F32))
            pcmb = st_b.enter_context(nc.sbuf_tensor("pcmb", [128, KSUB * 16], F16))
            qcmb = st_b.enter_context(nc.sbuf_tensor("qcmb", [128, KSUB * 128], F16))
            lre = st_b.enter_context(nc.sbuf_tensor("lre", [128, KSUB], F32))
            lim = st_b.enter_context(nc.sbuf_tensor("lim", [128, KSUB], F32))
            pwr = st_b.enter_context(nc.sbuf_tensor("pwr", [128, KSUB], F32))
            pwi = st_b.enter_context(nc.sbuf_tensor("pwi", [128, KSUB], F32))
            t0 = st_b.enter_context(nc.sbuf_tensor("t0", [128, KSUB], F32))
            t1b = st_b.enter_context(nc.sbuf_tensor("t1b", [128, KSUB], F32))
            t2b = st_b.enter_context(nc.sbuf_tensor("t2b", [128, KSUB], F32))
            t3b = st_b.enter_context(nc.sbuf_tensor("t3b", [128, KSUB], F32))
            ctre = st_b.enter_context(nc.sbuf_tensor("ctre", [128, KSUB], F32))
            ctim = st_b.enter_context(nc.sbuf_tensor("ctim", [128, KSUB], F32))
            big32 = st_b.enter_context(nc.sbuf_tensor("big32", [128, 1024], F32))
            ust0 = st_b.enter_context(nc.sbuf_tensor("ust0", [128, 128], F16))
            ust1 = st_b.enter_context(nc.sbuf_tensor("ust1", [128, 128], F16))
            gtmp = st_b.enter_context(nc.sbuf_tensor("gtmp", [128, 512], F32))
            xnds = st_b.enter_context(nc.sbuf_tensor("xnds", [128, 32 * CH_SL], F16))
            def om(pl, m2):
                return constb[:, (pl * M2 + m2) * 128:(pl * M2 + m2) * 128 + 128]

            def load_tables(pre_fix):
                @blk.sync
                def _(e):
                    for i, nm in enumerate(("Re", "Im", "ImN")):
                        dma(e, qS, "qS",
                            constb[:, i * M2 * 128:(i + 1) * M2 * 128].rearrange(
                                "p (m k) -> p m k", m=M2, k=128),
                            consts["c" + pre_fix + nm].rearrange(
                                "(m p) k -> p m k", m=M2))
                    e.wait_ge(qS, cnt["qS"])
                    inc(e, G, "G")
                return cnt["G"]

            def stage2(src, dstbuf, tgt_qs):
                @blk.tensor
                def _(e, _t=tgt_qs):
                    e.wait_ge(G, _t)
                for m2 in range(M2):
                    bkr = banks[4 + (m2 % 2) * 2]
                    bki = banks[5 + (m2 % 2) * 2]

                    @blk.tensor
                    def _(e, _m2=m2, _r=bkr, _i=bki):
                        if _m2 >= 2:
                            e.wait_ge(pS, cnt["pS"] - 1)
                        sre = src[:, _m2 * CH_SL:(_m2 + 1) * CH_SL]
                        sim = src[:, NPL + _m2 * CH_SL:NPL + (_m2 + 1) * CH_SL]
                        e.matmul(_r[:, 0:CH_SL], om(0, _m2), sre, start=True, stop=False)
                        mm1 = e.matmul(_r[:, 0:CH_SL], om(2, _m2), sim, start=False, stop=True)
                        tinc(mm1, pP, "pP")
                        e.matmul(_i[:, 0:CH_SL], om(1, _m2), sre, start=True, stop=False)
                        mm2 = e.matmul(_i[:, 0:CH_SL], om(0, _m2), sim, start=False, stop=True)
                        tinc(mm2, pP, "pP")

                    @blk.vector
                    def _(e, _m2=m2, _r=bkr, _i=bki):
                        e.wait_ge(pP, cnt["pP"])
                        e.tensor_copy(dstbuf[:, _m2 * CH_SL:(_m2 + 1) * CH_SL], _r[:, 0:CH_SL])
                        e.tensor_copy(dstbuf[:, NPL + _m2 * CH_SL:NPL + (_m2 + 1) * CH_SL],
                               _i[:, 0:CH_SL])
                        inc(e, pS, "pS")
                bar(["vector"])

            for sl in range(1 if PHASES == "B1" else N_SL):
                h0 = sl * CH_SL
                om_ld = load_tables("Om")

                for ti in range(32):
                    bk = banks[ti % 2]

                    @blk.tensor
                    def _(e, _ti=ti, _bk=bk):
                        if _ti >= 2 or sl > 0:
                            e.wait_ge(pS, cnt["pS"] - 1)
                        mm = e.matmul(_bk[0:CH_SL, 0:128],
                                 TB1[:, _ti * 1024 + h0:_ti * 1024 + h0 + CH_SL],
                                 ident[:], start=True, stop=True)
                        tinc(mm, pP, "pP")

                    @blk.vector
                    def _(e, _ti=ti, _bk=bk):
                        e.wait_ge(pP, cnt["pP"])
                        e.tensor_copy(xnh[0:CH_SL, _ti * 128:(_ti + 1) * 128],
                               _bk[0:CH_SL, 0:128])
                        inc(e, pS, "pS")
                bar(["vector"])

                if PHASES == "B0":
                    finish_dump(xnh)
                    return nc
                @blk.sync
                def _(e):
                    dma(e, qS, "qS", s_xnh, xnh[0:CH_SL, :])
                    e.wait_ge(qS, cnt["qS"])
                    srcv = s_xnh.rearrange("(hg s) (c t2 t1) -> s c t2 hg t1",
                                           hg=NHG, s=4, c=2, t2=16, t1=128)
                    for s in range(4):
                        for c in range(2):
                            dstp = zst.ap()[s * 32 + c:s * 32 + c + 31:2]
                            dma(e, qS, "qS",
                                dstp.rearrange("t2 (hg t1) -> t2 hg t1",
                                               hg=NHG, t1=128),
                                srcv[s, c])
                    e.wait_ge(qS, cnt["qS"])
                    inc(e, G, "G")
                g0 = cnt["G"]

                if PHASES == "B1s":
                    finish_dump(zst, 2048)
                    return nc
                for hg in range(NHG):
                    bk = banks[2 + hg % 2]

                    @blk.tensor
                    def _(e, _hg=hg, _bk=bk, _g0=g0):
                        if _hg == 0:
                            e.wait_ge(G, _g0)
                        if _hg >= 2:
                            e.wait_ge(pV, cnt["pV"] - 1)
                        mm = e.matmul(_bk[:, 0:256], zst[:, _hg * 128:(_hg + 1) * 128],
                                 b1sb[:], start=True, stop=True)
                        tinc(mm, pP, "pP")

                    @blk.vector
                    def _(e, _hg=hg, _bk=bk):
                        e.wait_ge(pP, cnt["pP"])
                        dstv = Ssl.ap().rearrange(
                            "p (cp m2 hg s) -> p hg s m2 cp",
                            cp=2, m2=M2, hg=NHG, s=4)[:, _hg]
                        e.tensor_copy(dstv, _bk[:, 0:256].rearrange(
                            "p (s m2 cp) -> p s m2 cp", s=4, m2=M2, cp=2))
                        inc(e, pV, "pV")
                bar(["vector"])

                if PHASES == "B2":
                    finish_dump(Ssl)
                    return nc
                stage2(Ssl, Zsl, om_ld)

                if PHASES == "B3":
                    finish_dump(Zsl)
                    return nc
                for sub in range(CH_SL // KSUB):
                    cs = slice(h0 + sub * KSUB, h0 + sub * KSUB + KSUB)

                    @blk.vector
                    def _(e):
                        e.tensor_copy(t0[:], prmA[:, cs])
                        e.drain()
                        e.tensor_tensor(t1b[:], t0[:], dtbc[:, cs], ALU.mult)
                        e.tensor_tensor(t2b[:], prmI[:, cs], dtbc[:, cs], ALU.mult)
                        e.drain()
                        e.tensor_scalar(t3b[:], t2b[:], INV2PI, None, ALU.mult, ALU.bypass)
                        e.drain()
                        e.tensor_scalar(t3b[:], t3b[:], MAGIC, MAGIC, ALU.add, ALU.subtract)
                        e.drain()
                        e.scalar_tensor_tensor(pwr[:], t3b[:], -CW1, t2b[:], ALU.mult, ALU.add)
                        e.drain()
                        e.scalar_tensor_tensor(pwr[:], t3b[:], -CW2, pwr[:], ALU.mult, ALU.add)
                        e.drain()
                        e.scalar_tensor_tensor(pwr[:], t3b[:], -CW3, pwr[:], ALU.mult, ALU.add)
                        e.tensor_scalar(t2b[:], t2b[:], HALFPI, None, ALU.add, ALU.bypass)
                        e.drain()
                        e.tensor_scalar(t3b[:], t2b[:], INV2PI, None, ALU.mult, ALU.bypass)
                        e.drain()
                        e.tensor_scalar(t3b[:], t3b[:], MAGIC, MAGIC, ALU.add, ALU.subtract)
                        e.drain()
                        e.scalar_tensor_tensor(pwi[:], t3b[:], -CW1, t2b[:], ALU.mult, ALU.add)
                        e.drain()
                        e.scalar_tensor_tensor(pwi[:], t3b[:], -CW2, pwi[:], ALU.mult, ALU.add)
                        e.drain()
                        e.scalar_tensor_tensor(pwi[:], t3b[:], -CW3, pwi[:], ALU.mult, ALU.add)
                        inc(e, pV, "pV")

                    @blk.scalar
                    def _(e):
                        e.wait_ge(pV, cnt["pV"])
                        e.activation(t1b[:], t1b[:], AF.Exp)
                        e.activation(pwr[:], pwr[:], AF.Sin)
                        e.activation(pwi[:], pwi[:], AF.Sin)
                        inc(e, pS, "pS")

                    @blk.vector
                    def _(e):
                        e.wait_ge(pS, cnt["pS"])
                        e.tensor_tensor(lre[:], t1b[:], pwi[:], ALU.mult)
                        e.tensor_tensor(lim[:], t1b[:], pwr[:], ALU.mult)
                        inc(e, pV, "pV")
                    if PHASES == "B4l" and sub == 0:
                        @blk.vector
                        def _(e):
                            kd = xnh
                            e.tensor_copy(kd[:, 0:16], lre[:])
                            e.tensor_copy(kd[:, 16:32], lim[:])
                            e.tensor_copy(kd[:, 32:48], t1b[:])
                            e.tensor_copy(kd[:, 48:64], pwr[:])
                            e.tensor_copy(kd[:, 64:80], pwi[:])
                            e.tensor_copy(kd[:, 80:96], dtbc[:, cs])
                            e.tensor_copy(kd[:, 96:112], prmI[:, cs])
                            e.tensor_copy(kd[:, 112:128], prmA[:, cs])
                            inc(e, pV, "pV")
                        bar(["vector"])
                        finish_dump(xnh, 1024)
                        return nc

                    @blk.scalar
                    def _(e):
                        e.wait_ge(pV, cnt["pV"])
                        e.activation(t1b[:], t0[:], AF.Square)
                        e.activation(t2b[:], prmI[:, cs], AF.Square)
                        inc(e, pS, "pS")

                    @blk.vector
                    def _(e):
                        e.wait_ge(pS, cnt["pS"])
                        e.tensor_tensor(t1b[:], t1b[:], t2b[:], ALU.add)
                        inc(e, pV, "pV")

                    @blk.gpsimd
                    def _(e):
                        e.wait_ge(pV, cnt["pV"])
                        e.tensor_tensor(t1b[:], t1b[:], negone[:, 0:KSUB], ALU.pow)
                        inc(e, pG2, "pG2")

                    @blk.vector
                    def _(e):
                        e.wait_ge(pG2, cnt["pG2"])
                        e.tensor_tensor(t2b[:], t0[:], t1b[:], ALU.mult)
                        e.scalar_tensor_tensor(t3b[:], prmI[:, cs], -1.0, t1b[:],
                                               ALU.mult, ALU.mult)
                        e.drain()
                        e.tensor_scalar(t0[:], lre[:], -1.0, None, ALU.add, ALU.bypass)
                        e.drain()
                        cmb(e, ctre[:], ctim[:], t0[:], lim[:], t2b[:], t3b[:], t1b[:])
                        e.drain()
                        cmb(e, t2b[:], t3b[:], ctre[:], ctim[:],
                            prmCr[:, cs], prmCi[:, cs], t1b[:])
                        e.drain()
                        e.tensor_copy(ctre[:], t2b[:])
                        e.tensor_copy(ctim[:], t3b[:])
                        e.tensor_copy(pwr[:], lre[:])
                        e.tensor_copy(pwi[:], lim[:])
                        e.drain()
                        for _ in range(7):
                            e.tensor_tensor(t0[:], pwr[:], pwr[:], ALU.mult)
                            e.tensor_tensor(t1b[:], pwi[:], pwi[:], ALU.mult)
                            e.tensor_tensor(t2b[:], pwr[:], pwi[:], ALU.mult)
                            e.drain()
                            e.tensor_tensor(pwr[:], t0[:], t1b[:], ALU.subtract)
                            e.tensor_scalar(pwi[:], t2b[:], 2.0, None, ALU.mult, ALU.bypass)
                            e.drain()
                        pr = pre.ap().rearrange("p (ch l) -> p l ch", ch=KSUB, l=16)
                        pi = pim.ap().rearrange("p (ch l) -> p l ch", ch=KSUB, l=16)
                        e.tensor_copy(pr[:, 0], ctre[:])
                        e.tensor_copy(pi[:, 0], ctim[:])
                        inc(e, pV, "pV")

                    for kstep in range(4):
                        w = 2 ** kstep

                        @blk.vector
                        def _(e, _w=w):
                            pr = pre.ap().rearrange("p (ch l) -> p l ch", ch=KSUB, l=16)
                            pi = pim.ap().rearrange("p (ch l) -> p l ch", ch=KSUB, l=16)
                            are, aim = pr[:, 0:_w], pi[:, 0:_w]
                            dre, dim_ = pr[:, _w:2 * _w], pi[:, _w:2 * _w]
                            bre = pwr.ap().rearrange("p (u ch) -> p u ch", u=1, ch=KSUB)
                            bim = pwi.ap().rearrange("p (u ch) -> p u ch", u=1, ch=KSUB)
                            bre2, are2 = bass.broadcast_tensor_aps(bre, are)
                            bim2, aim2 = bass.broadcast_tensor_aps(bim, aim)
                            tmp = big32[:, 0:_w * KSUB].rearrange(
                                "p (l c) -> p l c", l=_w, c=KSUB)
                            cmb(e, dre, dim_, are2, aim2, bre2, bim2, tmp)
                            if _w < 8:
                                e.drain()
                                e.tensor_tensor(t0[:], pwr[:], pwr[:], ALU.mult)
                                e.tensor_tensor(t1b[:], pwi[:], pwi[:], ALU.mult)
                                e.tensor_tensor(t2b[:], pwr[:], pwi[:], ALU.mult)
                                e.drain()
                                e.tensor_tensor(pwr[:], t0[:], t1b[:], ALU.subtract)
                                e.tensor_scalar(pwi[:], t2b[:], 2.0, None,
                                                ALU.mult, ALU.bypass)
                            inc(e, pV, "pV")

                    @blk.vector
                    def _(e):
                        qr = qre.ap().rearrange("p (ch l) -> p l ch", ch=KSUB, l=128)
                        qi = qim.ap().rearrange("p (ch l) -> p l ch", ch=KSUB, l=128)
                        e.memset(qr[:, 0], 1.0)
                        e.memset(qi[:, 0], 0.0)
                        e.tensor_copy(pwr[:], lre[:])
                        e.tensor_copy(pwi[:], lim[:])
                        e.drain()
                        inc(e, pV, "pV")
                    for kstep in range(7):
                        w = 2 ** kstep

                        @blk.vector
                        def _(e, _w=w):
                            qr = qre.ap().rearrange("p (ch l) -> p l ch", ch=KSUB, l=128)
                            qi = qim.ap().rearrange("p (ch l) -> p l ch", ch=KSUB, l=128)
                            are, aim = qr[:, 0:_w], qi[:, 0:_w]
                            dre, dim_ = qr[:, _w:2 * _w], qi[:, _w:2 * _w]
                            bre = pwr.ap().rearrange("p (u ch) -> p u ch", u=1, ch=KSUB)
                            bim = pwi.ap().rearrange("p (u ch) -> p u ch", u=1, ch=KSUB)
                            bre2, are2 = bass.broadcast_tensor_aps(bre, are)
                            bim2, aim2 = bass.broadcast_tensor_aps(bim, aim)
                            tmp = big32[:, 0:_w * KSUB].rearrange(
                                "p (l c) -> p l c", l=_w, c=KSUB)
                            cmb(e, dre, dim_, are2, aim2, bre2, bim2, tmp)
                            if _w < 64:
                                e.drain()
                                e.tensor_tensor(t0[:], pwr[:], pwr[:], ALU.mult)
                                e.tensor_tensor(t1b[:], pwi[:], pwi[:], ALU.mult)
                                e.tensor_tensor(t2b[:], pwr[:], pwi[:], ALU.mult)
                                e.drain()
                                e.tensor_tensor(pwr[:], t0[:], t1b[:], ALU.subtract)
                                e.tensor_scalar(pwi[:], t2b[:], 2.0, None,
                                                ALU.mult, ALU.bypass)
                            inc(e, pV, "pV")

                    @blk.vector
                    def _(e):
                        e.drain()
                        e.tensor_scalar(pcmb.ap()[0:64], pre.ap()[0:64], 2.0, None,
                                        ALU.mult, ALU.bypass)
                        e.tensor_scalar(pcmb.ap()[64:128], pim.ap()[64:128], -2.0, None,
                                        ALU.mult, ALU.bypass)
                        e.tensor_copy(qcmb.ap()[0:64], qre.ap()[0:64])
                        e.tensor_copy(qcmb.ap()[64:128], qim.ap()[64:128])
                        inc(e, pV, "pV")
                    gv = cnt["pV"]
                    if PHASES == "B4q" and sub == 0:
                        bar(["vector"])
                        finish_dump(qcmb, 2048)
                        return nc
                    if PHASES == "B4p" and sub == 0:
                        bar(["vector"])
                        finish_dump(pcmb, 256)
                        return nc

                    kstg = xnh
                    for ch in range(KSUB):
                        bk = banks[6 + (ch // 4) % 2]
                        fo = (ch % 4) * 128

                        @blk.tensor
                        def _(e, _ch=ch, _bk=bk, _fo=fo, _gv=gv):
                            if _ch == 0:
                                e.wait_ge(pV, _gv)
                            if _ch % 4 == 0 and _ch >= 8:
                                e.wait_ge(pS, cnt["pS"] - 1)
                            mm = e.matmul(_bk[0:16, _fo:_fo + 128],
                                     pcmb[:, _ch * 16:(_ch + 1) * 16],
                                     qcmb[:, _ch * 128:(_ch + 1) * 128],
                                     start=True, stop=True)
                            if _ch % 4 == 3:
                                tinc(mm, pP, "pP")

                        if ch % 4 == 3:
                            @blk.vector
                            def _(e, _ch=ch, _bk=bk):
                                e.wait_ge(pP, cnt["pP"])
                                base = (_ch - 3) * 128
                                e.tensor_copy(kstg[0:16, base:base + 512], _bk[0:16, 0:512])
                                inc(e, pS, "pS")
                    bar(["vector"])

                    @blk.sync
                    def _(e, _sub=sub):
                        src = kstg[0:16, 0:KSUB * 128].rearrange("t2 (ch t1) -> t2 ch t1",
                                                      ch=KSUB, t1=128)
                        dstv = s_kern.rearrange("(sb ch) (t2 t1) -> sb t2 ch t1",
                                                sb=CH_SL // KSUB, ch=KSUB,
                                                t2=16, t1=128)[_sub]
                        dma(e, qS, "qS", dstv, src)
                        e.wait_ge(qS, cnt["qS"])
                        inc(e, G, "G")
                bar(["sync"])

                @blk.sync
                def _(e):
                    for s in range(8):
                        srcv = s_kern.rearrange("(hgp s) (t2 t1) -> s t2 hgp t1",
                                                hgp=NHGP, s=8, t2=16, t1=128)[s]
                        dma(e, qS, "qS",
                            kstr.ap()[16 * s:16 * s + 16].rearrange(
                                "t2 (hgp t1) -> t2 hgp t1", hgp=NHGP, t1=128),
                            srcv)
                    e.wait_ge(qS, cnt["qS"])
                    inc(e, G, "G")
                g1 = cnt["G"]
                if PHASES == "B4w":
                    finish_dump(kstr, 1024)
                    return nc

                for hgp in range(NHGP):
                    bk = banks[2 + hgp % 2]

                    @blk.tensor
                    def _(e, _h=hgp, _bk=bk, _g=g1):
                        if _h == 0:
                            e.wait_ge(G, _g)
                        if _h >= 2:
                            e.wait_ge(pV, cnt["pV"] - 1)
                        mm = e.matmul(_bk[:, 0:512], kstr[:, _h * 128:(_h + 1) * 128],
                                 b1rsb[:], start=True, stop=True)
                        tinc(mm, pP, "pP")

                    @blk.vector
                    def _(e, _h=hgp, _bk=bk):
                        e.wait_ge(pP, cnt["pP"])
                        dstv = Ssl.ap().rearrange(
                            "p (cp m2 hgp s) -> p hgp s m2 cp",
                            cp=2, m2=M2, hgp=NHGP, s=8)[:, _h]
                        e.tensor_copy(dstv, _bk[:, 0:512].rearrange(
                            "p (s m2 cp) -> p s m2 cp", s=8, m2=M2, cp=2))
                        inc(e, pV, "pV")
                bar(["vector"])

                if PHASES == "B4r":
                    finish_dump(Ssl)
                    return nc
                stage2(Ssl, KFsl, om_ld)

                ph_ld = load_tables("Ph")

                if PHASES == "B4s":
                    finish_dump(KFsl)
                    return nc
                half = NPL // 2

                @blk.vector
                def _(e):
                    zr, zi = Zsl[:, 0:half], Zsl[:, NPL:NPL + half]
                    kr, ki = KFsl[:, 0:half], KFsl[:, NPL:NPL + half]
                    tt = big32[:, 0:half]
                    e.tensor_tensor(tt, zi, ki, ALU.mult)
                    e.tensor_tensor(WmT[:, 0:half], zr, kr, ALU.mult)
                    e.drain()
                    e.tensor_tensor(WmT[:, 0:half], WmT[:, 0:half], tt, ALU.subtract)
                    e.drain()
                    e.tensor_tensor(tt, zi, kr, ALU.mult)
                    e.tensor_tensor(WmT[:, NPL:NPL + half], zr, ki, ALU.mult)
                    e.drain()
                    e.tensor_tensor(WmT[:, NPL:NPL + half], WmT[:, NPL:NPL + half],
                                    tt, ALU.add)
                    inc(e, pV, "pV")

                @blk.gpsimd
                def _(e):
                    zr, zi = Zsl[:, half:NPL], Zsl[:, NPL + half:2 * NPL]
                    kr, ki = KFsl[:, half:NPL], KFsl[:, NPL + half:2 * NPL]
                    tt = qre[:, 0:half]
                    e.tensor_tensor(tt, zi, ki, ALU.mult)
                    e.tensor_tensor(WmT[:, half:NPL], zr, kr, ALU.mult)
                    e.drain()
                    e.tensor_tensor(WmT[:, half:NPL], WmT[:, half:NPL], tt, ALU.subtract)
                    e.drain()
                    e.tensor_tensor(tt, zi, kr, ALU.mult)
                    e.tensor_tensor(WmT[:, NPL + half:2 * NPL], zr, ki, ALU.mult)
                    e.drain()
                    e.tensor_tensor(WmT[:, NPL + half:2 * NPL],
                                    WmT[:, NPL + half:2 * NPL], tt, ALU.add)
                    inc(e, pG2, "pG2")

                for tix in range(32):
                    @blk.gpsimd
                    def _(e, _ti=tix):
                        e.tensor_tensor(
                            xnds[:, _ti * CH_SL:(_ti + 1) * CH_SL],
                            TB1[:, _ti * 1024 + h0:_ti * 1024 + h0 + CH_SL],
                            dbc[:, h0:h0 + CH_SL], ALU.mult)
                        if _ti == 31:
                            inc(e, pG2, "pG2")
                bar(["vector", "gpsimd"])

                if PHASES == "B5":
                    finish_dump(WmT)
                    return nc
                @blk.tensor
                def _(e, _t=ph_ld):
                    e.wait_ge(G, _t)
                for m2 in range(M2):
                    bkr = banks[4 + (m2 % 2) * 2]
                    bki = banks[5 + (m2 % 2) * 2]

                    @blk.tensor
                    def _(e, _m2=m2, _r=bkr, _i=bki):
                        if _m2 >= 2:
                            e.wait_ge(pS, cnt["pS"] - 1)
                        wre = WmT[:, _m2 * CH_SL:(_m2 + 1) * CH_SL]
                        wim = WmT[:, NPL + _m2 * CH_SL:NPL + (_m2 + 1) * CH_SL]
                        e.matmul(_r[:, 0:CH_SL], om(0, _m2), wre, start=True, stop=False)
                        mm1 = e.matmul(_r[:, 0:CH_SL], om(2, _m2), wim, start=False, stop=True)
                        tinc(mm1, pP, "pP")
                        e.matmul(_i[:, 0:CH_SL], om(1, _m2), wre, start=True, stop=False)
                        mm2 = e.matmul(_i[:, 0:CH_SL], om(0, _m2), wim, start=False, stop=True)
                        tinc(mm2, pP, "pP")

                    @blk.vector
                    def _(e, _m2=m2, _r=bkr, _i=bki):
                        e.wait_ge(pP, cnt["pP"])
                        for c, bk in ((0, _r), (1, _i)):
                            dstv = Zsl.ap().rearrange(
                                "p (c hsg hssub m2) -> p c m2 hsg hssub",
                                c=2, hsg=NHSG, hssub=4, m2=M2)[:, c, _m2]
                            e.tensor_copy(dstv, bk[:, 0:CH_SL].rearrange(
                                "p (hsg hssub) -> p hsg hssub", hsg=NHSG, hssub=4))
                        inc(e, pS, "pS")
                bar(["vector"])

                if PHASES == "B6":
                    finish_dump(Zsl)
                    return nc
                ust_ready = []
                psy_done = []
                add_done = []
                gelu_done = []
                for hsg in range(NHSG):
                    bky = banks[2 + hsg % 2]
                    go = (hsg % 2) * 128
                    go2 = 256 + (hsg % 2) * 128

                    @blk.tensor
                    def _(e, _hsg=hsg):
                        if _hsg >= 1:
                            e.wait_ge(pS, ust_ready[_hsg - 1])
                        e.matmul(psA[:, 0:128], Zsl[:, _hsg * 128:(_hsg + 1) * 128],
                                 ident[:], start=True, stop=True)
                        mm = e.matmul(psA[:, 128:256],
                                 Zsl[:, NPL + _hsg * 128:NPL + (_hsg + 1) * 128],
                                 ident[:], start=True, stop=True)
                        tinc(mm, pP, "pP")

                    @blk.vector
                    def _(e):
                        e.wait_ge(pP, cnt["pP"])
                        e.tensor_copy(ust0[:], psA[:, 0:128])
                        e.tensor_copy(ust1[:], psA[:, 128:256])
                        inc(e, pS, "pS")
                    ust_ready.append(cnt["pS"])

                    @blk.tensor
                    def _(e, _y=bky, _hsg=hsg, _gv=ust_ready[-1]):
                        e.wait_ge(pS, _gv)
                        if _hsg >= 2:
                            e.wait_ge(pS, psy_done[_hsg - 2])
                        e.matmul(_y[:, 0:128], ust0[:], th0sb[:], start=True, stop=False)
                        mm = e.matmul(_y[:, 0:128], ust1[:], th1sb[:], start=False, stop=True)
                        tinc(mm, pP, "pP")

                    @blk.vector
                    def _(e, _hsg=hsg, _y=bky, _go2=go2):
                        e.wait_ge(pP, cnt["pP"])
                        if _hsg >= 2:
                            e.wait_ge(pV, add_done[_hsg - 2])
                        e.tensor_copy(gtmp[:, _go2:_go2 + 128], _y[:, 0:128])
                        inc(e, pS, "pS")
                    psy_done.append(cnt["pS"])

                    @blk.vector
                    def _(e, _hsg=hsg, _go=go, _go2=go2, _pd=psy_done[-1]):
                        e.wait_ge(pS, _pd)
                        av = xnds.ap().rearrange(
                            "p (c t2 hsg hssub) -> p hsg hssub t2 c",
                            c=2, t2=16, hsg=NHSG, hssub=4)[:, _hsg]
                        e.tensor_tensor(
                            gtmp[:, _go:_go + 128].rearrange(
                                "p (hssub t2 c) -> p hssub t2 c", hssub=4, t2=16, c=2),
                            gtmp[:, _go2:_go2 + 128].rearrange(
                                "p (hssub t2 c) -> p hssub t2 c", hssub=4, t2=16, c=2),
                            av, ALU.add)
                        inc(e, pV, "pV")
                    add_done.append(cnt["pV"])

                    @blk.scalar
                    def _(e, _hsg=hsg, _go=go, _sl=sl, _ad=add_done[-1]):
                        e.wait_ge(pV, _ad)
                        dstv = TB1.ap().rearrange(
                            "p (c t2 sl hsg hssub) -> p sl hsg hssub t2 c",
                            c=2, t2=16, sl=N_SL, hsg=NHSG, hssub=4)[:, _sl, _hsg]
                        e.activation(dstv,
                                     gtmp[:, _go:_go + 128].rearrange(
                                         "p (hssub t2 c) -> p hssub t2 c",
                                         hssub=4, t2=16, c=2),
                                     AF.Gelu_apprx_tanh)
                        inc(e, pS, "pS")
                    gelu_done.append(cnt["pS"])
                bar(["scalar"])
        if PHASES in ("B1", "B"):
            finish_dump()
            return nc
        if DEBUG:
            @blk.sync
            def _(e):
                dma(e, qS, "qS", dbg_y2, TB1[:])
                e.wait_ge(qS, cnt["qS"])
            bar(["sync"])

        NT_TG = TGT // 128

        def ln_pass(src_ap, stat, dst_fn, wait_pp=None):
            bn6, bn2, mrr = stat

            @blk.vector
            def _(e):
                e.bn_stats(bn6[:, 0:6], src_ap[:, 0:512])
                e.bn_stats(bn6[:, 6:12], src_ap[:, 512:1024])
                e.drain()
                e.bn_aggr(bn2[:], bn6[:])
                inc(e, pV, "pV")

            @blk.gpsimd
            def _(e):
                e.wait_ge(pV, cnt["pV"])
                e.tensor_scalar(mrr[:, 1:2], bn2[:, 1:2], 1e-5, None, ALU.add, ALU.bypass)
                e.drain()
                e.tensor_tensor(mrr[:, 0:1], mrr[:, 1:2], nhalf[:, 0:1], ALU.pow)
                inc(e, pG2, "pG2")

            @blk.vector
            def _(e):
                e.wait_ge(pG2, cnt["pG2"])
                if wait_pp is not None:
                    e.wait_ge(pP, wait_pp)
                dst_fn(e, bn2[:, 0:1], mrr[:, 0:1])
                inc(e, pV, "pV")

        with contextlib.ExitStack() as st_c:
            y2T = st_c.enter_context(nc.sbuf_tensor("y2T", [128, 8 * TGT], F16))
            attf = st_c.enter_context(nc.sbuf_tensor("attf", [128, 2 * TGT], F16))
            attm = st_c.enter_context(nc.sbuf_tensor("attm", [128, NT_TG * 1024], F16))
            wcA = st_c.enter_context(nc.sbuf_tensor("wcA", [128, 2048], F16))
            wcB = st_c.enter_context(nc.sbuf_tensor("wcB", [128, 2048], F16))
            xtl = st_c.enter_context(nc.sbuf_tensor("xtl", [128, H], F16))
            xtl2 = st_c.enter_context(nc.sbuf_tensor("xtl2", [128, H], F16))
            evt = st_c.enter_context(nc.sbuf_tensor("evt", [128, TGT], F16))
            evt2 = st_c.enter_context(nc.sbuf_tensor("evt2", [128, TGT], F16))
            cb6 = st_c.enter_context(nc.sbuf_tensor("cb6", [128, 12], F32))
            cb2 = st_c.enter_context(nc.sbuf_tensor("cb2", [128, 2], F32))
            cmr = st_c.enter_context(nc.sbuf_tensor("cmr", [128, 2], F32))
            ctt = st_c.enter_context(nc.sbuf_tensor("ctt", [128, H], F16))
            stat = (cb6, cb2, cmr)

            @blk.sync
            def _(e):
                e.wait_ge(pG2, wcast_tgt["wout"])
            woutv = wout16.rearrange("(kc p) o -> p kc o", kc=8, p=128)
            for g in range(N_TG):
                for tt in range(NT_TG):
                    for hc in range(8):
                        bk = banks[(tt * 8 + hc) % 2]
                        tia = g * NT_TG + tt

                        @blk.tensor
                        def _(e, _t=tt, _hc=hc, _bk=bk, _tia=tia, _g=g):
                            if _t + _hc > 0 or _g > 0:
                                e.wait_ge(pS, cnt["pS"] - 1)
                            mm = e.matmul(_bk[:, 0:128],
                                     TB1[:, _tia * 1024 + _hc * 128:
                                         _tia * 1024 + (_hc + 1) * 128],
                                     ident[:], start=True, stop=True)
                            tinc(mm, pP, "pP")

                        @blk.vector
                        def _(e, _t=tt, _hc=hc, _bk=bk):
                            e.wait_ge(pP, cnt["pP"])
                            e.tensor_copy(y2T[:, _hc * TGT + _t * 128:
                                       _hc * TGT + (_t + 1) * 128], _bk[:, 0:128])
                            inc(e, pS, "pS")
                bar(["vector"])

                for oc in range(8):
                    wb = wcA if oc % 2 == 0 else wcB

                    wsem = wA if oc % 2 == 0 else wB
                    wnm = "wA" if oc % 2 == 0 else "wB"

                    @blk.sync
                    def _(e, _oc=oc, _wb=wb, _ws=wsem, _wn=wnm):
                        if _oc >= 2:
                            e.wait_ge(cW, cnt.get("cW", 0) - 1)
                        dma(e, _ws, _wn, _wb[:, 0:1024].rearrange(
                            "p (kc o) -> p kc o", kc=8, o=128),
                            woutv[:, :, _oc * 128:(_oc + 1) * 128])
                        dma(e, _ws, _wn, _wb[:, 1024:2048].rearrange(
                            "p (kc o) -> p kc o", kc=8, o=128),
                            woutv[:, :, (_oc + 8) * 128:(_oc + 9) * 128])
                    wt = cnt[wnm]

                    @blk.tensor
                    def _(e, _oc=oc, _wb=wb, _wt=wt, _ws=wsem):
                        e.wait_ge(_ws, _wt)
                        if _oc >= 1:
                            e.wait_ge(pV, cnt["pV"] - 0)
                        for kc in range(8):
                            mm = e.matmul(psE[:, 0:TGT], _wb[:, kc * 128:(kc + 1) * 128],
                                     y2T[:, kc * TGT:(kc + 1) * TGT],
                                     start=(kc == 0), stop=(kc == 7))
                        tinc(mm, pP, "pP")
                        for kc in range(8):
                            mm = e.matmul(psF[:, 0:TGT],
                                     _wb[:, 1024 + kc * 128:1024 + (kc + 1) * 128],
                                     y2T[:, kc * TGT:(kc + 1) * TGT],
                                     start=(kc == 0), stop=(kc == 7))
                        tinc(mm, pP, "pP")

                    @blk.scalar
                    def _(e, _oc=oc):
                        e.wait_ge(pP, cnt["pP"])
                        inc(e, cW, "cW")
                        e.activation(evt[:], psE[:, 0:TGT], AF.Identity,
                                     bias=bosb[:, _oc:_oc + 1])
                        e.activation(evt2[:], psF[:, 0:TGT], AF.Sigmoid,
                                     bias=bosb[:, 8 + _oc:9 + _oc])
                        inc(e, pS, "pS")

                    @blk.vector
                    def _(e, _oc=oc):
                        e.wait_ge(pS, cnt["pS"])
                        e.tensor_tensor(attf[:, 0:TGT], evt[:], evt2[:], ALU.mult)
                        inc(e, pV, "pV")
                    gva = cnt["pV"]

                    for tt in range(NT_TG):
                        bk = banks[2 + tt % 2]

                        @blk.tensor
                        def _(e, _oc=oc, _tt=tt, _bk=bk, _gv=gva):
                            e.wait_ge(pV, _gv)
                            if _oc + _tt > 0:
                                e.wait_ge(pS, cnt["pS"] - 1)
                            mm = e.matmul(_bk[:, 0:128], attf[:, _tt * 128:(_tt + 1) * 128],
                                     ident[:], start=True, stop=True)
                            tinc(mm, pP, "pP")

                        @blk.vector
                        def _(e, _oc=oc, _tt=tt, _bk=bk):
                            e.wait_ge(pP, cnt["pP"])
                            e.tensor_copy(attm[:, _tt * 1024 + _oc * 128:
                                        _tt * 1024 + (_oc + 1) * 128], _bk[:, 0:128])
                            inc(e, pS, "pS")
                bar(["vector"])

                if g == 0:
                    @blk.sync
                    def _(e):
                        e.wait_ge(pG2, cnt['xf16_ready'])
                for tt in range(NT_TG):
                    tia = g * NT_TG + tt
                    xb2 = xtl if tt % 2 == 0 else xtl2
                    xsem = xA if tt % 2 == 0 else xB
                    xnm = "xA" if tt % 2 == 0 else "xB"

                    @blk.sync
                    def _(e, _tia=tia, _xb=xb2, _tt=tt, _xs=xsem, _xn=xnm):
                        if _tt >= 2 or g > 0:
                            e.wait_ge(pV, cnt["pV"] - 2)
                        dma(e, _xs, _xn, _xb[:],
                            xf16.rearrange("(ti t1) h -> ti t1 h", t1=128)[_tia])
                    xt_t = cnt[xnm]

                    @blk.vector
                    def _(e, _t=xt_t, _xs=xsem):
                        e.wait_ge(_xs, _t)

                    src = attm[:, tt * 1024:(tt + 1) * 1024]

                    def dst_fn(e, m_ap, r_ap, _tia=tia, _src=src, _xb=xb2):
                        e.tensor_scalar(ctt[:], _src, m_ap, r_ap,
                                        ALU.subtract, ALU.mult)
                        e.drain()
                        e.tensor_tensor(TB1[:, _tia * 1024:(_tia + 1) * 1024],
                                        ctt[:], _xb[:], ALU.add)
                    ln_pass(src, stat, dst_fn)
                bar(["vector"])
        if DEBUG:
            @blk.sync
            def _(e):
                dma(e, qS, "qS", dbg_x1, TB1[:])
                e.wait_ge(qS, cnt["qS"])
            bar(["sync"])

        if PHASES == "Cpre":
            finish_dump()
            return nc

        with contextlib.ExitStack() as st_f:
            x1n = st_f.enter_context(nc.sbuf_tensor("x1n", [128, H], F16))
            x1nT = st_f.enter_context(nc.sbuf_tensor("x1nT", [128, 8 * TGT], F16))
            HID = st_f.enter_context(nc.sbuf_tensor("HID", [128, 32 * TGT], F16))
            HID2 = st_f.enter_context(nc.sbuf_tensor("HID2", [128, 32 * TGT], F16))
            f3m = st_f.enter_context(nc.sbuf_tensor("f3m", [128, NT_TG * 1024], F16))
            whA = st_f.enter_context(nc.sbuf_tensor("whA", [128, 4096], F16))
            whB = st_f.enter_context(nc.sbuf_tensor("whB", [128, 4096], F16))
            w12A = st_f.enter_context(nc.sbuf_tensor("w12A", [128, 2048], F16))
            w12B = st_f.enter_context(nc.sbuf_tensor("w12B", [128, 2048], F16))
            outb = st_f.enter_context(nc.sbuf_tensor("outb", [128, H], F32))
            fb6 = st_f.enter_context(nc.sbuf_tensor("fb6", [128, 12], F32))
            fb2 = st_f.enter_context(nc.sbuf_tensor("fb2", [128, 2], F32))
            fmr = st_f.enter_context(nc.sbuf_tensor("fmr", [128, 2], F32))
            stat2 = (fb6, fb2, fmr)
            F3 = x1nT
            ftt = x1n

            @blk.sync
            def _(e):
                e.wait_ge(pG2, wcast_tgt["h2"])
            w1v = w116.rearrange("(kc p) o -> p kc o", kc=8, p=128)
            w2v = w216.rearrange("(kc p) o -> p kc o", kc=8, p=128)
            h1v = h116.rearrange("(kc p) o -> p kc o", kc=32, p=128)
            h2v = h216.rearrange("(kc p) o -> p kc o", kc=32, p=128)
            w3v = w316.rearrange("(kc p) o -> p kc o", kc=32, p=128)
            out_t = 0

            for g in range(N_TG):
                for tt in range(NT_TG):
                    tia = g * NT_TG + tt
                    src = TB1[:, tia * 1024:(tia + 1) * 1024]

                    def dst_fn(e, m_ap, r_ap, _src=src):
                        e.tensor_scalar(x1n[:], _src, m_ap, r_ap,
                                        ALU.subtract, ALU.mult)
                    ln_pass(src, stat2, dst_fn,
                            wait_pp=(cnt["pP"] if (tt > 0 or g > 0) else None))
                    gvb = cnt["pV"]
                    for hc in range(8):
                        bk = banks[hc % 2]

                        @blk.tensor
                        def _(e, _tt=tt, _hc=hc, _bk=bk, _gv=gvb):
                            e.wait_ge(pV, _gv)
                            if _tt + _hc > 0 or g > 0:
                                e.wait_ge(pS, cnt["pS"] - 1)
                            mm = e.matmul(_bk[:, 0:128], x1n[:, _hc * 128:(_hc + 1) * 128],
                                     ident[:], start=True, stop=True)
                            tinc(mm, pP, "pP")

                        @blk.vector
                        def _(e, _tt=tt, _hc=hc, _bk=bk):
                            e.wait_ge(pP, cnt["pP"])
                            e.tensor_copy(x1nT[:, _hc * TGT + _tt * 128:
                                        _hc * TGT + (_tt + 1) * 128], _bk[:, 0:128])
                            inc(e, pS, "pS")
                bar(["vector"])

                if PHASES == "C45" and g == 0:
                    finish_dump(x1nT, 4096)
                    return nc
                for ht in range(32):
                    wb = w12A if ht % 2 == 0 else w12B

                    wsem = wA if ht % 2 == 0 else wB
                    wnm = "wA" if ht % 2 == 0 else "wB"

                    @blk.sync
                    def _(e, _ht=ht, _wb=wb, _ws=wsem, _wn=wnm):
                        if _ht >= 2:
                            e.wait_ge(cW, cnt["cW"] - 1)
                        dma(e, _ws, _wn, _wb[:, 0:1024].rearrange(
                            "p (kc o) -> p kc o", kc=8, o=128),
                            w1v[:, :, _ht * 128:(_ht + 1) * 128])
                        dma(e, _ws, _wn, _wb[:, 1024:2048].rearrange(
                            "p (kc o) -> p kc o", kc=8, o=128),
                            w2v[:, :, _ht * 128:(_ht + 1) * 128])
                    wt = cnt[wnm]

                    @blk.tensor
                    def _(e, _ht=ht, _wb=wb, _wt=wt, _ws=wsem):
                        e.wait_ge(_ws, _wt)
                        if _ht >= 1:
                            e.wait_ge(pV, cnt["pV"])
                        for kc in range(8):
                            mm = e.matmul(psE[:, 0:TGT], _wb[:, kc * 128:(kc + 1) * 128],
                                     x1nT[:, kc * TGT:(kc + 1) * TGT],
                                     start=(kc == 0), stop=(kc == 7))
                        tinc(mm, pP, "pP")
                        for kc in range(8):
                            mm = e.matmul(psF[:, 0:TGT],
                                     _wb[:, 1024 + kc * 128:1024 + (kc + 1) * 128],
                                     x1nT[:, kc * TGT:(kc + 1) * TGT],
                                     start=(kc == 0), stop=(kc == 7))
                        tinc(mm, pP, "pP")

                    @blk.scalar
                    def _(e, _ht=ht):
                        e.wait_ge(pP, cnt["pP"])
                        inc(e, cW, "cW")
                        e.activation(f3m[:, 0:512], psE[:, 0:TGT], AF.Silu,
                                     scale=s1sb[:, _ht:_ht + 1])
                        e.activation(f3m[:, 512:1024], psF[:, 0:TGT], AF.Identity,
                                     scale=s2sb[:, _ht:_ht + 1])
                        inc(e, pS, "pS")

                    @blk.vector
                    def _(e, _ht=ht):
                        e.wait_ge(pS, cnt["pS"])
                        e.tensor_tensor(HID[:, _ht * TGT:(_ht + 1) * TGT],
                                        f3m[:, 0:512], f3m[:, 512:1024], ALU.mult)
                        inc(e, pV, "pV")
                bar(["vector"])

                if PHASES == "C6a" and g == 0:
                    finish_dump(HID, 4096)
                    return nc
                for stagei, (wv, scl, srcbuf, dstbuf) in enumerate(
                        ((h1v, sh1sb, HID, HID2), (h2v, sh2sb, HID2, HID))):
                    for ot in range(32):
                        wb = whA if ot % 2 == 0 else whB

                        wsem = wA if ot % 2 == 0 else wB
                        wnm = "wA" if ot % 2 == 0 else "wB"

                        @blk.sync
                        def _(e, _ot=ot, _wb=wb, _wv=wv, _ws=wsem, _wn=wnm):
                            if _ot >= 2 or stagei > 0 or g > 0:
                                e.wait_ge(cW, cnt["cW"] - 1)
                            dma(e, _ws, _wn, _wb[:].rearrange(
                                "p (kc o) -> p kc o", kc=32, o=128),
                                _wv[:, :, _ot * 128:(_ot + 1) * 128])
                        wt = cnt[wnm]

                        pbk = psE if ot % 2 == 0 else psF

                        @blk.tensor
                        def _(e, _ot=ot, _wb=wb, _wt=wt, _src=srcbuf, _pb=pbk,
                              _ws=wsem):
                            e.wait_ge(_ws, _wt)
                            if _ot >= 2:
                                e.wait_ge(pS, cnt["pS"] - 1)
                            for kc in range(32):
                                mm = e.matmul(_pb[:, 0:TGT], _wb[:, kc * 128:(kc + 1) * 128],
                                         _src[:, kc * TGT:(kc + 1) * TGT],
                                         start=(kc == 0), stop=(kc == 31))
                            tinc(mm, pP, "pP")

                        @blk.scalar
                        def _(e, _ot=ot, _dst=dstbuf, _scl=scl, _pb=pbk):
                            e.wait_ge(pP, cnt["pP"])
                            inc(e, cW, "cW")
                            e.activation(_dst[:, _ot * TGT:(_ot + 1) * TGT],
                                         _pb[:, 0:TGT], AF.Gelu,
                                         scale=_scl[:, _ot:_ot + 1])
                            inc(e, pS, "pS")
                    bar(["scalar"])

                if PHASES == "C6b" and g == 0:
                    finish_dump(HID, 4096)
                    return nc
                for dc in range(8):
                    wb = whA if dc % 2 == 0 else whB

                    wsem = wA if dc % 2 == 0 else wB
                    wnm = "wA" if dc % 2 == 0 else "wB"

                    @blk.sync
                    def _(e, _dc=dc, _wb=wb, _ws=wsem, _wn=wnm):
                        e.wait_ge(cW, cnt["cW"] - 1)
                        dma(e, _ws, _wn, _wb[:].rearrange(
                            "p (kc o) -> p kc o", kc=32, o=128),
                            w3v[:, :, _dc * 128:(_dc + 1) * 128])
                    wt = cnt[wnm]

                    pbk = psE if dc % 2 == 0 else psF

                    @blk.tensor
                    def _(e, _dc=dc, _wb=wb, _wt=wt, _pb=pbk, _ws=wsem):
                        e.wait_ge(_ws, _wt)
                        if _dc >= 2:
                            e.wait_ge(pS, cnt["pS"] - 1)
                        for kc in range(32):
                            mm = e.matmul(_pb[:, 0:TGT], _wb[:, kc * 128:(kc + 1) * 128],
                                     HID[:, kc * TGT:(kc + 1) * TGT],
                                     start=(kc == 0), stop=(kc == 31))
                        tinc(mm, pP, "pP")

                    @blk.scalar
                    def _(e, _dc=dc, _pb=pbk):
                        e.wait_ge(pP, cnt["pP"])
                        inc(e, cW, "cW")
                        e.activation(F3[:, _dc * TGT:(_dc + 1) * TGT], _pb[:, 0:TGT],
                                     AF.Identity, scale=s3sb[:, _dc:_dc + 1])
                        inc(e, pS, "pS")
                bar(["scalar"])

                if PHASES == "C6c" and g == 0:
                    finish_dump(F3, 4096)
                    return nc
                for tt in range(NT_TG):
                    for dc in range(8):
                        bk = banks[dc % 2]

                        @blk.tensor
                        def _(e, _tt=tt, _dc=dc, _bk=bk):
                            if _tt + _dc > 0 or g > 0:
                                e.wait_ge(pS, cnt["pS"] - 1)
                            mm = e.matmul(_bk[:, 0:128],
                                     F3[:, _dc * TGT + _tt * 128:
                                        _dc * TGT + (_tt + 1) * 128],
                                     ident[:], start=True, stop=True)
                            tinc(mm, pP, "pP")

                        @blk.vector
                        def _(e, _tt=tt, _dc=dc, _bk=bk):
                            e.wait_ge(pP, cnt["pP"])
                            e.tensor_copy(f3m[:, _tt * 1024 + _dc * 128:
                                       _tt * 1024 + (_dc + 1) * 128], _bk[:, 0:128])
                            inc(e, pS, "pS")
                bar(["vector"])
                for tt in range(NT_TG):
                    tia = g * NT_TG + tt
                    src = f3m[:, tt * 1024:(tt + 1) * 1024]
                    prev_out = out_t

                    def dst_fn(e, m_ap, r_ap, _src=src, _tia=tia, _po=prev_out):
                        if _po > 0:
                            e.wait_ge(qS, _po)
                        e.tensor_scalar(ftt[:], _src, m_ap, r_ap,
                                        ALU.subtract, ALU.mult)
                        e.drain()
                        e.tensor_tensor(outb[:], ftt[:],
                                        TB1[:, _tia * 1024:(_tia + 1) * 1024], ALU.add)
                    ln_pass(src, stat2, dst_fn)
                    gvv = cnt["pV"]

                    @blk.sync
                    def _(e, _tia=tia, _gv=gvv):
                        e.wait_ge(pV, _gv)
                        dma(e, qS, "qS", y_out.rearrange(
                            "b (t2 t1) h -> (b t2) t1 h", t1=128)[_tia], outb[:])
                    out_t = cnt["qS"]
            @blk.sync
            def _(e):
                e.wait_ge(qS, cnt["qS"])
            bar(["sync"])
    return nc


_tables = _host_tables()
_nc_cache = {}


def kernel(L0_attn_x, L0_attn_log_dt, L0_attn_A_real_log, L0_attn_A_imag,
           L0_attn_C_real, L0_attn_C_imag, L0_attn_D, L0_attn_W_out,
           L0_attn_b_out, L0_ffn_w1_w, L0_ffn_w1_s, L0_ffn_w2_w, L0_ffn_w2_s,
           L0_ffn_h1_w, L0_ffn_h1_s, L0_ffn_h2_w, L0_ffn_h2_s, L0_ffn_w3_w,
           L0_ffn_w3_s):
    f32 = np.float32
    c = np.ascontiguousarray
    common = dict(
        log_dt=c(np.asarray(L0_attn_log_dt, f32).reshape(1, H)),
        arlT=c(np.asarray(L0_attn_A_real_log, f32).T),
        aimT=c(np.asarray(L0_attn_A_imag, f32).T),
        creT=c(np.asarray(L0_attn_C_real, f32).T),
        cimT=c(np.asarray(L0_attn_C_imag, f32).T),
        d_vec=c(np.asarray(L0_attn_D, f32).reshape(1, H)),
        wout=c(np.asarray(L0_attn_W_out, f32)),
        bout=c(np.asarray(L0_attn_b_out, f32).reshape(1, 2 * H)),
        w1T=c(np.asarray(L0_ffn_w1_w, f32).T),
        w2T=c(np.asarray(L0_ffn_w2_w, f32).T),
        h1T=c(np.asarray(L0_ffn_h1_w, f32).T),
        h2T=c(np.asarray(L0_ffn_h2_w, f32).T),
        w3T=c(np.asarray(L0_ffn_w3_w, f32).T),
        s1=c(np.asarray(L0_ffn_w1_s, f32).reshape(1, DH)),
        s2=c(np.asarray(L0_ffn_w2_s, f32).reshape(1, DH)),
        sh1=c(np.asarray(L0_ffn_h1_s, f32).reshape(1, DH)),
        sh2=c(np.asarray(L0_ffn_h2_s, f32).reshape(1, DH)),
        s3=c(np.asarray(L0_ffn_w3_s, f32).reshape(1, H)),
        **_tables,
    )
    x = np.asarray(L0_attn_x, f32)
    in_maps = []
    for core in range(8):
        m = dict(common)
        m["x"] = c(x[2 * core:2 * core + 2])
        in_maps.append(m)
    if "nc" not in _nc_cache:
        _nc_cache["nc"] = build_nc()
    res = run_bass_kernel_spmd(_nc_cache["nc"], in_maps, core_ids=list(range(8)))
    if DEBUG:
        _dbg.clear()
        _dbg.update({k: v for k, v in res.results[0].items() if k.startswith("dbg")})
    out = np.concatenate([res.results[i]["y"] for i in range(8)], axis=0)
    return out


# version 06
# speedup vs baseline: 1.5203x; 1.0023x over previous
import contextlib
import numpy as np
import concourse.bass as bass
import concourse.mybir as mybir
from concourse.bass_utils import run_bass_kernel_spmd

F32 = mybir.dt.float32
F16 = mybir.dt.float16
AF = mybir.ActivationFunctionType
ALU = mybir.AluOpType

L = 2048
H = 1024
NST = 64
NF = 4096
T2 = 16
M2 = 32
DH = 4096

CH_SL = 64
N_SL = H // CH_SL
KSUB = 16
TGT = 512
N_TG = 2 * L // TGT

MAGIC = float(1.5 * 2 ** 23)
INV2PI = float(1.0 / (2 * np.pi))
CW1 = 6.28125
CW2 = 0.0019353628158569336
CW3 = float(2 * np.pi - CW1 - CW2)
HALFPI = float(np.pi / 2)

DEBUG = False
PHASES = "ALL"
_dbg = {}


def _host_tables():
    f16 = np.float16
    W32 = np.exp(-2j * np.pi / 32)
    B1 = np.zeros((128, 256), np.float32)
    for s in range(4):
        for t2 in range(T2):
            for m2 in range(M2):
                w = W32 ** (t2 * m2)
                B1[s * 32 + t2 * 2 + 0, s * 64 + m2 * 2 + 0] = w.real
                B1[s * 32 + t2 * 2 + 0, s * 64 + m2 * 2 + 1] = w.imag
                B1[s * 32 + t2 * 2 + 1, s * 64 + m2 * 2 + 0] = -w.imag
                B1[s * 32 + t2 * 2 + 1, s * 64 + m2 * 2 + 1] = w.real
    B1R = np.zeros((128, 512), np.float32)
    for s in range(8):
        for t2 in range(T2):
            for m2 in range(M2):
                w = W32 ** (t2 * m2)
                B1R[s * 16 + t2, s * 64 + m2 * 2 + 0] = w.real
                B1R[s * 16 + t2, s * 64 + m2 * 2 + 1] = w.imag
    Wn = np.exp(-2j * np.pi / NF)
    W128 = np.exp(-2j * np.pi / 128)
    t1 = np.arange(128)[:, None]
    m1 = np.arange(128)[None, :]
    Om = np.zeros((M2, 128, 128), complex)
    for m2 in range(M2):
        Om[m2] = (Wn ** (t1 * m2)) * (W128 ** (t1 * m1))
    V = np.conj(Wn)
    V128 = np.exp(+2j * np.pi / 128)
    Ph = np.zeros((M2, 128, 128), complex)
    for m2 in range(M2):
        Ph[m2] = (V128 ** (m1.T * t1.T)) * (V ** (t1.T * m2)) / NF
    V32 = np.exp(+2j * np.pi / 32)
    Th = np.zeros((2, 128, 128), np.float32)
    for hssub in range(4):
        for m2 in range(M2):
            k = hssub * 32 + m2
            for t2 in range(T2):
                v = V32 ** (m2 * t2)
                Th[0, k, hssub * 32 + t2 * 2 + 0] = v.real
                Th[0, k, hssub * 32 + t2 * 2 + 1] = v.imag
                Th[1, k, hssub * 32 + t2 * 2 + 0] = -v.imag
                Th[1, k, hssub * 32 + t2 * 2 + 1] = v.real
    stk = lambda a: np.ascontiguousarray(a.reshape(-1, a.shape[-1]).astype(f16))
    return dict(
        cB1=B1.astype(f16), cB1R=B1R.astype(f16),
        cOmRe=stk(Om.real), cOmIm=stk(Om.imag), cOmImN=stk(-Om.imag),
        cPhRe=stk(Ph.real), cPhIm=stk(Ph.imag), cPhImN=stk(-Ph.imag),
        cTh0=Th[0].astype(f16), cTh1=Th[1].astype(f16),
        cIdent=np.eye(128, dtype=f16),
    )


def build_nc():
    nc = bass.Bass("TRN2", target_bir_lowering=False)
    dram = nc.dram_tensor

    x_in = dram("x", [2, L, H], F32, kind="ExternalInput").ap()
    log_dt = dram("log_dt", [1, H], F32, kind="ExternalInput").ap()
    arlT = dram("arlT", [NST, H], F32, kind="ExternalInput").ap()
    aimT = dram("aimT", [NST, H], F32, kind="ExternalInput").ap()
    creT = dram("creT", [NST, H], F32, kind="ExternalInput").ap()
    cimT = dram("cimT", [NST, H], F32, kind="ExternalInput").ap()
    d_in = dram("d_vec", [1, H], F32, kind="ExternalInput").ap()
    wout = dram("wout", [H, 2 * H], F32, kind="ExternalInput").ap()
    bout = dram("bout", [1, 2 * H], F32, kind="ExternalInput").ap()
    w1T = dram("w1T", [H, DH], F32, kind="ExternalInput").ap()
    w2T = dram("w2T", [H, DH], F32, kind="ExternalInput").ap()
    h1T = dram("h1T", [DH, DH], F32, kind="ExternalInput").ap()
    h2T = dram("h2T", [DH, DH], F32, kind="ExternalInput").ap()
    w3T = dram("w3T", [DH, H], F32, kind="ExternalInput").ap()
    s1_in = dram("s1", [1, DH], F32, kind="ExternalInput").ap()
    s2_in = dram("s2", [1, DH], F32, kind="ExternalInput").ap()
    sh1_in = dram("sh1", [1, DH], F32, kind="ExternalInput").ap()
    sh2_in = dram("sh2", [1, DH], F32, kind="ExternalInput").ap()
    s3_in = dram("s3", [1, H], F32, kind="ExternalInput").ap()
    consts = {}
    for nm, shp in (("cB1", [128, 256]), ("cB1R", [128, 512]),
                    ("cOmRe", [M2 * 128, 128]), ("cOmIm", [M2 * 128, 128]),
                    ("cOmImN", [M2 * 128, 128]), ("cPhRe", [M2 * 128, 128]),
                    ("cPhIm", [M2 * 128, 128]), ("cPhImN", [M2 * 128, 128]),
                    ("cTh0", [128, 128]), ("cTh1", [128, 128]),
                    ("cIdent", [128, 128])):
        consts[nm] = dram(nm, shp, F16, kind="ExternalInput").ap()

    y_out = dram("y", [2, L, H], F32, kind="ExternalOutput").ap()

    xf16 = dram("xf16", [2 * L, H], F16, kind="Internal").ap()
    s_xnh = dram("s_xnh", [CH_SL, NF], F16, kind="Internal").ap()
    s_kern = dram("s_kern", [CH_SL, L], F16, kind="Internal").ap()
    wout16 = dram("wout16", [H, 2 * H], F16, kind="Internal").ap()
    w116 = dram("w116", [H, DH], F16, kind="Internal").ap()
    w216 = dram("w216", [H, DH], F16, kind="Internal").ap()
    h116 = dram("h116", [DH, DH], F16, kind="Internal").ap()
    h216 = dram("h216", [DH, DH], F16, kind="Internal").ap()
    w316 = dram("w316", [DH, H], F16, kind="Internal").ap()
    if DEBUG:
        dbg_xn = dram("dbg_xn", [128, 32 * 1024], F16, kind="ExternalOutput").ap()
        dbg_y2 = dram("dbg_y2", [128, 32 * 1024], F16, kind="ExternalOutput").ap()
        dbg_x1 = dram("dbg_x1", [128, 32 * 1024], F16, kind="ExternalOutput").ap()

    cnt = {}

    def tinc(instr, sem, name, n=1):
        instr.then_inc(sem, n)
        cnt[name] = cnt.get(name, 0) + n

    def inc(e, sem, name, n=1):
        last = getattr(e, "last", None)
        if last is not None:
            last.then_inc(sem, n)
            e.last = None
        else:
            try:
                en = str(e.engine)
            except Exception:
                en = ""
            if ("DVE" in en) or ("Activation" in en) or ("Pool" in en):
                e.drain()
            e.sem_inc(sem, n)
        cnt[name] = cnt.get(name, 0) + n

    st_main = contextlib.ExitStack()
    with st_main:
        TB1 = st_main.enter_context(nc.sbuf_tensor("TB1", [128, 32 * 1024], F16))
        ident = st_main.enter_context(nc.sbuf_tensor("ident", [128, 128], F16))
        b1sb = st_main.enter_context(nc.sbuf_tensor("b1sb", [128, 256], F16))
        b1rsb = st_main.enter_context(nc.sbuf_tensor("b1rsb", [128, 512], F16))
        th0sb = st_main.enter_context(nc.sbuf_tensor("th0sb", [128, 128], F16))
        th1sb = st_main.enter_context(nc.sbuf_tensor("th1sb", [128, 128], F16))
        dbc = st_main.enter_context(nc.sbuf_tensor("dbc", [128, H], F32))
        dtbc = st_main.enter_context(nc.sbuf_tensor("dtbc", [128, H], F32))
        prmA = st_main.enter_context(nc.sbuf_tensor("prmA", [128, H], F32))
        prmI = st_main.enter_context(nc.sbuf_tensor("prmI", [128, H], F32))
        prmCr = st_main.enter_context(nc.sbuf_tensor("prmCr", [128, H], F16))
        prmCi = st_main.enter_context(nc.sbuf_tensor("prmCi", [128, H], F16))
        bosb = st_main.enter_context(nc.sbuf_tensor("bosb", [128, 16], F32))
        s1sb = st_main.enter_context(nc.sbuf_tensor("s1sb", [128, 32], F32))
        s2sb = st_main.enter_context(nc.sbuf_tensor("s2sb", [128, 32], F32))
        sh1sb = st_main.enter_context(nc.sbuf_tensor("sh1sb", [128, 32], F32))
        sh2sb = st_main.enter_context(nc.sbuf_tensor("sh2sb", [128, 32], F32))
        s3sb = st_main.enter_context(nc.sbuf_tensor("s3sb", [128, 8], F32))
        row1 = st_main.enter_context(nc.sbuf_tensor("row1", [1, 2 * H], F32))
        epsc = st_main.enter_context(nc.sbuf_tensor("epsc", [128, 1], F32))
        ones1 = st_main.enter_context(nc.sbuf_tensor("ones1", [1, 128], F32))
        negone = st_main.enter_context(nc.sbuf_tensor("negone", [128, 64], F32))
        nhalf = st_main.enter_context(nc.sbuf_tensor("nhalf", [128, 1], F32))
        psA = st_main.enter_context(nc.psum_tensor("psA", [128, 512], F32))
        psB = st_main.enter_context(nc.psum_tensor("psB", [128, 512], F32))
        psC = st_main.enter_context(nc.psum_tensor("psC", [128, 512], F32))
        psD = st_main.enter_context(nc.psum_tensor("psD", [128, 512], F32))
        psE = st_main.enter_context(nc.psum_tensor("psE", [128, 512], F32))
        psF = st_main.enter_context(nc.psum_tensor("psF", [128, 512], F32))
        psG = st_main.enter_context(nc.psum_tensor("psG", [128, 512], F32))
        psH = st_main.enter_context(nc.psum_tensor("psH", [128, 512], F32))
        qS = st_main.enter_context(nc.semaphore("qS"))
        qG = st_main.enter_context(nc.semaphore("qG"))
        G = st_main.enter_context(nc.semaphore("G"))
        pP = st_main.enter_context(nc.semaphore("pP"))
        pV = st_main.enter_context(nc.semaphore("pV"))
        pS = st_main.enter_context(nc.semaphore("pS"))
        pG2 = st_main.enter_context(nc.semaphore("pG2"))
        wA = st_main.enter_context(nc.semaphore("wA"))
        wB = st_main.enter_context(nc.semaphore("wB"))
        cW = st_main.enter_context(nc.semaphore("cW"))
        xA = st_main.enter_context(nc.semaphore("xA"))
        xB = st_main.enter_context(nc.semaphore("xB"))
        blk = st_main.enter_context(nc.Block())
        class _Rec:
            def __init__(self, e):
                self._e = e
                self.last = None
            def __getattr__(self, name):
                f = getattr(self._e, name)
                if name in ("wait_ge", "sem_inc", "drain", "engine", "dma_start"):
                    return f
                if not callable(f):
                    return f
                def g(*a, **k):
                    r = f(*a, **k)
                    self.last = r
                    return r
                return g

        class _BlkWrap:
            def __init__(self, b):
                self._b = b
            def __getattr__(self, engname):
                dec = getattr(self._b, engname)
                if engname.startswith("_"):
                    return dec
                def mydec(fn):
                    def fn2(e):
                        if engname in ("vector", "scalar", "gpsimd"):
                            return fn(_Rec(e))
                        return fn(e)
                    return dec(fn2)
                return mydec

        blk = _BlkWrap(blk)

        banks = [psA, psB, psC, psD, psE, psF, psG, psH]

        def dma(e, sem, name, out, in_, **kw):
            e.dma_start(out=out, in_=in_, **kw).then_inc(sem, 16)
            cnt[name] = cnt.get(name, 0) + 16
            return cnt[name]

        def bar(engines_inc):
            for ename in engines_inc:
                def f(e):
                    inc(e, G, "G")
                getattr(blk, ename)(f)
            tgt = cnt.get("G", 0)
            for ename in ("sync", "gpsimd", "tensor", "vector", "scalar"):
                def f(e, _t=tgt):
                    e.wait_ge(G, _t)
                getattr(blk, ename)(f)

        def cmb(e, dre, dim_, are, aim, bre, bim, tmp):
            e.tensor_tensor(tmp, aim, bim, ALU.mult)
            e.tensor_tensor(dre, are, bre, ALU.mult)
            e.drain()
            e.tensor_tensor(dre, dre, tmp, ALU.subtract)
            e.tensor_tensor(tmp, aim, bre, ALU.mult)
            e.tensor_tensor(dim_, are, bim, ALU.mult)
            e.drain()
            e.tensor_tensor(dim_, dim_, tmp, ALU.add)

        wcast_tgt = {}

        @blk.gpsimd
        def _(e):
            for dst, srcp in ((prmCr, creT), (prmCi, cimT)):
                dma(e, qG, "qG", dst.ap()[0:64], srcp)
                dma(e, qG, "qG", dst.ap()[64:128], srcp)
            e.wait_ge(qG, cnt["qG"])
            inc(e, pG2, "pG2")
            cnt["prm_ready"] = cnt["pG2"]
            dma(e, qG, "qG", xf16, x_in.rearrange("b t h -> (b t) h"), max_dma_last_dim=4096)
            e.wait_ge(qG, cnt["qG"])
            inc(e, pG2, "pG2")
            cnt["xf16_ready"] = cnt["pG2"]
            wlist = (("wout", wout, wout16), ("w1", w1T, w116),
                     ("w2", w2T, w216), ("w3", w3T, w316),
                     ("h1", h1T, h116), ("h2", h2T, h216))
            if PHASES == "Cpre":
                wlist = wlist[:1]
            elif PHASES in ("C6a", "C6b", "C6c", "C45"):
                pass
            elif PHASES != "ALL":
                wlist = ()
            for nm, srcw, dst in wlist:
                dma(e, qG, "qG", dst, srcw, max_dma_last_dim=8192)
                e.wait_ge(qG, cnt["qG"])
                inc(e, pG2, "pG2")
                wcast_tgt[nm] = cnt["pG2"]

        @blk.sync
        def _(e):
            dma(e, qS, "qS", ident[:], consts["cIdent"])
            dma(e, qS, "qS", b1sb[:], consts["cB1"])
            dma(e, qS, "qS", b1rsb[:], consts["cB1R"])
            dma(e, qS, "qS", th0sb[:], consts["cTh0"])
            dma(e, qS, "qS", th1sb[:], consts["cTh1"])
            for dst, src in ((prmA, arlT), (prmI, aimT)):
                dma(e, qS, "qS", dst.ap()[0:64], src)
                dma(e, qS, "qS", dst.ap()[64:128], src)
            e.wait_ge(pG2, cnt["prm_ready"])
            dma(e, qS, "qS", row1[:1, 0:H], log_dt)
            with nc.allow_non_contiguous_dma(reason="tiny one-time scale loads"):
                for dst, src, n in ((s1sb, s1_in, 32), (s2sb, s2_in, 32),
                                    (sh1sb, sh1_in, 32), (sh2sb, sh2_in, 32),
                                    (s3sb, s3_in, 8), (bosb, bout, 16)):
                    dma(e, qS, "qS", dst.ap().rearrange("p (c o) -> p c o", c=n, o=1),
                        src.rearrange("o (c p) -> p c o", c=n, p=128))
            e.wait_ge(qS, cnt["qS"])
            inc(e, G, "G")
        g_a = cnt["G"]

        @blk.vector
        def _(e):
            e.memset(epsc[:], 1e-5)
            e.memset(ones1[:1, :], 1.0)
            e.memset(negone[:], -1.0)
            e.memset(nhalf[:], -0.5)
            e.drain()

        @blk.scalar
        def _(e):
            e.wait_ge(G, g_a)
            e.activation(row1[:1, 0:H], row1[:1, 0:H], AF.Exp)
            e.activation(prmA[:], prmA[:], AF.Exp)
            inc(e, pS, "pS")

        @blk.vector
        def _(e):
            e.wait_ge(pS, cnt["pS"])
            e.tensor_scalar(prmA[:], prmA[:], -1.0, None, ALU.mult, ALU.bypass)
            inc(e, pV, "pV")

        @blk.sync
        def _(e):
            e.wait_ge(pS, cnt["pS"])
            dma(e, qS, "qS", dtbc.ap()[0:1, 0:H], row1[:1, 0:H])
            e.wait_ge(qS, cnt["qS"])
            w = 1
            while w < 128:
                dma(e, qS, "qS", dtbc.ap()[w:2 * w, 0:H], dtbc.ap()[0:w, 0:H])
                e.wait_ge(qS, cnt["qS"])
                w *= 2
            dma(e, qS, "qS", row1[:1, 0:H], d_in)
            e.wait_ge(qS, cnt["qS"])
            dma(e, qS, "qS", dbc.ap()[0:1, 0:H], row1[:1, 0:H])
            e.wait_ge(qS, cnt["qS"])
            w = 1
            while w < 128:
                dma(e, qS, "qS", dbc.ap()[w:2 * w, 0:H], dbc.ap()[0:w, 0:H])
                e.wait_ge(qS, cnt["qS"])
                w *= 2
            inc(e, G, "G")
        bar(["sync"])

        with contextlib.ExitStack() as st_a:
            xa = st_a.enter_context(nc.sbuf_tensor("xa", [128, H], F16))
            xb = st_a.enter_context(nc.sbuf_tensor("xb", [128, H], F16))
            bn6 = st_a.enter_context(nc.sbuf_tensor("bn6", [128, 12], F32))
            bn2 = st_a.enter_context(nc.sbuf_tensor("bn2", [128, 2], F32))
            mrr = st_a.enter_context(nc.sbuf_tensor("mrr", [128, 2], F32))
            xv = x_in.rearrange("b (t2 t1) h -> (b t2) t1 h", t1=128)
            for ti in range(32):
                buf = xa if ti % 2 == 0 else xb

                xsem = xA if ti % 2 == 0 else xB
                xnm = "xA" if ti % 2 == 0 else "xB"

                @blk.gpsimd
                def _(e, _ti=ti, _buf=buf, _xs=xsem, _xn=xnm):
                    if _ti >= 2:
                        e.wait_ge(pV, cnt.get("pV", 0) - 1)
                    dma(e, _xs, _xn, _buf[:], xv[_ti])
                qg_t = cnt[xnm]

                @blk.vector
                def _(e, _buf=buf, _qt=qg_t, _xs=xsem):
                    e.wait_ge(_xs, _qt)
                    e.bn_stats(bn6[:, 0:6], _buf[:, 0:512])
                    e.bn_stats(bn6[:, 6:12], _buf[:, 512:1024])
                    e.drain()
                    e.bn_aggr(bn2[:], bn6[:])
                    inc(e, pV, "pV")

                @blk.gpsimd
                def _(e):
                    e.wait_ge(pV, cnt["pV"])
                    e.tensor_scalar(mrr[:, 1:2], bn2[:, 1:2], 1e-5, None, ALU.add, ALU.bypass)
                    e.drain()
                    e.tensor_tensor(mrr[:, 0:1], mrr[:, 1:2], nhalf[:, 0:1], ALU.pow)
                    inc(e, pG2, "pG2")

                @blk.vector
                def _(e, _ti=ti, _buf=buf):
                    e.wait_ge(pG2, cnt["pG2"])
                    e.tensor_scalar(TB1[:, _ti * 1024:(_ti + 1) * 1024], _buf[:],
                                    bn2[:, 0:1], mrr[:, 0:1], ALU.subtract, ALU.mult)
                    inc(e, pV, "pV")
            bar(["vector"])
        if DEBUG:
            @blk.sync
            def _(e):
                dma(e, qS, "qS", dbg_xn, TB1[:])
                e.wait_ge(qS, cnt["qS"])
            bar(["sync"])

        def finish_dump(extra=None, nc_extra=4096):
            @blk.gpsimd
            def _(e):
                e.wait_ge(G, cnt.get("G", 0))
                e.wait_ge(pV, cnt.get("pV", 0))
                e.wait_ge(pS, cnt.get("pS", 0))
                dma(e, qG, "qG",
                    y_out.rearrange("b (t2 t1) h -> t1 b t2 h", t1=128),
                    TB1.ap().rearrange("p (b t2 h) -> p b t2 h", b=2, t2=16, h=1024),
                    max_dma_last_dim=4096)
                if extra is not None:
                    e.wait_ge(qG, cnt["qG"])
                    yv = y_out.rearrange("b (tq p t1q) h -> b tq p (t1q h)",
                                         p=128, t1q=1)
                    for j in range(nc_extra // 1024):
                        dma(e, qG, "qG", yv[1, j],
                            extra.ap()[:, j * 1024:(j + 1) * 1024])
                e.wait_ge(qG, cnt["qG"])

            @blk.sync
            def _(e):
                e.wait_ge(qS, cnt["qS"])

        if PHASES == "A":
            finish_dump()
            return nc

        NHG = CH_SL // 4
        NHGP = CH_SL // 8
        NHSG = CH_SL // 4
        NPL = M2 * CH_SL

        with contextlib.ExitStack() as st_b:
            constb = st_b.enter_context(nc.sbuf_tensor("constb", [128, M2 * 3 * 128], F16))
            xnh = st_b.enter_context(nc.sbuf_tensor("xnh", [128, NF], F16))
            zst = st_b.enter_context(nc.sbuf_tensor("zst", [128, NHG * 128], F16))
            Ssl = st_b.enter_context(nc.sbuf_tensor("Ssl", [128, 2 * NPL], F16))
            Zsl = st_b.enter_context(nc.sbuf_tensor("Zsl", [128, 2 * NPL], F16))
            KFsl = st_b.enter_context(nc.sbuf_tensor("KFsl", [128, 2 * NPL], F16))
            WmT = st_b.enter_context(nc.sbuf_tensor("WmT", [128, 2 * NPL], F16))
            kstr = st_b.enter_context(nc.sbuf_tensor("kstr", [128, NHGP * 128], F16))
            qre = st_b.enter_context(nc.sbuf_tensor("qre", [128, KSUB * 128], F32))
            qim = st_b.enter_context(nc.sbuf_tensor("qim", [128, KSUB * 128], F32))
            pre = st_b.enter_context(nc.sbuf_tensor("pre", [128, KSUB * 16], F32))
            pim = st_b.enter_context(nc.sbuf_tensor("pim", [128, KSUB * 16], F32))
            pcmb = st_b.enter_context(nc.sbuf_tensor("pcmb", [128, KSUB * 16], F16))
            qcmb = st_b.enter_context(nc.sbuf_tensor("qcmb", [128, KSUB * 128], F16))
            lre = st_b.enter_context(nc.sbuf_tensor("lre", [128, KSUB], F32))
            lim = st_b.enter_context(nc.sbuf_tensor("lim", [128, KSUB], F32))
            pwr = st_b.enter_context(nc.sbuf_tensor("pwr", [128, KSUB], F32))
            pwi = st_b.enter_context(nc.sbuf_tensor("pwi", [128, KSUB], F32))
            t0 = st_b.enter_context(nc.sbuf_tensor("t0", [128, KSUB], F32))
            t1b = st_b.enter_context(nc.sbuf_tensor("t1b", [128, KSUB], F32))
            t2b = st_b.enter_context(nc.sbuf_tensor("t2b", [128, KSUB], F32))
            t3b = st_b.enter_context(nc.sbuf_tensor("t3b", [128, KSUB], F32))
            ctre = st_b.enter_context(nc.sbuf_tensor("ctre", [128, KSUB], F32))
            ctim = st_b.enter_context(nc.sbuf_tensor("ctim", [128, KSUB], F32))
            big32 = st_b.enter_context(nc.sbuf_tensor("big32", [128, 1024], F32))
            ust0 = st_b.enter_context(nc.sbuf_tensor("ust0", [128, 128], F16))
            ust1 = st_b.enter_context(nc.sbuf_tensor("ust1", [128, 128], F16))
            gtmp = st_b.enter_context(nc.sbuf_tensor("gtmp", [128, 512], F32))
            xnds = st_b.enter_context(nc.sbuf_tensor("xnds", [128, 32 * CH_SL], F16))
            def om(pl, m2):
                return constb[:, (pl * M2 + m2) * 128:(pl * M2 + m2) * 128 + 128]

            def load_tables(pre_fix):
                @blk.sync
                def _(e):
                    for i, nm in enumerate(("Re", "Im", "ImN")):
                        dma(e, qS, "qS",
                            constb[:, i * M2 * 128:(i + 1) * M2 * 128].rearrange(
                                "p (m k) -> p m k", m=M2, k=128),
                            consts["c" + pre_fix + nm].rearrange(
                                "(m p) k -> p m k", m=M2))
                    e.wait_ge(qS, cnt["qS"])
                    inc(e, G, "G")
                return cnt["G"]

            def stage2(src, dstbuf, tgt_qs):
                @blk.tensor
                def _(e, _t=tgt_qs):
                    e.wait_ge(G, _t)
                for m2 in range(M2):
                    bkr = banks[4 + (m2 % 2) * 2]
                    bki = banks[5 + (m2 % 2) * 2]

                    @blk.tensor
                    def _(e, _m2=m2, _r=bkr, _i=bki):
                        if _m2 >= 2:
                            e.wait_ge(pS, cnt["pS"] - 1)
                        sre = src[:, _m2 * CH_SL:(_m2 + 1) * CH_SL]
                        sim = src[:, NPL + _m2 * CH_SL:NPL + (_m2 + 1) * CH_SL]
                        e.matmul(_r[:, 0:CH_SL], om(0, _m2), sre, start=True, stop=False)
                        mm1 = e.matmul(_r[:, 0:CH_SL], om(2, _m2), sim, start=False, stop=True)
                        tinc(mm1, pP, "pP")
                        e.matmul(_i[:, 0:CH_SL], om(1, _m2), sre, start=True, stop=False)
                        mm2 = e.matmul(_i[:, 0:CH_SL], om(0, _m2), sim, start=False, stop=True)
                        tinc(mm2, pP, "pP")

                    @blk.vector
                    def _(e, _m2=m2, _r=bkr, _i=bki):
                        e.wait_ge(pP, cnt["pP"])
                        e.tensor_copy(dstbuf[:, _m2 * CH_SL:(_m2 + 1) * CH_SL], _r[:, 0:CH_SL])
                        e.tensor_copy(dstbuf[:, NPL + _m2 * CH_SL:NPL + (_m2 + 1) * CH_SL],
                               _i[:, 0:CH_SL])
                        inc(e, pS, "pS")
                bar(["vector"])

            for sl in range(1 if PHASES == "B1" else N_SL):
                h0 = sl * CH_SL
                om_ld = load_tables("Om")

                for ti in range(32):
                    bk = banks[ti % 2]

                    @blk.tensor
                    def _(e, _ti=ti, _bk=bk):
                        if _ti >= 2 or sl > 0:
                            e.wait_ge(pS, cnt["pS"] - 1)
                        mm = e.matmul(_bk[0:CH_SL, 0:128],
                                 TB1[:, _ti * 1024 + h0:_ti * 1024 + h0 + CH_SL],
                                 ident[:], start=True, stop=True)
                        tinc(mm, pP, "pP")

                    @blk.vector
                    def _(e, _ti=ti, _bk=bk):
                        e.wait_ge(pP, cnt["pP"])
                        e.tensor_copy(xnh[0:CH_SL, _ti * 128:(_ti + 1) * 128],
                               _bk[0:CH_SL, 0:128])
                        inc(e, pS, "pS")
                bar(["vector"])

                if PHASES == "B0":
                    finish_dump(xnh)
                    return nc
                @blk.sync
                def _(e):
                    dma(e, qS, "qS", s_xnh, xnh[0:CH_SL, :])
                    e.wait_ge(qS, cnt["qS"])
                    srcv = s_xnh.rearrange("(hg s) (c t2 t1) -> s c t2 hg t1",
                                           hg=NHG, s=4, c=2, t2=16, t1=128)
                    for s in range(4):
                        for c in range(2):
                            dstp = zst.ap()[s * 32 + c:s * 32 + c + 31:2]
                            dma(e, qS, "qS",
                                dstp.rearrange("t2 (hg t1) -> t2 hg t1",
                                               hg=NHG, t1=128),
                                srcv[s, c])
                    e.wait_ge(qS, cnt["qS"])
                    inc(e, G, "G")
                g0 = cnt["G"]

                if PHASES == "B1s":
                    finish_dump(zst, 2048)
                    return nc
                for hg in range(NHG):
                    bk = banks[2 + hg % 2]

                    @blk.tensor
                    def _(e, _hg=hg, _bk=bk, _g0=g0):
                        if _hg == 0:
                            e.wait_ge(G, _g0)
                        if _hg >= 2:
                            e.wait_ge(pV, cnt["pV"] - 1)
                        mm = e.matmul(_bk[:, 0:256], zst[:, _hg * 128:(_hg + 1) * 128],
                                 b1sb[:], start=True, stop=True)
                        tinc(mm, pP, "pP")

                    @blk.vector
                    def _(e, _hg=hg, _bk=bk):
                        e.wait_ge(pP, cnt["pP"])
                        dstv = Ssl.ap().rearrange(
                            "p (cp m2 hg s) -> p hg s m2 cp",
                            cp=2, m2=M2, hg=NHG, s=4)[:, _hg]
                        e.tensor_copy(dstv, _bk[:, 0:256].rearrange(
                            "p (s m2 cp) -> p s m2 cp", s=4, m2=M2, cp=2))
                        inc(e, pV, "pV")
                bar(["vector"])

                if PHASES == "B2":
                    finish_dump(Ssl)
                    return nc
                stage2(Ssl, Zsl, om_ld)

                if PHASES == "B3":
                    finish_dump(Zsl)
                    return nc
                for sub in range(CH_SL // KSUB):
                    cs = slice(h0 + sub * KSUB, h0 + sub * KSUB + KSUB)

                    @blk.vector
                    def _(e):
                        e.tensor_copy(t0[:], prmA[:, cs])
                        e.drain()
                        e.tensor_tensor(t1b[:], t0[:], dtbc[:, cs], ALU.mult)
                        e.tensor_tensor(t2b[:], prmI[:, cs], dtbc[:, cs], ALU.mult)
                        e.drain()
                        e.tensor_scalar(t3b[:], t2b[:], INV2PI, None, ALU.mult, ALU.bypass)
                        e.drain()
                        e.tensor_scalar(t3b[:], t3b[:], MAGIC, MAGIC, ALU.add, ALU.subtract)
                        e.drain()
                        e.scalar_tensor_tensor(pwr[:], t3b[:], -CW1, t2b[:], ALU.mult, ALU.add)
                        e.drain()
                        e.scalar_tensor_tensor(pwr[:], t3b[:], -CW2, pwr[:], ALU.mult, ALU.add)
                        e.drain()
                        e.scalar_tensor_tensor(pwr[:], t3b[:], -CW3, pwr[:], ALU.mult, ALU.add)
                        e.tensor_scalar(t2b[:], t2b[:], HALFPI, None, ALU.add, ALU.bypass)
                        e.drain()
                        e.tensor_scalar(t3b[:], t2b[:], INV2PI, None, ALU.mult, ALU.bypass)
                        e.drain()
                        e.tensor_scalar(t3b[:], t3b[:], MAGIC, MAGIC, ALU.add, ALU.subtract)
                        e.drain()
                        e.scalar_tensor_tensor(pwi[:], t3b[:], -CW1, t2b[:], ALU.mult, ALU.add)
                        e.drain()
                        e.scalar_tensor_tensor(pwi[:], t3b[:], -CW2, pwi[:], ALU.mult, ALU.add)
                        e.drain()
                        e.scalar_tensor_tensor(pwi[:], t3b[:], -CW3, pwi[:], ALU.mult, ALU.add)
                        inc(e, pV, "pV")

                    @blk.scalar
                    def _(e):
                        e.wait_ge(pV, cnt["pV"])
                        e.activation(t1b[:], t1b[:], AF.Exp)
                        e.activation(pwr[:], pwr[:], AF.Sin)
                        e.activation(pwi[:], pwi[:], AF.Sin)
                        inc(e, pS, "pS")

                    @blk.vector
                    def _(e):
                        e.wait_ge(pS, cnt["pS"])
                        e.tensor_tensor(lre[:], t1b[:], pwi[:], ALU.mult)
                        e.tensor_tensor(lim[:], t1b[:], pwr[:], ALU.mult)
                        inc(e, pV, "pV")
                    if PHASES == "B4l" and sub == 0:
                        @blk.vector
                        def _(e):
                            kd = xnh
                            e.tensor_copy(kd[:, 0:16], lre[:])
                            e.tensor_copy(kd[:, 16:32], lim[:])
                            e.tensor_copy(kd[:, 32:48], t1b[:])
                            e.tensor_copy(kd[:, 48:64], pwr[:])
                            e.tensor_copy(kd[:, 64:80], pwi[:])
                            e.tensor_copy(kd[:, 80:96], dtbc[:, cs])
                            e.tensor_copy(kd[:, 96:112], prmI[:, cs])
                            e.tensor_copy(kd[:, 112:128], prmA[:, cs])
                            inc(e, pV, "pV")
                        bar(["vector"])
                        finish_dump(xnh, 1024)
                        return nc

                    @blk.scalar
                    def _(e):
                        e.wait_ge(pV, cnt["pV"])
                        e.activation(t1b[:], t0[:], AF.Square)
                        e.activation(t2b[:], prmI[:, cs], AF.Square)
                        inc(e, pS, "pS")

                    @blk.vector
                    def _(e):
                        e.wait_ge(pS, cnt["pS"])
                        e.tensor_tensor(t1b[:], t1b[:], t2b[:], ALU.add)
                        inc(e, pV, "pV")

                    @blk.gpsimd
                    def _(e):
                        e.wait_ge(pV, cnt["pV"])
                        e.tensor_tensor(t1b[:], t1b[:], negone[:, 0:KSUB], ALU.pow)
                        inc(e, pG2, "pG2")

                    @blk.vector
                    def _(e):
                        e.wait_ge(pG2, cnt["pG2"])
                        e.tensor_tensor(t2b[:], t0[:], t1b[:], ALU.mult)
                        e.scalar_tensor_tensor(t3b[:], prmI[:, cs], -1.0, t1b[:],
                                               ALU.mult, ALU.mult)
                        e.drain()
                        e.tensor_scalar(t0[:], lre[:], -1.0, None, ALU.add, ALU.bypass)
                        e.drain()
                        cmb(e, ctre[:], ctim[:], t0[:], lim[:], t2b[:], t3b[:], t1b[:])
                        e.drain()
                        cmb(e, t2b[:], t3b[:], ctre[:], ctim[:],
                            prmCr[:, cs], prmCi[:, cs], t1b[:])
                        e.drain()
                        e.tensor_copy(ctre[:], t2b[:])
                        e.tensor_copy(ctim[:], t3b[:])
                        e.tensor_copy(pwr[:], lre[:])
                        e.tensor_copy(pwi[:], lim[:])
                        e.drain()
                        for _ in range(7):
                            e.tensor_tensor(t0[:], pwr[:], pwr[:], ALU.mult)
                            e.tensor_tensor(t1b[:], pwi[:], pwi[:], ALU.mult)
                            e.tensor_tensor(t2b[:], pwr[:], pwi[:], ALU.mult)
                            e.drain()
                            e.tensor_tensor(pwr[:], t0[:], t1b[:], ALU.subtract)
                            e.tensor_scalar(pwi[:], t2b[:], 2.0, None, ALU.mult, ALU.bypass)
                            e.drain()
                        pr = pre.ap().rearrange("p (ch l) -> p l ch", ch=KSUB, l=16)
                        pi = pim.ap().rearrange("p (ch l) -> p l ch", ch=KSUB, l=16)
                        e.tensor_copy(pr[:, 0], ctre[:])
                        e.tensor_copy(pi[:, 0], ctim[:])
                        inc(e, pV, "pV")

                    for kstep in range(4):
                        w = 2 ** kstep

                        @blk.vector
                        def _(e, _w=w):
                            pr = pre.ap().rearrange("p (ch l) -> p l ch", ch=KSUB, l=16)
                            pi = pim.ap().rearrange("p (ch l) -> p l ch", ch=KSUB, l=16)
                            are, aim = pr[:, 0:_w], pi[:, 0:_w]
                            dre, dim_ = pr[:, _w:2 * _w], pi[:, _w:2 * _w]
                            bre = pwr.ap().rearrange("p (u ch) -> p u ch", u=1, ch=KSUB)
                            bim = pwi.ap().rearrange("p (u ch) -> p u ch", u=1, ch=KSUB)
                            bre2, are2 = bass.broadcast_tensor_aps(bre, are)
                            bim2, aim2 = bass.broadcast_tensor_aps(bim, aim)
                            tmp = big32[:, 0:_w * KSUB].rearrange(
                                "p (l c) -> p l c", l=_w, c=KSUB)
                            cmb(e, dre, dim_, are2, aim2, bre2, bim2, tmp)
                            if _w < 8:
                                e.drain()
                                e.tensor_tensor(t0[:], pwr[:], pwr[:], ALU.mult)
                                e.tensor_tensor(t1b[:], pwi[:], pwi[:], ALU.mult)
                                e.tensor_tensor(t2b[:], pwr[:], pwi[:], ALU.mult)
                                e.drain()
                                e.tensor_tensor(pwr[:], t0[:], t1b[:], ALU.subtract)
                                e.tensor_scalar(pwi[:], t2b[:], 2.0, None,
                                                ALU.mult, ALU.bypass)
                            inc(e, pV, "pV")

                    @blk.vector
                    def _(e):
                        qr = qre.ap().rearrange("p (ch l) -> p l ch", ch=KSUB, l=128)
                        qi = qim.ap().rearrange("p (ch l) -> p l ch", ch=KSUB, l=128)
                        e.memset(qr[:, 0], 1.0)
                        e.memset(qi[:, 0], 0.0)
                        e.tensor_copy(pwr[:], lre[:])
                        e.tensor_copy(pwi[:], lim[:])
                        e.drain()
                        inc(e, pV, "pV")
                    for kstep in range(7):
                        w = 2 ** kstep

                        @blk.vector
                        def _(e, _w=w):
                            qr = qre.ap().rearrange("p (ch l) -> p l ch", ch=KSUB, l=128)
                            qi = qim.ap().rearrange("p (ch l) -> p l ch", ch=KSUB, l=128)
                            are, aim = qr[:, 0:_w], qi[:, 0:_w]
                            dre, dim_ = qr[:, _w:2 * _w], qi[:, _w:2 * _w]
                            bre = pwr.ap().rearrange("p (u ch) -> p u ch", u=1, ch=KSUB)
                            bim = pwi.ap().rearrange("p (u ch) -> p u ch", u=1, ch=KSUB)
                            bre2, are2 = bass.broadcast_tensor_aps(bre, are)
                            bim2, aim2 = bass.broadcast_tensor_aps(bim, aim)
                            tmp = big32[:, 0:_w * KSUB].rearrange(
                                "p (l c) -> p l c", l=_w, c=KSUB)
                            cmb(e, dre, dim_, are2, aim2, bre2, bim2, tmp)
                            if _w < 64:
                                e.drain()
                                e.tensor_tensor(t0[:], pwr[:], pwr[:], ALU.mult)
                                e.tensor_tensor(t1b[:], pwi[:], pwi[:], ALU.mult)
                                e.tensor_tensor(t2b[:], pwr[:], pwi[:], ALU.mult)
                                e.drain()
                                e.tensor_tensor(pwr[:], t0[:], t1b[:], ALU.subtract)
                                e.tensor_scalar(pwi[:], t2b[:], 2.0, None,
                                                ALU.mult, ALU.bypass)
                            inc(e, pV, "pV")

                    @blk.vector
                    def _(e):
                        e.drain()
                        e.tensor_scalar(pcmb.ap()[0:64], pre.ap()[0:64], 2.0, None,
                                        ALU.mult, ALU.bypass)
                        e.tensor_scalar(pcmb.ap()[64:128], pim.ap()[64:128], -2.0, None,
                                        ALU.mult, ALU.bypass)
                        e.tensor_copy(qcmb.ap()[0:64], qre.ap()[0:64])
                        e.tensor_copy(qcmb.ap()[64:128], qim.ap()[64:128])
                        inc(e, pV, "pV")
                    gv = cnt["pV"]
                    if PHASES == "B4q" and sub == 0:
                        bar(["vector"])
                        finish_dump(qcmb, 2048)
                        return nc
                    if PHASES == "B4p" and sub == 0:
                        bar(["vector"])
                        finish_dump(pcmb, 256)
                        return nc

                    kstg = xnh
                    for ch in range(KSUB):
                        bk = banks[6 + (ch // 4) % 2]
                        fo = (ch % 4) * 128

                        @blk.tensor
                        def _(e, _ch=ch, _bk=bk, _fo=fo, _gv=gv):
                            if _ch == 0:
                                e.wait_ge(pV, _gv)
                            if _ch % 4 == 0 and _ch >= 8:
                                e.wait_ge(pS, cnt["pS"] - 1)
                            mm = e.matmul(_bk[0:16, _fo:_fo + 128],
                                     pcmb[:, _ch * 16:(_ch + 1) * 16],
                                     qcmb[:, _ch * 128:(_ch + 1) * 128],
                                     start=True, stop=True)
                            if _ch % 4 == 3:
                                tinc(mm, pP, "pP")

                        if ch % 4 == 3:
                            @blk.vector
                            def _(e, _ch=ch, _bk=bk):
                                e.wait_ge(pP, cnt["pP"])
                                base = (_ch - 3) * 128
                                e.tensor_copy(kstg[0:16, base:base + 512], _bk[0:16, 0:512])
                                inc(e, pS, "pS")
                    bar(["vector"])

                    @blk.sync
                    def _(e, _sub=sub):
                        src = kstg[0:16, 0:KSUB * 128].rearrange("t2 (ch t1) -> t2 ch t1",
                                                      ch=KSUB, t1=128)
                        dstv = s_kern.rearrange("(sb ch) (t2 t1) -> sb t2 ch t1",
                                                sb=CH_SL // KSUB, ch=KSUB,
                                                t2=16, t1=128)[_sub]
                        dma(e, qS, "qS", dstv, src)
                        e.wait_ge(qS, cnt["qS"])
                        inc(e, G, "G")
                bar(["sync"])

                @blk.sync
                def _(e):
                    for s in range(8):
                        srcv = s_kern.rearrange("(hgp s) (t2 t1) -> s t2 hgp t1",
                                                hgp=NHGP, s=8, t2=16, t1=128)[s]
                        dma(e, qS, "qS",
                            kstr.ap()[16 * s:16 * s + 16].rearrange(
                                "t2 (hgp t1) -> t2 hgp t1", hgp=NHGP, t1=128),
                            srcv)
                    e.wait_ge(qS, cnt["qS"])
                    inc(e, G, "G")
                g1 = cnt["G"]
                if PHASES == "B4w":
                    finish_dump(kstr, 1024)
                    return nc

                for hgp in range(NHGP):
                    bk = banks[2 + hgp % 2]

                    @blk.tensor
                    def _(e, _h=hgp, _bk=bk, _g=g1):
                        if _h == 0:
                            e.wait_ge(G, _g)
                        if _h >= 2:
                            e.wait_ge(pV, cnt["pV"] - 1)
                        mm = e.matmul(_bk[:, 0:512], kstr[:, _h * 128:(_h + 1) * 128],
                                 b1rsb[:], start=True, stop=True)
                        tinc(mm, pP, "pP")

                    @blk.vector
                    def _(e, _h=hgp, _bk=bk):
                        e.wait_ge(pP, cnt["pP"])
                        dstv = Ssl.ap().rearrange(
                            "p (cp m2 hgp s) -> p hgp s m2 cp",
                            cp=2, m2=M2, hgp=NHGP, s=8)[:, _h]
                        e.tensor_copy(dstv, _bk[:, 0:512].rearrange(
                            "p (s m2 cp) -> p s m2 cp", s=8, m2=M2, cp=2))
                        inc(e, pV, "pV")
                bar(["vector"])

                if PHASES == "B4r":
                    finish_dump(Ssl)
                    return nc
                stage2(Ssl, KFsl, om_ld)

                ph_ld = load_tables("Ph")

                if PHASES == "B4s":
                    finish_dump(KFsl)
                    return nc
                half = NPL // 2

                @blk.vector
                def _(e):
                    zr, zi = Zsl[:, 0:half], Zsl[:, NPL:NPL + half]
                    kr, ki = KFsl[:, 0:half], KFsl[:, NPL:NPL + half]
                    tt = big32[:, 0:half]
                    e.tensor_tensor(tt, zi, ki, ALU.mult)
                    e.tensor_tensor(WmT[:, 0:half], zr, kr, ALU.mult)
                    e.drain()
                    e.tensor_tensor(WmT[:, 0:half], WmT[:, 0:half], tt, ALU.subtract)
                    e.drain()
                    e.tensor_tensor(tt, zi, kr, ALU.mult)
                    e.tensor_tensor(WmT[:, NPL:NPL + half], zr, ki, ALU.mult)
                    e.drain()
                    e.tensor_tensor(WmT[:, NPL:NPL + half], WmT[:, NPL:NPL + half],
                                    tt, ALU.add)
                    inc(e, pV, "pV")

                @blk.gpsimd
                def _(e):
                    zr, zi = Zsl[:, half:NPL], Zsl[:, NPL + half:2 * NPL]
                    kr, ki = KFsl[:, half:NPL], KFsl[:, NPL + half:2 * NPL]
                    tt = qre[:, 0:half]
                    e.tensor_tensor(tt, zi, ki, ALU.mult)
                    e.tensor_tensor(WmT[:, half:NPL], zr, kr, ALU.mult)
                    e.drain()
                    e.tensor_tensor(WmT[:, half:NPL], WmT[:, half:NPL], tt, ALU.subtract)
                    e.drain()
                    e.tensor_tensor(tt, zi, kr, ALU.mult)
                    e.tensor_tensor(WmT[:, NPL + half:2 * NPL], zr, ki, ALU.mult)
                    e.drain()
                    e.tensor_tensor(WmT[:, NPL + half:2 * NPL],
                                    WmT[:, NPL + half:2 * NPL], tt, ALU.add)
                    inc(e, pG2, "pG2")

                for tix in range(32):
                    @blk.gpsimd
                    def _(e, _ti=tix):
                        e.tensor_tensor(
                            xnds[:, _ti * CH_SL:(_ti + 1) * CH_SL],
                            TB1[:, _ti * 1024 + h0:_ti * 1024 + h0 + CH_SL],
                            dbc[:, h0:h0 + CH_SL], ALU.mult)
                        if _ti == 31:
                            inc(e, pG2, "pG2")
                bar(["vector", "gpsimd"])

                if PHASES == "B5":
                    finish_dump(WmT)
                    return nc
                @blk.tensor
                def _(e, _t=ph_ld):
                    e.wait_ge(G, _t)
                for m2 in range(M2):
                    bkr = banks[4 + (m2 % 2) * 2]
                    bki = banks[5 + (m2 % 2) * 2]

                    @blk.tensor
                    def _(e, _m2=m2, _r=bkr, _i=bki):
                        if _m2 >= 2:
                            e.wait_ge(pS, cnt["pS"] - 1)
                        wre = WmT[:, _m2 * CH_SL:(_m2 + 1) * CH_SL]
                        wim = WmT[:, NPL + _m2 * CH_SL:NPL + (_m2 + 1) * CH_SL]
                        e.matmul(_r[:, 0:CH_SL], om(0, _m2), wre, start=True, stop=False)
                        mm1 = e.matmul(_r[:, 0:CH_SL], om(2, _m2), wim, start=False, stop=True)
                        tinc(mm1, pP, "pP")
                        e.matmul(_i[:, 0:CH_SL], om(1, _m2), wre, start=True, stop=False)
                        mm2 = e.matmul(_i[:, 0:CH_SL], om(0, _m2), wim, start=False, stop=True)
                        tinc(mm2, pP, "pP")

                    @blk.vector
                    def _(e, _m2=m2, _r=bkr, _i=bki):
                        e.wait_ge(pP, cnt["pP"])
                        for c, bk in ((0, _r), (1, _i)):
                            dstv = Zsl.ap().rearrange(
                                "p (c hsg hssub m2) -> p c m2 hsg hssub",
                                c=2, hsg=NHSG, hssub=4, m2=M2)[:, c, _m2]
                            e.tensor_copy(dstv, bk[:, 0:CH_SL].rearrange(
                                "p (hsg hssub) -> p hsg hssub", hsg=NHSG, hssub=4))
                        inc(e, pS, "pS")
                bar(["vector"])

                if PHASES == "B6":
                    finish_dump(Zsl)
                    return nc
                ust_ready = []
                psy_done = []
                add_done = []
                gelu_done = []
                for hsg in range(NHSG):
                    bky = banks[2 + hsg % 2]
                    go = (hsg % 2) * 128
                    go2 = 256 + (hsg % 2) * 128

                    @blk.tensor
                    def _(e, _hsg=hsg):
                        if _hsg >= 1:
                            e.wait_ge(pS, ust_ready[_hsg - 1])
                        e.matmul(psA[:, 0:128], Zsl[:, _hsg * 128:(_hsg + 1) * 128],
                                 ident[:], start=True, stop=True)
                        mm = e.matmul(psA[:, 128:256],
                                 Zsl[:, NPL + _hsg * 128:NPL + (_hsg + 1) * 128],
                                 ident[:], start=True, stop=True)
                        tinc(mm, pP, "pP")

                    @blk.vector
                    def _(e):
                        e.wait_ge(pP, cnt["pP"])
                        e.tensor_copy(ust0[:], psA[:, 0:128])
                        e.tensor_copy(ust1[:], psA[:, 128:256])
                        inc(e, pS, "pS")
                    ust_ready.append(cnt["pS"])

                    @blk.tensor
                    def _(e, _y=bky, _hsg=hsg, _gv=ust_ready[-1]):
                        e.wait_ge(pS, _gv)
                        if _hsg >= 2:
                            e.wait_ge(pS, psy_done[_hsg - 2])
                        e.matmul(_y[:, 0:128], ust0[:], th0sb[:], start=True, stop=False)
                        mm = e.matmul(_y[:, 0:128], ust1[:], th1sb[:], start=False, stop=True)
                        tinc(mm, pP, "pP")

                    @blk.vector
                    def _(e, _hsg=hsg, _y=bky, _go2=go2):
                        e.wait_ge(pP, cnt["pP"])
                        if _hsg >= 2:
                            e.wait_ge(pV, add_done[_hsg - 2])
                        e.tensor_copy(gtmp[:, _go2:_go2 + 128], _y[:, 0:128])
                        inc(e, pS, "pS")
                    psy_done.append(cnt["pS"])

                    @blk.vector
                    def _(e, _hsg=hsg, _go=go, _go2=go2, _pd=psy_done[-1]):
                        e.wait_ge(pS, _pd)
                        av = xnds.ap().rearrange(
                            "p (c t2 hsg hssub) -> p hsg hssub t2 c",
                            c=2, t2=16, hsg=NHSG, hssub=4)[:, _hsg]
                        e.tensor_tensor(
                            gtmp[:, _go:_go + 128].rearrange(
                                "p (hssub t2 c) -> p hssub t2 c", hssub=4, t2=16, c=2),
                            gtmp[:, _go2:_go2 + 128].rearrange(
                                "p (hssub t2 c) -> p hssub t2 c", hssub=4, t2=16, c=2),
                            av, ALU.add)
                        inc(e, pV, "pV")
                    add_done.append(cnt["pV"])

                    @blk.scalar
                    def _(e, _hsg=hsg, _go=go, _sl=sl, _ad=add_done[-1]):
                        e.wait_ge(pV, _ad)
                        dstv = TB1.ap().rearrange(
                            "p (c t2 sl hsg hssub) -> p sl hsg hssub t2 c",
                            c=2, t2=16, sl=N_SL, hsg=NHSG, hssub=4)[:, _sl, _hsg]
                        e.activation(dstv,
                                     gtmp[:, _go:_go + 128].rearrange(
                                         "p (hssub t2 c) -> p hssub t2 c",
                                         hssub=4, t2=16, c=2),
                                     AF.Gelu_apprx_tanh)
                        inc(e, pS, "pS")
                    gelu_done.append(cnt["pS"])
                bar(["scalar"])
        if PHASES in ("B1", "B"):
            finish_dump()
            return nc
        if DEBUG:
            @blk.sync
            def _(e):
                dma(e, qS, "qS", dbg_y2, TB1[:])
                e.wait_ge(qS, cnt["qS"])
            bar(["sync"])

        NT_TG = TGT // 128

        def ln_pass(src_ap, stat, dst_fn, wait_pp=None):
            bn6, bn2, mrr = stat

            @blk.vector
            def _(e):
                e.bn_stats(bn6[:, 0:6], src_ap[:, 0:512])
                e.bn_stats(bn6[:, 6:12], src_ap[:, 512:1024])
                e.drain()
                e.bn_aggr(bn2[:], bn6[:])
                inc(e, pV, "pV")

            @blk.gpsimd
            def _(e):
                e.wait_ge(pV, cnt["pV"])
                e.tensor_scalar(mrr[:, 1:2], bn2[:, 1:2], 1e-5, None, ALU.add, ALU.bypass)
                e.drain()
                e.tensor_tensor(mrr[:, 0:1], mrr[:, 1:2], nhalf[:, 0:1], ALU.pow)
                inc(e, pG2, "pG2")

            @blk.vector
            def _(e):
                e.wait_ge(pG2, cnt["pG2"])
                if wait_pp is not None:
                    e.wait_ge(pP, wait_pp)
                dst_fn(e, bn2[:, 0:1], mrr[:, 0:1])
                inc(e, pV, "pV")

        with contextlib.ExitStack() as st_c:
            y2T = st_c.enter_context(nc.sbuf_tensor("y2T", [128, 8 * TGT], F16))
            attf = st_c.enter_context(nc.sbuf_tensor("attf", [128, 2 * TGT], F16))
            attm = st_c.enter_context(nc.sbuf_tensor("attm", [128, NT_TG * 1024], F16))
            wcA = st_c.enter_context(nc.sbuf_tensor("wcA", [128, 2048], F16))
            wcB = st_c.enter_context(nc.sbuf_tensor("wcB", [128, 2048], F16))
            xtl = st_c.enter_context(nc.sbuf_tensor("xtl", [128, H], F16))
            xtl2 = st_c.enter_context(nc.sbuf_tensor("xtl2", [128, H], F16))
            evt = st_c.enter_context(nc.sbuf_tensor("evt", [128, TGT], F16))
            evt2 = st_c.enter_context(nc.sbuf_tensor("evt2", [128, TGT], F16))
            cb6 = st_c.enter_context(nc.sbuf_tensor("cb6", [128, 12], F32))
            cb2 = st_c.enter_context(nc.sbuf_tensor("cb2", [128, 2], F32))
            cmr = st_c.enter_context(nc.sbuf_tensor("cmr", [128, 2], F32))
            ctt = st_c.enter_context(nc.sbuf_tensor("ctt", [128, H], F16))
            stat = (cb6, cb2, cmr)

            @blk.sync
            def _(e):
                e.wait_ge(pG2, wcast_tgt["wout"])
            woutv = wout16.rearrange("(kc p) o -> p kc o", kc=8, p=128)
            for g in range(N_TG):
                for tt in range(NT_TG):
                    for hc in range(8):
                        bk = banks[(tt * 8 + hc) % 2]
                        tia = g * NT_TG + tt

                        @blk.tensor
                        def _(e, _t=tt, _hc=hc, _bk=bk, _tia=tia, _g=g):
                            if _t + _hc > 0 or _g > 0:
                                e.wait_ge(pS, cnt["pS"] - 1)
                            mm = e.matmul(_bk[:, 0:128],
                                     TB1[:, _tia * 1024 + _hc * 128:
                                         _tia * 1024 + (_hc + 1) * 128],
                                     ident[:], start=True, stop=True)
                            tinc(mm, pP, "pP")

                        @blk.vector
                        def _(e, _t=tt, _hc=hc, _bk=bk):
                            e.wait_ge(pP, cnt["pP"])
                            e.tensor_copy(y2T[:, _hc * TGT + _t * 128:
                                       _hc * TGT + (_t + 1) * 128], _bk[:, 0:128])
                            inc(e, pS, "pS")
                bar(["vector"])

                for oc in range(8):
                    wb = wcA if oc % 2 == 0 else wcB

                    wsem = wA if oc % 2 == 0 else wB
                    wnm = "wA" if oc % 2 == 0 else "wB"

                    @blk.sync
                    def _(e, _oc=oc, _wb=wb, _ws=wsem, _wn=wnm):
                        if _oc >= 2:
                            e.wait_ge(cW, cnt.get("cW", 0) - 1)
                        dma(e, _ws, _wn, _wb[:, 0:1024].rearrange(
                            "p (kc o) -> p kc o", kc=8, o=128),
                            woutv[:, :, _oc * 128:(_oc + 1) * 128])
                        dma(e, _ws, _wn, _wb[:, 1024:2048].rearrange(
                            "p (kc o) -> p kc o", kc=8, o=128),
                            woutv[:, :, (_oc + 8) * 128:(_oc + 9) * 128])
                    wt = cnt[wnm]

                    @blk.tensor
                    def _(e, _oc=oc, _wb=wb, _wt=wt, _ws=wsem):
                        e.wait_ge(_ws, _wt)
                        if _oc >= 1:
                            e.wait_ge(pV, cnt["pV"] - 0)
                        for kc in range(8):
                            mm = e.matmul(psE[:, 0:TGT], _wb[:, kc * 128:(kc + 1) * 128],
                                     y2T[:, kc * TGT:(kc + 1) * TGT],
                                     start=(kc == 0), stop=(kc == 7))
                        tinc(mm, pP, "pP")
                        for kc in range(8):
                            mm = e.matmul(psF[:, 0:TGT],
                                     _wb[:, 1024 + kc * 128:1024 + (kc + 1) * 128],
                                     y2T[:, kc * TGT:(kc + 1) * TGT],
                                     start=(kc == 0), stop=(kc == 7))
                        tinc(mm, pP, "pP")

                    @blk.scalar
                    def _(e, _oc=oc):
                        e.wait_ge(pP, cnt["pP"])
                        inc(e, cW, "cW")
                        e.activation(evt2[:], psF[:, 0:TGT], AF.Sigmoid,
                                     bias=bosb[:, 8 + _oc:9 + _oc])
                        inc(e, pS, "pS")

                    @blk.vector
                    def _(e, _oc=oc):
                        e.wait_ge(pS, cnt["pS"])
                        e.tensor_scalar(evt[:], psE[:, 0:TGT], bosb[:, _oc:_oc + 1],
                                        None, ALU.add, ALU.bypass)
                        e.drain()
                        e.tensor_tensor(attf[:, 0:TGT], evt[:], evt2[:], ALU.mult)
                        inc(e, pV, "pV")
                    gva = cnt["pV"]

                    for tt in range(NT_TG):
                        bk = banks[2 + tt % 2]

                        @blk.tensor
                        def _(e, _oc=oc, _tt=tt, _bk=bk, _gv=gva):
                            e.wait_ge(pV, _gv)
                            if _oc + _tt > 0:
                                e.wait_ge(pS, cnt["pS"] - 1)
                            mm = e.matmul(_bk[:, 0:128], attf[:, _tt * 128:(_tt + 1) * 128],
                                     ident[:], start=True, stop=True)
                            tinc(mm, pP, "pP")

                        @blk.vector
                        def _(e, _oc=oc, _tt=tt, _bk=bk):
                            e.wait_ge(pP, cnt["pP"])
                            e.tensor_copy(attm[:, _tt * 1024 + _oc * 128:
                                        _tt * 1024 + (_oc + 1) * 128], _bk[:, 0:128])
                            inc(e, pS, "pS")
                bar(["vector"])

                if g == 0:
                    @blk.sync
                    def _(e):
                        e.wait_ge(pG2, cnt['xf16_ready'])
                for tt in range(NT_TG):
                    tia = g * NT_TG + tt
                    xb2 = xtl if tt % 2 == 0 else xtl2
                    xsem = xA if tt % 2 == 0 else xB
                    xnm = "xA" if tt % 2 == 0 else "xB"

                    @blk.sync
                    def _(e, _tia=tia, _xb=xb2, _tt=tt, _xs=xsem, _xn=xnm):
                        if _tt >= 2 or g > 0:
                            e.wait_ge(pV, cnt["pV"] - 2)
                        dma(e, _xs, _xn, _xb[:],
                            xf16.rearrange("(ti t1) h -> ti t1 h", t1=128)[_tia])
                    xt_t = cnt[xnm]

                    @blk.vector
                    def _(e, _t=xt_t, _xs=xsem):
                        e.wait_ge(_xs, _t)

                    src = attm[:, tt * 1024:(tt + 1) * 1024]

                    def dst_fn(e, m_ap, r_ap, _tia=tia, _src=src, _xb=xb2):
                        e.tensor_scalar(ctt[:], _src, m_ap, r_ap,
                                        ALU.subtract, ALU.mult)
                        e.drain()
                        e.tensor_tensor(TB1[:, _tia * 1024:(_tia + 1) * 1024],
                                        ctt[:], _xb[:], ALU.add)
                    ln_pass(src, stat, dst_fn)
                bar(["vector"])
        if DEBUG:
            @blk.sync
            def _(e):
                dma(e, qS, "qS", dbg_x1, TB1[:])
                e.wait_ge(qS, cnt["qS"])
            bar(["sync"])

        if PHASES == "Cpre":
            finish_dump()
            return nc

        with contextlib.ExitStack() as st_f:
            x1n = st_f.enter_context(nc.sbuf_tensor("x1n", [128, H], F16))
            x1nT = st_f.enter_context(nc.sbuf_tensor("x1nT", [128, 8 * TGT], F16))
            HID = st_f.enter_context(nc.sbuf_tensor("HID", [128, 32 * TGT], F16))
            HID2 = st_f.enter_context(nc.sbuf_tensor("HID2", [128, 32 * TGT], F16))
            f3m = st_f.enter_context(nc.sbuf_tensor("f3m", [128, NT_TG * 1024], F16))
            whA = st_f.enter_context(nc.sbuf_tensor("whA", [128, 4096], F16))
            whB = st_f.enter_context(nc.sbuf_tensor("whB", [128, 4096], F16))
            w12A = st_f.enter_context(nc.sbuf_tensor("w12A", [128, 2048], F16))
            w12B = st_f.enter_context(nc.sbuf_tensor("w12B", [128, 2048], F16))
            outb = st_f.enter_context(nc.sbuf_tensor("outb", [128, H], F32))
            fb6 = st_f.enter_context(nc.sbuf_tensor("fb6", [128, 12], F32))
            fb2 = st_f.enter_context(nc.sbuf_tensor("fb2", [128, 2], F32))
            fmr = st_f.enter_context(nc.sbuf_tensor("fmr", [128, 2], F32))
            stat2 = (fb6, fb2, fmr)
            F3 = x1nT
            ftt = x1n

            @blk.sync
            def _(e):
                e.wait_ge(pG2, wcast_tgt["h2"])
            w1v = w116.rearrange("(kc p) o -> p kc o", kc=8, p=128)
            w2v = w216.rearrange("(kc p) o -> p kc o", kc=8, p=128)
            h1v = h116.rearrange("(kc p) o -> p kc o", kc=32, p=128)
            h2v = h216.rearrange("(kc p) o -> p kc o", kc=32, p=128)
            w3v = w316.rearrange("(kc p) o -> p kc o", kc=32, p=128)
            out_t = 0

            for g in range(N_TG):
                for tt in range(NT_TG):
                    tia = g * NT_TG + tt
                    src = TB1[:, tia * 1024:(tia + 1) * 1024]

                    def dst_fn(e, m_ap, r_ap, _src=src):
                        e.tensor_scalar(x1n[:], _src, m_ap, r_ap,
                                        ALU.subtract, ALU.mult)
                    ln_pass(src, stat2, dst_fn,
                            wait_pp=(cnt["pP"] if (tt > 0 or g > 0) else None))
                    gvb = cnt["pV"]
                    for hc in range(8):
                        bk = banks[hc % 2]

                        @blk.tensor
                        def _(e, _tt=tt, _hc=hc, _bk=bk, _gv=gvb):
                            e.wait_ge(pV, _gv)
                            if _tt + _hc > 0 or g > 0:
                                e.wait_ge(pS, cnt["pS"] - 1)
                            mm = e.matmul(_bk[:, 0:128], x1n[:, _hc * 128:(_hc + 1) * 128],
                                     ident[:], start=True, stop=True)
                            tinc(mm, pP, "pP")

                        @blk.vector
                        def _(e, _tt=tt, _hc=hc, _bk=bk):
                            e.wait_ge(pP, cnt["pP"])
                            e.tensor_copy(x1nT[:, _hc * TGT + _tt * 128:
                                        _hc * TGT + (_tt + 1) * 128], _bk[:, 0:128])
                            inc(e, pS, "pS")
                bar(["vector"])

                if PHASES == "C45" and g == 0:
                    finish_dump(x1nT, 4096)
                    return nc
                for ht in range(32):
                    wb = w12A if ht % 2 == 0 else w12B

                    wsem = wA if ht % 2 == 0 else wB
                    wnm = "wA" if ht % 2 == 0 else "wB"

                    @blk.sync
                    def _(e, _ht=ht, _wb=wb, _ws=wsem, _wn=wnm):
                        if _ht >= 2:
                            e.wait_ge(cW, cnt["cW"] - 1)
                        dma(e, _ws, _wn, _wb[:, 0:1024].rearrange(
                            "p (kc o) -> p kc o", kc=8, o=128),
                            w1v[:, :, _ht * 128:(_ht + 1) * 128])
                        dma(e, _ws, _wn, _wb[:, 1024:2048].rearrange(
                            "p (kc o) -> p kc o", kc=8, o=128),
                            w2v[:, :, _ht * 128:(_ht + 1) * 128])
                    wt = cnt[wnm]

                    @blk.tensor
                    def _(e, _ht=ht, _wb=wb, _wt=wt, _ws=wsem):
                        e.wait_ge(_ws, _wt)
                        if _ht >= 1:
                            e.wait_ge(pV, cnt["pV"])
                        for kc in range(8):
                            mm = e.matmul(psE[:, 0:TGT], _wb[:, kc * 128:(kc + 1) * 128],
                                     x1nT[:, kc * TGT:(kc + 1) * TGT],
                                     start=(kc == 0), stop=(kc == 7))
                        tinc(mm, pP, "pP")
                        for kc in range(8):
                            mm = e.matmul(psF[:, 0:TGT],
                                     _wb[:, 1024 + kc * 128:1024 + (kc + 1) * 128],
                                     x1nT[:, kc * TGT:(kc + 1) * TGT],
                                     start=(kc == 0), stop=(kc == 7))
                        tinc(mm, pP, "pP")

                    @blk.scalar
                    def _(e, _ht=ht):
                        e.wait_ge(pP, cnt["pP"])
                        inc(e, cW, "cW")
                        e.activation(f3m[:, 0:512], psE[:, 0:TGT], AF.Silu,
                                     scale=s1sb[:, _ht:_ht + 1])
                        inc(e, pS, "pS")

                    @blk.vector
                    def _(e, _ht=ht):
                        e.wait_ge(pS, cnt["pS"])
                        e.tensor_scalar(f3m[:, 512:1024], psF[:, 0:TGT],
                                        s2sb[:, _ht:_ht + 1], None, ALU.mult, ALU.bypass)
                        e.drain()
                        e.tensor_tensor(HID[:, _ht * TGT:(_ht + 1) * TGT],
                                        f3m[:, 0:512], f3m[:, 512:1024], ALU.mult)
                        inc(e, pV, "pV")
                bar(["vector"])

                if PHASES == "C6a" and g == 0:
                    finish_dump(HID, 4096)
                    return nc
                for stagei, (wv, scl, srcbuf, dstbuf) in enumerate(
                        ((h1v, sh1sb, HID, HID2), (h2v, sh2sb, HID2, HID))):
                    for ot in range(32):
                        wb = whA if ot % 2 == 0 else whB

                        wsem = wA if ot % 2 == 0 else wB
                        wnm = "wA" if ot % 2 == 0 else "wB"

                        @blk.sync
                        def _(e, _ot=ot, _wb=wb, _wv=wv, _ws=wsem, _wn=wnm):
                            if _ot >= 2 or stagei > 0 or g > 0:
                                e.wait_ge(cW, cnt["cW"] - 1)
                            dma(e, _ws, _wn, _wb[:].rearrange(
                                "p (kc o) -> p kc o", kc=32, o=128),
                                _wv[:, :, _ot * 128:(_ot + 1) * 128])
                        wt = cnt[wnm]

                        pbk = psE if ot % 2 == 0 else psF

                        @blk.tensor
                        def _(e, _ot=ot, _wb=wb, _wt=wt, _src=srcbuf, _pb=pbk,
                              _ws=wsem):
                            e.wait_ge(_ws, _wt)
                            if _ot >= 2:
                                e.wait_ge(pS, cnt["pS"] - 1)
                            for kc in range(32):
                                mm = e.matmul(_pb[:, 0:TGT], _wb[:, kc * 128:(kc + 1) * 128],
                                         _src[:, kc * TGT:(kc + 1) * TGT],
                                         start=(kc == 0), stop=(kc == 31))
                            tinc(mm, pP, "pP")

                        @blk.scalar
                        def _(e, _ot=ot, _dst=dstbuf, _scl=scl, _pb=pbk):
                            e.wait_ge(pP, cnt["pP"])
                            inc(e, cW, "cW")
                            e.activation(_dst[:, _ot * TGT:(_ot + 1) * TGT],
                                         _pb[:, 0:TGT], AF.Gelu,
                                         scale=_scl[:, _ot:_ot + 1])
                            inc(e, pS, "pS")
                    bar(["scalar"])

                if PHASES == "C6b" and g == 0:
                    finish_dump(HID, 4096)
                    return nc
                for dc in range(8):
                    wb = whA if dc % 2 == 0 else whB

                    wsem = wA if dc % 2 == 0 else wB
                    wnm = "wA" if dc % 2 == 0 else "wB"

                    @blk.sync
                    def _(e, _dc=dc, _wb=wb, _ws=wsem, _wn=wnm):
                        e.wait_ge(cW, cnt["cW"] - 1)
                        dma(e, _ws, _wn, _wb[:].rearrange(
                            "p (kc o) -> p kc o", kc=32, o=128),
                            w3v[:, :, _dc * 128:(_dc + 1) * 128])
                    wt = cnt[wnm]

                    pbk = psE if dc % 2 == 0 else psF

                    @blk.tensor
                    def _(e, _dc=dc, _wb=wb, _wt=wt, _pb=pbk, _ws=wsem):
                        e.wait_ge(_ws, _wt)
                        if _dc >= 2:
                            e.wait_ge(pS, cnt["pS"] - 1)
                        for kc in range(32):
                            mm = e.matmul(_pb[:, 0:TGT], _wb[:, kc * 128:(kc + 1) * 128],
                                     HID[:, kc * TGT:(kc + 1) * TGT],
                                     start=(kc == 0), stop=(kc == 31))
                        tinc(mm, pP, "pP")

                    @blk.vector
                    def _(e, _dc=dc, _pb=pbk):
                        e.wait_ge(pP, cnt["pP"])
                        inc(e, cW, "cW")
                        e.tensor_scalar(F3[:, _dc * TGT:(_dc + 1) * TGT], _pb[:, 0:TGT],
                                        s3sb[:, _dc:_dc + 1], None, ALU.mult, ALU.bypass)
                        inc(e, pS, "pS")
                bar(["vector"])

                if PHASES == "C6c" and g == 0:
                    finish_dump(F3, 4096)
                    return nc
                for tt in range(NT_TG):
                    for dc in range(8):
                        bk = banks[dc % 2]

                        @blk.tensor
                        def _(e, _tt=tt, _dc=dc, _bk=bk):
                            if _tt + _dc > 0 or g > 0:
                                e.wait_ge(pS, cnt["pS"] - 1)
                            mm = e.matmul(_bk[:, 0:128],
                                     F3[:, _dc * TGT + _tt * 128:
                                        _dc * TGT + (_tt + 1) * 128],
                                     ident[:], start=True, stop=True)
                            tinc(mm, pP, "pP")

                        @blk.vector
                        def _(e, _tt=tt, _dc=dc, _bk=bk):
                            e.wait_ge(pP, cnt["pP"])
                            e.tensor_copy(f3m[:, _tt * 1024 + _dc * 128:
                                       _tt * 1024 + (_dc + 1) * 128], _bk[:, 0:128])
                            inc(e, pS, "pS")
                bar(["vector"])
                for tt in range(NT_TG):
                    tia = g * NT_TG + tt
                    src = f3m[:, tt * 1024:(tt + 1) * 1024]
                    prev_out = out_t

                    def dst_fn(e, m_ap, r_ap, _src=src, _tia=tia, _po=prev_out):
                        if _po > 0:
                            e.wait_ge(qS, _po)
                        e.tensor_scalar(ftt[:], _src, m_ap, r_ap,
                                        ALU.subtract, ALU.mult)
                        e.drain()
                        e.tensor_tensor(outb[:], ftt[:],
                                        TB1[:, _tia * 1024:(_tia + 1) * 1024], ALU.add)
                    ln_pass(src, stat2, dst_fn)
                    gvv = cnt["pV"]

                    @blk.sync
                    def _(e, _tia=tia, _gv=gvv):
                        e.wait_ge(pV, _gv)
                        dma(e, qS, "qS", y_out.rearrange(
                            "b (t2 t1) h -> (b t2) t1 h", t1=128)[_tia], outb[:])
                    out_t = cnt["qS"]
            @blk.sync
            def _(e):
                e.wait_ge(qS, cnt["qS"])
            bar(["sync"])
    return nc


_tables = _host_tables()
_nc_cache = {}


def kernel(L0_attn_x, L0_attn_log_dt, L0_attn_A_real_log, L0_attn_A_imag,
           L0_attn_C_real, L0_attn_C_imag, L0_attn_D, L0_attn_W_out,
           L0_attn_b_out, L0_ffn_w1_w, L0_ffn_w1_s, L0_ffn_w2_w, L0_ffn_w2_s,
           L0_ffn_h1_w, L0_ffn_h1_s, L0_ffn_h2_w, L0_ffn_h2_s, L0_ffn_w3_w,
           L0_ffn_w3_s):
    f32 = np.float32
    c = np.ascontiguousarray
    common = dict(
        log_dt=c(np.asarray(L0_attn_log_dt, f32).reshape(1, H)),
        arlT=c(np.asarray(L0_attn_A_real_log, f32).T),
        aimT=c(np.asarray(L0_attn_A_imag, f32).T),
        creT=c(np.asarray(L0_attn_C_real, f32).T),
        cimT=c(np.asarray(L0_attn_C_imag, f32).T),
        d_vec=c(np.asarray(L0_attn_D, f32).reshape(1, H)),
        wout=c(np.asarray(L0_attn_W_out, f32)),
        bout=c(np.asarray(L0_attn_b_out, f32).reshape(1, 2 * H)),
        w1T=c(np.asarray(L0_ffn_w1_w, f32).T),
        w2T=c(np.asarray(L0_ffn_w2_w, f32).T),
        h1T=c(np.asarray(L0_ffn_h1_w, f32).T),
        h2T=c(np.asarray(L0_ffn_h2_w, f32).T),
        w3T=c(np.asarray(L0_ffn_w3_w, f32).T),
        s1=c(np.asarray(L0_ffn_w1_s, f32).reshape(1, DH)),
        s2=c(np.asarray(L0_ffn_w2_s, f32).reshape(1, DH)),
        sh1=c(np.asarray(L0_ffn_h1_s, f32).reshape(1, DH)),
        sh2=c(np.asarray(L0_ffn_h2_s, f32).reshape(1, DH)),
        s3=c(np.asarray(L0_ffn_w3_s, f32).reshape(1, H)),
        **_tables,
    )
    x = np.asarray(L0_attn_x, f32)
    in_maps = []
    for core in range(8):
        m = dict(common)
        m["x"] = c(x[2 * core:2 * core + 2])
        in_maps.append(m)
    if "nc" not in _nc_cache:
        _nc_cache["nc"] = build_nc()
    res = run_bass_kernel_spmd(_nc_cache["nc"], in_maps, core_ids=list(range(8)))
    if DEBUG:
        _dbg.clear()
        _dbg.update({k: v for k, v in res.results[0].items() if k.startswith("dbg")})
    out = np.concatenate([res.results[i]["y"] for i in range(8)], axis=0)
    return out


# version 07
# speedup vs baseline: 1.5437x; 1.0154x over previous
import contextlib
import numpy as np
import concourse.bass as bass
import concourse.mybir as mybir
from concourse.bass_utils import run_bass_kernel_spmd

F32 = mybir.dt.float32
F16 = mybir.dt.float16
AF = mybir.ActivationFunctionType
ALU = mybir.AluOpType

L = 2048
H = 1024
NST = 64
NF = 4096
T2 = 16
M2 = 32
DH = 4096

CH_SL = 64
N_SL = H // CH_SL
KSUB = 16
TGT = 512
N_TG = 2 * L // TGT

MAGIC = float(1.5 * 2 ** 23)
INV2PI = float(1.0 / (2 * np.pi))
CW1 = 6.28125
CW2 = 0.0019353628158569336
CW3 = float(2 * np.pi - CW1 - CW2)
HALFPI = float(np.pi / 2)

DEBUG = False
PHASES = "ALL"
_dbg = {}


def _host_tables():
    f16 = np.float16
    W32 = np.exp(-2j * np.pi / 32)
    B1 = np.zeros((128, 256), np.float32)
    for s in range(4):
        for t2 in range(T2):
            for m2 in range(M2):
                w = W32 ** (t2 * m2)
                B1[s * 32 + t2 * 2 + 0, s * 64 + m2 * 2 + 0] = w.real
                B1[s * 32 + t2 * 2 + 0, s * 64 + m2 * 2 + 1] = w.imag
                B1[s * 32 + t2 * 2 + 1, s * 64 + m2 * 2 + 0] = -w.imag
                B1[s * 32 + t2 * 2 + 1, s * 64 + m2 * 2 + 1] = w.real
    B1R = np.zeros((128, 512), np.float32)
    for s in range(8):
        for t2 in range(T2):
            for m2 in range(M2):
                w = W32 ** (t2 * m2)
                B1R[s * 16 + t2, s * 64 + m2 * 2 + 0] = w.real
                B1R[s * 16 + t2, s * 64 + m2 * 2 + 1] = w.imag
    Wn = np.exp(-2j * np.pi / NF)
    W128 = np.exp(-2j * np.pi / 128)
    t1 = np.arange(128)[:, None]
    m1 = np.arange(128)[None, :]
    Om = np.zeros((M2, 128, 128), complex)
    for m2 in range(M2):
        Om[m2] = (Wn ** (t1 * m2)) * (W128 ** (t1 * m1))
    V = np.conj(Wn)
    V128 = np.exp(+2j * np.pi / 128)
    Ph = np.zeros((M2, 128, 128), complex)
    for m2 in range(M2):
        Ph[m2] = (V128 ** (m1.T * t1.T)) * (V ** (t1.T * m2)) / NF
    V32 = np.exp(+2j * np.pi / 32)
    Th = np.zeros((2, 128, 128), np.float32)
    for hssub in range(4):
        for m2 in range(M2):
            k = hssub * 32 + m2
            for t2 in range(T2):
                v = V32 ** (m2 * t2)
                Th[0, k, hssub * 32 + t2 * 2 + 0] = v.real
                Th[0, k, hssub * 32 + t2 * 2 + 1] = v.imag
                Th[1, k, hssub * 32 + t2 * 2 + 0] = -v.imag
                Th[1, k, hssub * 32 + t2 * 2 + 1] = v.real
    stk = lambda a: np.ascontiguousarray(a.reshape(-1, a.shape[-1]).astype(f16))
    return dict(
        cB1=B1.astype(f16), cB1R=B1R.astype(f16),
        cOmRe=stk(Om.real), cOmIm=stk(Om.imag), cOmImN=stk(-Om.imag),
        cPhRe=stk(Ph.real), cPhIm=stk(Ph.imag), cPhImN=stk(-Ph.imag),
        cTh0=Th[0].astype(f16), cTh1=Th[1].astype(f16),
        cIdent=np.eye(128, dtype=f16),
    )


def build_nc():
    nc = bass.Bass("TRN2", target_bir_lowering=False)
    dram = nc.dram_tensor

    x_in = dram("x", [2, L, H], F32, kind="ExternalInput").ap()
    log_dt = dram("log_dt", [1, H], F32, kind="ExternalInput").ap()
    arlT = dram("arlT", [NST, H], F32, kind="ExternalInput").ap()
    aimT = dram("aimT", [NST, H], F32, kind="ExternalInput").ap()
    creT = dram("creT", [NST, H], F32, kind="ExternalInput").ap()
    cimT = dram("cimT", [NST, H], F32, kind="ExternalInput").ap()
    d_in = dram("d_vec", [1, H], F32, kind="ExternalInput").ap()
    wout = dram("wout", [H, 2 * H], F32, kind="ExternalInput").ap()
    bout = dram("bout", [1, 2 * H], F32, kind="ExternalInput").ap()
    w1T = dram("w1T", [H, DH], F32, kind="ExternalInput").ap()
    w2T = dram("w2T", [H, DH], F32, kind="ExternalInput").ap()
    h1T = dram("h1T", [DH, DH], F32, kind="ExternalInput").ap()
    h2T = dram("h2T", [DH, DH], F32, kind="ExternalInput").ap()
    w3T = dram("w3T", [DH, H], F32, kind="ExternalInput").ap()
    s1_in = dram("s1", [1, DH], F32, kind="ExternalInput").ap()
    s2_in = dram("s2", [1, DH], F32, kind="ExternalInput").ap()
    sh1_in = dram("sh1", [1, DH], F32, kind="ExternalInput").ap()
    sh2_in = dram("sh2", [1, DH], F32, kind="ExternalInput").ap()
    s3_in = dram("s3", [1, H], F32, kind="ExternalInput").ap()
    consts = {}
    for nm, shp in (("cB1", [128, 256]), ("cB1R", [128, 512]),
                    ("cOmRe", [M2 * 128, 128]), ("cOmIm", [M2 * 128, 128]),
                    ("cOmImN", [M2 * 128, 128]), ("cPhRe", [M2 * 128, 128]),
                    ("cPhIm", [M2 * 128, 128]), ("cPhImN", [M2 * 128, 128]),
                    ("cTh0", [128, 128]), ("cTh1", [128, 128]),
                    ("cIdent", [128, 128])):
        consts[nm] = dram(nm, shp, F16, kind="ExternalInput").ap()

    y_out = dram("y", [2, L, H], F32, kind="ExternalOutput").ap()

    xf16 = dram("xf16", [2 * L, H], F16, kind="Internal").ap()
    s_xnh = dram("s_xnh", [CH_SL, NF], F16, kind="Internal").ap()
    s_kern = dram("s_kern", [CH_SL, L], F16, kind="Internal").ap()
    wout16 = dram("wout16", [H, 2 * H], F16, kind="Internal").ap()
    w116 = dram("w116", [H, DH], F16, kind="Internal").ap()
    w216 = dram("w216", [H, DH], F16, kind="Internal").ap()
    h116 = dram("h116", [DH, DH], F16, kind="Internal").ap()
    h216 = dram("h216", [DH, DH], F16, kind="Internal").ap()
    w316 = dram("w316", [DH, H], F16, kind="Internal").ap()
    if DEBUG:
        dbg_xn = dram("dbg_xn", [128, 32 * 1024], F16, kind="ExternalOutput").ap()
        dbg_y2 = dram("dbg_y2", [128, 32 * 1024], F16, kind="ExternalOutput").ap()
        dbg_x1 = dram("dbg_x1", [128, 32 * 1024], F16, kind="ExternalOutput").ap()

    cnt = {}

    def tinc(instr, sem, name, n=1):
        instr.then_inc(sem, n)
        cnt[name] = cnt.get(name, 0) + n

    def inc(e, sem, name, n=1):
        last = getattr(e, "last", None)
        if last is not None:
            last.then_inc(sem, n)
            e.last = None
        else:
            try:
                en = str(e.engine)
            except Exception:
                en = ""
            if ("DVE" in en) or ("Activation" in en) or ("Pool" in en):
                e.drain()
            e.sem_inc(sem, n)
        cnt[name] = cnt.get(name, 0) + n

    st_main = contextlib.ExitStack()
    with st_main:
        TB1 = st_main.enter_context(nc.sbuf_tensor("TB1", [128, 32 * 1024], F16))
        ident = st_main.enter_context(nc.sbuf_tensor("ident", [128, 128], F16))
        b1sb = st_main.enter_context(nc.sbuf_tensor("b1sb", [128, 256], F16))
        b1rsb = st_main.enter_context(nc.sbuf_tensor("b1rsb", [128, 512], F16))
        th0sb = st_main.enter_context(nc.sbuf_tensor("th0sb", [128, 128], F16))
        th1sb = st_main.enter_context(nc.sbuf_tensor("th1sb", [128, 128], F16))
        dbc = st_main.enter_context(nc.sbuf_tensor("dbc", [128, H], F32))
        dtbc = st_main.enter_context(nc.sbuf_tensor("dtbc", [128, H], F32))
        prmA = st_main.enter_context(nc.sbuf_tensor("prmA", [128, H], F32))
        prmI = st_main.enter_context(nc.sbuf_tensor("prmI", [128, H], F32))
        prmCr = st_main.enter_context(nc.sbuf_tensor("prmCr", [128, H], F16))
        prmCi = st_main.enter_context(nc.sbuf_tensor("prmCi", [128, H], F16))
        bosb = st_main.enter_context(nc.sbuf_tensor("bosb", [128, 16], F32))
        s1sb = st_main.enter_context(nc.sbuf_tensor("s1sb", [128, 32], F32))
        s2sb = st_main.enter_context(nc.sbuf_tensor("s2sb", [128, 32], F32))
        sh1sb = st_main.enter_context(nc.sbuf_tensor("sh1sb", [128, 32], F32))
        sh2sb = st_main.enter_context(nc.sbuf_tensor("sh2sb", [128, 32], F32))
        s3sb = st_main.enter_context(nc.sbuf_tensor("s3sb", [128, 8], F32))
        row1 = st_main.enter_context(nc.sbuf_tensor("row1", [1, 2 * H], F32))
        epsc = st_main.enter_context(nc.sbuf_tensor("epsc", [128, 1], F32))
        ones1 = st_main.enter_context(nc.sbuf_tensor("ones1", [1, 128], F32))
        negone = st_main.enter_context(nc.sbuf_tensor("negone", [128, 64], F32))
        nhalf = st_main.enter_context(nc.sbuf_tensor("nhalf", [128, 1], F32))
        psA = st_main.enter_context(nc.psum_tensor("psA", [128, 512], F32))
        psB = st_main.enter_context(nc.psum_tensor("psB", [128, 512], F32))
        psC = st_main.enter_context(nc.psum_tensor("psC", [128, 512], F32))
        psD = st_main.enter_context(nc.psum_tensor("psD", [128, 512], F32))
        psE = st_main.enter_context(nc.psum_tensor("psE", [128, 512], F32))
        psF = st_main.enter_context(nc.psum_tensor("psF", [128, 512], F32))
        psG = st_main.enter_context(nc.psum_tensor("psG", [128, 512], F32))
        psH = st_main.enter_context(nc.psum_tensor("psH", [128, 512], F32))
        qS = st_main.enter_context(nc.semaphore("qS"))
        qG = st_main.enter_context(nc.semaphore("qG"))
        G = st_main.enter_context(nc.semaphore("G"))
        pP = st_main.enter_context(nc.semaphore("pP"))
        pV = st_main.enter_context(nc.semaphore("pV"))
        pS = st_main.enter_context(nc.semaphore("pS"))
        pG2 = st_main.enter_context(nc.semaphore("pG2"))
        wA = st_main.enter_context(nc.semaphore("wA"))
        wB = st_main.enter_context(nc.semaphore("wB"))
        cW = st_main.enter_context(nc.semaphore("cW"))
        xA = st_main.enter_context(nc.semaphore("xA"))
        xB = st_main.enter_context(nc.semaphore("xB"))
        blk = st_main.enter_context(nc.Block())
        class _Rec:
            def __init__(self, e):
                self._e = e
                self.last = None
            def __getattr__(self, name):
                f = getattr(self._e, name)
                if name in ("wait_ge", "sem_inc", "drain", "engine", "dma_start"):
                    return f
                if not callable(f):
                    return f
                def g(*a, **k):
                    r = f(*a, **k)
                    self.last = r
                    return r
                return g

        class _BlkWrap:
            def __init__(self, b):
                self._b = b
            def __getattr__(self, engname):
                dec = getattr(self._b, engname)
                if engname.startswith("_"):
                    return dec
                def mydec(fn):
                    def fn2(e):
                        if engname in ("vector", "scalar", "gpsimd"):
                            return fn(_Rec(e))
                        return fn(e)
                    return dec(fn2)
                return mydec

        blk = _BlkWrap(blk)

        banks = [psA, psB, psC, psD, psE, psF, psG, psH]

        def dma(e, sem, name, out, in_, **kw):
            e.dma_start(out=out, in_=in_, **kw).then_inc(sem, 16)
            cnt[name] = cnt.get(name, 0) + 16
            return cnt[name]

        def bar(engines_inc):
            for ename in engines_inc:
                def f(e):
                    inc(e, G, "G")
                getattr(blk, ename)(f)
            tgt = cnt.get("G", 0)
            for ename in ("sync", "gpsimd", "tensor", "vector", "scalar"):
                def f(e, _t=tgt):
                    e.wait_ge(G, _t)
                getattr(blk, ename)(f)

        def cmb(e, dre, dim_, are, aim, bre, bim, tmp):
            e.tensor_tensor(tmp, aim, bim, ALU.mult)
            e.tensor_tensor(dre, are, bre, ALU.mult)
            e.drain()
            e.tensor_tensor(dre, dre, tmp, ALU.subtract)
            e.tensor_tensor(tmp, aim, bre, ALU.mult)
            e.tensor_tensor(dim_, are, bim, ALU.mult)
            e.drain()
            e.tensor_tensor(dim_, dim_, tmp, ALU.add)

        wcast_tgt = {}

        @blk.gpsimd
        def _(e):
            for dst, srcp in ((prmCr, creT), (prmCi, cimT)):
                dma(e, qG, "qG", dst.ap()[0:64], srcp)
                dma(e, qG, "qG", dst.ap()[64:128], srcp)
            e.wait_ge(qG, cnt["qG"])
            inc(e, pG2, "pG2")
            cnt["prm_ready"] = cnt["pG2"]
            dma(e, qG, "qG", xf16, x_in.rearrange("b t h -> (b t) h"), max_dma_last_dim=4096)
            e.wait_ge(qG, cnt["qG"])
            inc(e, pG2, "pG2")
            cnt["xf16_ready"] = cnt["pG2"]
            pass

        @blk.sync
        def _(e):
            dma(e, qS, "qS", ident[:], consts["cIdent"])
            dma(e, qS, "qS", b1sb[:], consts["cB1"])
            dma(e, qS, "qS", b1rsb[:], consts["cB1R"])
            dma(e, qS, "qS", th0sb[:], consts["cTh0"])
            dma(e, qS, "qS", th1sb[:], consts["cTh1"])
            for dst, src in ((prmA, arlT), (prmI, aimT)):
                dma(e, qS, "qS", dst.ap()[0:64], src)
                dma(e, qS, "qS", dst.ap()[64:128], src)
            e.wait_ge(pG2, cnt["prm_ready"])
            dma(e, qS, "qS", row1[:1, 0:H], log_dt)
            with nc.allow_non_contiguous_dma(reason="tiny one-time scale loads"):
                for dst, src, n in ((s1sb, s1_in, 32), (s2sb, s2_in, 32),
                                    (sh1sb, sh1_in, 32), (sh2sb, sh2_in, 32),
                                    (s3sb, s3_in, 8), (bosb, bout, 16)):
                    dma(e, qS, "qS", dst.ap().rearrange("p (c o) -> p c o", c=n, o=1),
                        src.rearrange("o (c p) -> p c o", c=n, p=128))
            e.wait_ge(qS, cnt["qS"])
            inc(e, G, "G")
        g_a = cnt["G"]

        @blk.vector
        def _(e):
            e.memset(epsc[:], 1e-5)
            e.memset(ones1[:1, :], 1.0)
            e.memset(negone[:], -1.0)
            e.memset(nhalf[:], -0.5)
            e.drain()

        @blk.scalar
        def _(e):
            e.wait_ge(G, g_a)
            e.activation(row1[:1, 0:H], row1[:1, 0:H], AF.Exp)
            e.activation(prmA[:], prmA[:], AF.Exp)
            inc(e, pS, "pS")

        @blk.vector
        def _(e):
            e.wait_ge(pS, cnt["pS"])
            e.tensor_scalar(prmA[:], prmA[:], -1.0, None, ALU.mult, ALU.bypass)
            inc(e, pV, "pV")

        @blk.sync
        def _(e):
            e.wait_ge(pS, cnt["pS"])
            dma(e, qS, "qS", dtbc.ap()[0:1, 0:H], row1[:1, 0:H])
            e.wait_ge(qS, cnt["qS"])
            w = 1
            while w < 128:
                dma(e, qS, "qS", dtbc.ap()[w:2 * w, 0:H], dtbc.ap()[0:w, 0:H])
                e.wait_ge(qS, cnt["qS"])
                w *= 2
            dma(e, qS, "qS", row1[:1, 0:H], d_in)
            e.wait_ge(qS, cnt["qS"])
            dma(e, qS, "qS", dbc.ap()[0:1, 0:H], row1[:1, 0:H])
            e.wait_ge(qS, cnt["qS"])
            w = 1
            while w < 128:
                dma(e, qS, "qS", dbc.ap()[w:2 * w, 0:H], dbc.ap()[0:w, 0:H])
                e.wait_ge(qS, cnt["qS"])
                w *= 2
            inc(e, G, "G")
        bar(["sync"])

        with contextlib.ExitStack() as st_a:
            xa = st_a.enter_context(nc.sbuf_tensor("xa", [128, H], F16))
            xb = st_a.enter_context(nc.sbuf_tensor("xb", [128, H], F16))
            bn6 = st_a.enter_context(nc.sbuf_tensor("bn6", [128, 12], F32))
            bn2 = st_a.enter_context(nc.sbuf_tensor("bn2", [128, 2], F32))
            mrr = st_a.enter_context(nc.sbuf_tensor("mrr", [128, 2], F32))
            xv = x_in.rearrange("b (t2 t1) h -> (b t2) t1 h", t1=128)
            for ti in range(32):
                buf = xa if ti % 2 == 0 else xb

                xsem = xA if ti % 2 == 0 else xB
                xnm = "xA" if ti % 2 == 0 else "xB"

                @blk.gpsimd
                def _(e, _ti=ti, _buf=buf, _xs=xsem, _xn=xnm):
                    if _ti >= 2:
                        e.wait_ge(pV, cnt.get("pV", 0) - 1)
                    dma(e, _xs, _xn, _buf[:], xv[_ti])
                qg_t = cnt[xnm]

                @blk.vector
                def _(e, _buf=buf, _qt=qg_t, _xs=xsem):
                    e.wait_ge(_xs, _qt)
                    e.bn_stats(bn6[:, 0:6], _buf[:, 0:512])
                    e.bn_stats(bn6[:, 6:12], _buf[:, 512:1024])
                    e.drain()
                    e.bn_aggr(bn2[:], bn6[:])
                    inc(e, pV, "pV")

                @blk.gpsimd
                def _(e):
                    e.wait_ge(pV, cnt["pV"])
                    e.tensor_scalar(mrr[:, 1:2], bn2[:, 1:2], 1e-5, None, ALU.add, ALU.bypass)
                    e.drain()
                    e.tensor_tensor(mrr[:, 0:1], mrr[:, 1:2], nhalf[:, 0:1], ALU.pow)
                    inc(e, pG2, "pG2")

                @blk.vector
                def _(e, _ti=ti, _buf=buf):
                    e.wait_ge(pG2, cnt["pG2"])
                    e.tensor_scalar(TB1[:, _ti * 1024:(_ti + 1) * 1024], _buf[:],
                                    bn2[:, 0:1], mrr[:, 0:1], ALU.subtract, ALU.mult)
                    inc(e, pV, "pV")
            bar(["vector"])

        @blk.gpsimd
        def _(e):
            wlist = (("wout", wout, wout16), ("w1", w1T, w116),
                     ("w2", w2T, w216), ("w3", w3T, w316),
                     ("h1", h1T, h116), ("h2", h2T, h216))
            if PHASES == "Cpre":
                wlist = wlist[:1]
            elif PHASES in ("C6a", "C6b", "C6c", "C45"):
                pass
            elif PHASES != "ALL":
                wlist = ()
            for nm, srcw, dst in wlist:
                dma(e, qG, "qG", dst, srcw, max_dma_last_dim=8192)
            cnt["wcast_qg"] = cnt["qG"]
        if DEBUG:
            @blk.sync
            def _(e):
                dma(e, qS, "qS", dbg_xn, TB1[:])
                e.wait_ge(qS, cnt["qS"])
            bar(["sync"])

        def finish_dump(extra=None, nc_extra=4096):
            @blk.gpsimd
            def _(e):
                e.wait_ge(G, cnt.get("G", 0))
                e.wait_ge(pV, cnt.get("pV", 0))
                e.wait_ge(pS, cnt.get("pS", 0))
                dma(e, qG, "qG",
                    y_out.rearrange("b (t2 t1) h -> t1 b t2 h", t1=128),
                    TB1.ap().rearrange("p (b t2 h) -> p b t2 h", b=2, t2=16, h=1024),
                    max_dma_last_dim=4096)
                if extra is not None:
                    e.wait_ge(qG, cnt["qG"])
                    yv = y_out.rearrange("b (tq p t1q) h -> b tq p (t1q h)",
                                         p=128, t1q=1)
                    for j in range(nc_extra // 1024):
                        dma(e, qG, "qG", yv[1, j],
                            extra.ap()[:, j * 1024:(j + 1) * 1024])
                e.wait_ge(qG, cnt["qG"])

            @blk.sync
            def _(e):
                e.wait_ge(qS, cnt["qS"])

        if PHASES == "A":
            finish_dump()
            return nc

        NHG = CH_SL // 4
        NHGP = CH_SL // 8
        NHSG = CH_SL // 4
        NPL = M2 * CH_SL

        with contextlib.ExitStack() as st_b:
            constb = st_b.enter_context(nc.sbuf_tensor("constb", [128, M2 * 3 * 128], F16))
            xnh = st_b.enter_context(nc.sbuf_tensor("xnh", [128, NF], F16))
            zst = st_b.enter_context(nc.sbuf_tensor("zst", [128, NHG * 128], F16))
            Ssl = st_b.enter_context(nc.sbuf_tensor("Ssl", [128, 2 * NPL], F16))
            Zsl = st_b.enter_context(nc.sbuf_tensor("Zsl", [128, 2 * NPL], F16))
            KFsl = st_b.enter_context(nc.sbuf_tensor("KFsl", [128, 2 * NPL], F16))
            WmT = st_b.enter_context(nc.sbuf_tensor("WmT", [128, 2 * NPL], F16))
            kstr = st_b.enter_context(nc.sbuf_tensor("kstr", [128, NHGP * 128], F16))
            qre = st_b.enter_context(nc.sbuf_tensor("qre", [128, KSUB * 128], F32))
            qim = st_b.enter_context(nc.sbuf_tensor("qim", [128, KSUB * 128], F32))
            pre = st_b.enter_context(nc.sbuf_tensor("pre", [128, KSUB * 16], F32))
            pim = st_b.enter_context(nc.sbuf_tensor("pim", [128, KSUB * 16], F32))
            pcmb = st_b.enter_context(nc.sbuf_tensor("pcmb", [128, KSUB * 16], F16))
            qcmb = st_b.enter_context(nc.sbuf_tensor("qcmb", [128, KSUB * 128], F16))
            lre = st_b.enter_context(nc.sbuf_tensor("lre", [128, KSUB], F32))
            lim = st_b.enter_context(nc.sbuf_tensor("lim", [128, KSUB], F32))
            pwr = st_b.enter_context(nc.sbuf_tensor("pwr", [128, KSUB], F32))
            pwi = st_b.enter_context(nc.sbuf_tensor("pwi", [128, KSUB], F32))
            t0 = st_b.enter_context(nc.sbuf_tensor("t0", [128, KSUB], F32))
            t1b = st_b.enter_context(nc.sbuf_tensor("t1b", [128, KSUB], F32))
            t2b = st_b.enter_context(nc.sbuf_tensor("t2b", [128, KSUB], F32))
            t3b = st_b.enter_context(nc.sbuf_tensor("t3b", [128, KSUB], F32))
            ctre = st_b.enter_context(nc.sbuf_tensor("ctre", [128, KSUB], F32))
            ctim = st_b.enter_context(nc.sbuf_tensor("ctim", [128, KSUB], F32))
            big32 = st_b.enter_context(nc.sbuf_tensor("big32", [128, 1024], F32))
            ust0 = st_b.enter_context(nc.sbuf_tensor("ust0", [128, 128], F16))
            ust1 = st_b.enter_context(nc.sbuf_tensor("ust1", [128, 128], F16))
            gtmp = st_b.enter_context(nc.sbuf_tensor("gtmp", [128, 512], F32))
            xnds = st_b.enter_context(nc.sbuf_tensor("xnds", [128, 32 * CH_SL], F16))
            def om(pl, m2):
                return constb[:, (pl * M2 + m2) * 128:(pl * M2 + m2) * 128 + 128]

            def load_tables(pre_fix):
                @blk.sync
                def _(e):
                    for i, nm in enumerate(("Re", "Im", "ImN")):
                        dma(e, qS, "qS",
                            constb[:, i * M2 * 128:(i + 1) * M2 * 128].rearrange(
                                "p (m k) -> p m k", m=M2, k=128),
                            consts["c" + pre_fix + nm].rearrange(
                                "(m p) k -> p m k", m=M2))
                    e.wait_ge(qS, cnt["qS"])
                    inc(e, G, "G")
                return cnt["G"]

            def stage2(src, dstbuf, tgt_qs):
                @blk.tensor
                def _(e, _t=tgt_qs):
                    e.wait_ge(G, _t)
                for m2 in range(M2):
                    bkr = banks[4 + (m2 % 2) * 2]
                    bki = banks[5 + (m2 % 2) * 2]

                    @blk.tensor
                    def _(e, _m2=m2, _r=bkr, _i=bki):
                        if _m2 >= 2:
                            e.wait_ge(pS, cnt["pS"] - 1)
                        sre = src[:, _m2 * CH_SL:(_m2 + 1) * CH_SL]
                        sim = src[:, NPL + _m2 * CH_SL:NPL + (_m2 + 1) * CH_SL]
                        e.matmul(_r[:, 0:CH_SL], om(0, _m2), sre, start=True, stop=False)
                        mm1 = e.matmul(_r[:, 0:CH_SL], om(2, _m2), sim, start=False, stop=True)
                        tinc(mm1, pP, "pP")
                        e.matmul(_i[:, 0:CH_SL], om(1, _m2), sre, start=True, stop=False)
                        mm2 = e.matmul(_i[:, 0:CH_SL], om(0, _m2), sim, start=False, stop=True)
                        tinc(mm2, pP, "pP")

                    @blk.vector
                    def _(e, _m2=m2, _r=bkr, _i=bki):
                        e.wait_ge(pP, cnt["pP"])
                        e.tensor_copy(dstbuf[:, _m2 * CH_SL:(_m2 + 1) * CH_SL], _r[:, 0:CH_SL])
                        e.tensor_copy(dstbuf[:, NPL + _m2 * CH_SL:NPL + (_m2 + 1) * CH_SL],
                               _i[:, 0:CH_SL])
                        inc(e, pS, "pS")
                bar(["vector"])

            for sl in range(1 if PHASES == "B1" else N_SL):
                h0 = sl * CH_SL
                om_ld = load_tables("Om")

                for ti in range(32):
                    bk = banks[ti % 2]

                    @blk.tensor
                    def _(e, _ti=ti, _bk=bk):
                        if _ti >= 2 or sl > 0:
                            e.wait_ge(pS, cnt["pS"] - 1)
                        mm = e.matmul(_bk[0:CH_SL, 0:128],
                                 TB1[:, _ti * 1024 + h0:_ti * 1024 + h0 + CH_SL],
                                 ident[:], start=True, stop=True)
                        tinc(mm, pP, "pP")

                    @blk.vector
                    def _(e, _ti=ti, _bk=bk):
                        e.wait_ge(pP, cnt["pP"])
                        e.tensor_copy(xnh[0:CH_SL, _ti * 128:(_ti + 1) * 128],
                               _bk[0:CH_SL, 0:128])
                        inc(e, pS, "pS")
                bar(["vector"])

                if PHASES == "B0":
                    finish_dump(xnh)
                    return nc
                @blk.sync
                def _(e):
                    dma(e, qS, "qS", s_xnh, xnh[0:CH_SL, :])
                    e.wait_ge(qS, cnt["qS"])
                    srcv = s_xnh.rearrange("(hg s) (c t2 t1) -> s c t2 hg t1",
                                           hg=NHG, s=4, c=2, t2=16, t1=128)
                    for s in range(4):
                        for c in range(2):
                            dstp = zst.ap()[s * 32 + c:s * 32 + c + 31:2]
                            dma(e, qS, "qS",
                                dstp.rearrange("t2 (hg t1) -> t2 hg t1",
                                               hg=NHG, t1=128),
                                srcv[s, c])
                    e.wait_ge(qS, cnt["qS"])
                    inc(e, G, "G")
                g0 = cnt["G"]

                if PHASES == "B1s":
                    finish_dump(zst, 2048)
                    return nc
                for hg in range(NHG):
                    bk = banks[2 + hg % 2]

                    @blk.tensor
                    def _(e, _hg=hg, _bk=bk, _g0=g0):
                        if _hg == 0:
                            e.wait_ge(G, _g0)
                        if _hg >= 2:
                            e.wait_ge(pV, cnt["pV"] - 1)
                        mm = e.matmul(_bk[:, 0:256], zst[:, _hg * 128:(_hg + 1) * 128],
                                 b1sb[:], start=True, stop=True)
                        tinc(mm, pP, "pP")

                    @blk.vector
                    def _(e, _hg=hg, _bk=bk):
                        e.wait_ge(pP, cnt["pP"])
                        dstv = Ssl.ap().rearrange(
                            "p (cp m2 hg s) -> p hg s m2 cp",
                            cp=2, m2=M2, hg=NHG, s=4)[:, _hg]
                        e.tensor_copy(dstv, _bk[:, 0:256].rearrange(
                            "p (s m2 cp) -> p s m2 cp", s=4, m2=M2, cp=2))
                        inc(e, pV, "pV")
                bar(["vector"])

                if PHASES == "B2":
                    finish_dump(Ssl)
                    return nc
                stage2(Ssl, Zsl, om_ld)

                if PHASES == "B3":
                    finish_dump(Zsl)
                    return nc
                for sub in range(CH_SL // KSUB):
                    cs = slice(h0 + sub * KSUB, h0 + sub * KSUB + KSUB)

                    @blk.vector
                    def _(e):
                        e.tensor_copy(t0[:], prmA[:, cs])
                        e.drain()
                        e.tensor_tensor(t1b[:], t0[:], dtbc[:, cs], ALU.mult)
                        e.tensor_tensor(t2b[:], prmI[:, cs], dtbc[:, cs], ALU.mult)
                        e.drain()
                        e.tensor_scalar(t3b[:], t2b[:], INV2PI, None, ALU.mult, ALU.bypass)
                        e.drain()
                        e.tensor_scalar(t3b[:], t3b[:], MAGIC, MAGIC, ALU.add, ALU.subtract)
                        e.drain()
                        e.scalar_tensor_tensor(pwr[:], t3b[:], -CW1, t2b[:], ALU.mult, ALU.add)
                        e.drain()
                        e.scalar_tensor_tensor(pwr[:], t3b[:], -CW2, pwr[:], ALU.mult, ALU.add)
                        e.drain()
                        e.scalar_tensor_tensor(pwr[:], t3b[:], -CW3, pwr[:], ALU.mult, ALU.add)
                        e.tensor_scalar(t2b[:], t2b[:], HALFPI, None, ALU.add, ALU.bypass)
                        e.drain()
                        e.tensor_scalar(t3b[:], t2b[:], INV2PI, None, ALU.mult, ALU.bypass)
                        e.drain()
                        e.tensor_scalar(t3b[:], t3b[:], MAGIC, MAGIC, ALU.add, ALU.subtract)
                        e.drain()
                        e.scalar_tensor_tensor(pwi[:], t3b[:], -CW1, t2b[:], ALU.mult, ALU.add)
                        e.drain()
                        e.scalar_tensor_tensor(pwi[:], t3b[:], -CW2, pwi[:], ALU.mult, ALU.add)
                        e.drain()
                        e.scalar_tensor_tensor(pwi[:], t3b[:], -CW3, pwi[:], ALU.mult, ALU.add)
                        inc(e, pV, "pV")

                    @blk.scalar
                    def _(e):
                        e.wait_ge(pV, cnt["pV"])
                        e.activation(t1b[:], t1b[:], AF.Exp)
                        e.activation(pwr[:], pwr[:], AF.Sin)
                        e.activation(pwi[:], pwi[:], AF.Sin)
                        inc(e, pS, "pS")

                    @blk.vector
                    def _(e):
                        e.wait_ge(pS, cnt["pS"])
                        e.tensor_tensor(lre[:], t1b[:], pwi[:], ALU.mult)
                        e.tensor_tensor(lim[:], t1b[:], pwr[:], ALU.mult)
                        inc(e, pV, "pV")
                    if PHASES == "B4l" and sub == 0:
                        @blk.vector
                        def _(e):
                            kd = xnh
                            e.tensor_copy(kd[:, 0:16], lre[:])
                            e.tensor_copy(kd[:, 16:32], lim[:])
                            e.tensor_copy(kd[:, 32:48], t1b[:])
                            e.tensor_copy(kd[:, 48:64], pwr[:])
                            e.tensor_copy(kd[:, 64:80], pwi[:])
                            e.tensor_copy(kd[:, 80:96], dtbc[:, cs])
                            e.tensor_copy(kd[:, 96:112], prmI[:, cs])
                            e.tensor_copy(kd[:, 112:128], prmA[:, cs])
                            inc(e, pV, "pV")
                        bar(["vector"])
                        finish_dump(xnh, 1024)
                        return nc

                    @blk.scalar
                    def _(e):
                        e.wait_ge(pV, cnt["pV"])
                        e.activation(t1b[:], t0[:], AF.Square)
                        e.activation(t2b[:], prmI[:, cs], AF.Square)
                        inc(e, pS, "pS")

                    @blk.vector
                    def _(e):
                        e.wait_ge(pS, cnt["pS"])
                        e.tensor_tensor(t1b[:], t1b[:], t2b[:], ALU.add)
                        inc(e, pV, "pV")

                    @blk.gpsimd
                    def _(e):
                        e.wait_ge(pV, cnt["pV"])
                        e.tensor_tensor(t1b[:], t1b[:], negone[:, 0:KSUB], ALU.pow)
                        inc(e, pG2, "pG2")

                    @blk.vector
                    def _(e):
                        e.wait_ge(pG2, cnt["pG2"])
                        e.tensor_tensor(t2b[:], t0[:], t1b[:], ALU.mult)
                        e.scalar_tensor_tensor(t3b[:], prmI[:, cs], -1.0, t1b[:],
                                               ALU.mult, ALU.mult)
                        e.drain()
                        e.tensor_scalar(t0[:], lre[:], -1.0, None, ALU.add, ALU.bypass)
                        e.drain()
                        cmb(e, ctre[:], ctim[:], t0[:], lim[:], t2b[:], t3b[:], t1b[:])
                        e.drain()
                        cmb(e, t2b[:], t3b[:], ctre[:], ctim[:],
                            prmCr[:, cs], prmCi[:, cs], t1b[:])
                        e.drain()
                        e.tensor_copy(ctre[:], t2b[:])
                        e.tensor_copy(ctim[:], t3b[:])
                        e.tensor_copy(pwr[:], lre[:])
                        e.tensor_copy(pwi[:], lim[:])
                        e.drain()
                        for _ in range(7):
                            e.tensor_tensor(t0[:], pwr[:], pwr[:], ALU.mult)
                            e.tensor_tensor(t1b[:], pwi[:], pwi[:], ALU.mult)
                            e.tensor_tensor(t2b[:], pwr[:], pwi[:], ALU.mult)
                            e.drain()
                            e.tensor_tensor(pwr[:], t0[:], t1b[:], ALU.subtract)
                            e.tensor_scalar(pwi[:], t2b[:], 2.0, None, ALU.mult, ALU.bypass)
                            e.drain()
                        pr = pre.ap().rearrange("p (ch l) -> p l ch", ch=KSUB, l=16)
                        pi = pim.ap().rearrange("p (ch l) -> p l ch", ch=KSUB, l=16)
                        e.tensor_copy(pr[:, 0], ctre[:])
                        e.tensor_copy(pi[:, 0], ctim[:])
                        inc(e, pV, "pV")

                    for kstep in range(4):
                        w = 2 ** kstep

                        @blk.vector
                        def _(e, _w=w):
                            pr = pre.ap().rearrange("p (ch l) -> p l ch", ch=KSUB, l=16)
                            pi = pim.ap().rearrange("p (ch l) -> p l ch", ch=KSUB, l=16)
                            are, aim = pr[:, 0:_w], pi[:, 0:_w]
                            dre, dim_ = pr[:, _w:2 * _w], pi[:, _w:2 * _w]
                            bre = pwr.ap().rearrange("p (u ch) -> p u ch", u=1, ch=KSUB)
                            bim = pwi.ap().rearrange("p (u ch) -> p u ch", u=1, ch=KSUB)
                            bre2, are2 = bass.broadcast_tensor_aps(bre, are)
                            bim2, aim2 = bass.broadcast_tensor_aps(bim, aim)
                            tmp = big32[:, 0:_w * KSUB].rearrange(
                                "p (l c) -> p l c", l=_w, c=KSUB)
                            cmb(e, dre, dim_, are2, aim2, bre2, bim2, tmp)
                            if _w < 8:
                                e.drain()
                                e.tensor_tensor(t0[:], pwr[:], pwr[:], ALU.mult)
                                e.tensor_tensor(t1b[:], pwi[:], pwi[:], ALU.mult)
                                e.tensor_tensor(t2b[:], pwr[:], pwi[:], ALU.mult)
                                e.drain()
                                e.tensor_tensor(pwr[:], t0[:], t1b[:], ALU.subtract)
                                e.tensor_scalar(pwi[:], t2b[:], 2.0, None,
                                                ALU.mult, ALU.bypass)
                            inc(e, pV, "pV")

                    @blk.vector
                    def _(e):
                        qr = qre.ap().rearrange("p (ch l) -> p l ch", ch=KSUB, l=128)
                        qi = qim.ap().rearrange("p (ch l) -> p l ch", ch=KSUB, l=128)
                        e.memset(qr[:, 0], 1.0)
                        e.memset(qi[:, 0], 0.0)
                        e.tensor_copy(pwr[:], lre[:])
                        e.tensor_copy(pwi[:], lim[:])
                        e.drain()
                        inc(e, pV, "pV")
                    for kstep in range(7):
                        w = 2 ** kstep

                        @blk.vector
                        def _(e, _w=w):
                            qr = qre.ap().rearrange("p (ch l) -> p l ch", ch=KSUB, l=128)
                            qi = qim.ap().rearrange("p (ch l) -> p l ch", ch=KSUB, l=128)
                            are, aim = qr[:, 0:_w], qi[:, 0:_w]
                            dre, dim_ = qr[:, _w:2 * _w], qi[:, _w:2 * _w]
                            bre = pwr.ap().rearrange("p (u ch) -> p u ch", u=1, ch=KSUB)
                            bim = pwi.ap().rearrange("p (u ch) -> p u ch", u=1, ch=KSUB)
                            bre2, are2 = bass.broadcast_tensor_aps(bre, are)
                            bim2, aim2 = bass.broadcast_tensor_aps(bim, aim)
                            tmp = big32[:, 0:_w * KSUB].rearrange(
                                "p (l c) -> p l c", l=_w, c=KSUB)
                            cmb(e, dre, dim_, are2, aim2, bre2, bim2, tmp)
                            if _w < 64:
                                e.drain()
                                e.tensor_tensor(t0[:], pwr[:], pwr[:], ALU.mult)
                                e.tensor_tensor(t1b[:], pwi[:], pwi[:], ALU.mult)
                                e.tensor_tensor(t2b[:], pwr[:], pwi[:], ALU.mult)
                                e.drain()
                                e.tensor_tensor(pwr[:], t0[:], t1b[:], ALU.subtract)
                                e.tensor_scalar(pwi[:], t2b[:], 2.0, None,
                                                ALU.mult, ALU.bypass)
                            inc(e, pV, "pV")

                    @blk.vector
                    def _(e):
                        e.drain()
                        e.tensor_scalar(pcmb.ap()[0:64], pre.ap()[0:64], 2.0, None,
                                        ALU.mult, ALU.bypass)
                        e.tensor_scalar(pcmb.ap()[64:128], pim.ap()[64:128], -2.0, None,
                                        ALU.mult, ALU.bypass)
                        e.tensor_copy(qcmb.ap()[0:64], qre.ap()[0:64])
                        e.tensor_copy(qcmb.ap()[64:128], qim.ap()[64:128])
                        inc(e, pV, "pV")
                    gv = cnt["pV"]
                    if PHASES == "B4q" and sub == 0:
                        bar(["vector"])
                        finish_dump(qcmb, 2048)
                        return nc
                    if PHASES == "B4p" and sub == 0:
                        bar(["vector"])
                        finish_dump(pcmb, 256)
                        return nc

                    kstg = xnh
                    for ch in range(KSUB):
                        bk = banks[6 + (ch // 4) % 2]
                        fo = (ch % 4) * 128

                        @blk.tensor
                        def _(e, _ch=ch, _bk=bk, _fo=fo, _gv=gv):
                            if _ch == 0:
                                e.wait_ge(pV, _gv)
                            if _ch % 4 == 0 and _ch >= 8:
                                e.wait_ge(pS, cnt["pS"] - 1)
                            mm = e.matmul(_bk[0:16, _fo:_fo + 128],
                                     pcmb[:, _ch * 16:(_ch + 1) * 16],
                                     qcmb[:, _ch * 128:(_ch + 1) * 128],
                                     start=True, stop=True)
                            if _ch % 4 == 3:
                                tinc(mm, pP, "pP")

                        if ch % 4 == 3:
                            @blk.vector
                            def _(e, _ch=ch, _bk=bk):
                                e.wait_ge(pP, cnt["pP"])
                                base = (_ch - 3) * 128
                                e.tensor_copy(kstg[0:16, base:base + 512], _bk[0:16, 0:512])
                                inc(e, pS, "pS")
                    bar(["vector"])

                    @blk.sync
                    def _(e, _sub=sub):
                        src = kstg[0:16, 0:KSUB * 128].rearrange("t2 (ch t1) -> t2 ch t1",
                                                      ch=KSUB, t1=128)
                        dstv = s_kern.rearrange("(sb ch) (t2 t1) -> sb t2 ch t1",
                                                sb=CH_SL // KSUB, ch=KSUB,
                                                t2=16, t1=128)[_sub]
                        dma(e, qS, "qS", dstv, src)
                        e.wait_ge(qS, cnt["qS"])
                        inc(e, G, "G")
                bar(["sync"])

                @blk.sync
                def _(e):
                    for s in range(8):
                        srcv = s_kern.rearrange("(hgp s) (t2 t1) -> s t2 hgp t1",
                                                hgp=NHGP, s=8, t2=16, t1=128)[s]
                        dma(e, qS, "qS",
                            kstr.ap()[16 * s:16 * s + 16].rearrange(
                                "t2 (hgp t1) -> t2 hgp t1", hgp=NHGP, t1=128),
                            srcv)
                    e.wait_ge(qS, cnt["qS"])
                    inc(e, G, "G")
                g1 = cnt["G"]
                if PHASES == "B4w":
                    finish_dump(kstr, 1024)
                    return nc

                for hgp in range(NHGP):
                    bk = banks[2 + hgp % 2]

                    @blk.tensor
                    def _(e, _h=hgp, _bk=bk, _g=g1):
                        if _h == 0:
                            e.wait_ge(G, _g)
                        if _h >= 2:
                            e.wait_ge(pV, cnt["pV"] - 1)
                        mm = e.matmul(_bk[:, 0:512], kstr[:, _h * 128:(_h + 1) * 128],
                                 b1rsb[:], start=True, stop=True)
                        tinc(mm, pP, "pP")

                    @blk.vector
                    def _(e, _h=hgp, _bk=bk):
                        e.wait_ge(pP, cnt["pP"])
                        dstv = Ssl.ap().rearrange(
                            "p (cp m2 hgp s) -> p hgp s m2 cp",
                            cp=2, m2=M2, hgp=NHGP, s=8)[:, _h]
                        e.tensor_copy(dstv, _bk[:, 0:512].rearrange(
                            "p (s m2 cp) -> p s m2 cp", s=8, m2=M2, cp=2))
                        inc(e, pV, "pV")
                bar(["vector"])

                if PHASES == "B4r":
                    finish_dump(Ssl)
                    return nc
                stage2(Ssl, KFsl, om_ld)

                ph_ld = load_tables("Ph")

                if PHASES == "B4s":
                    finish_dump(KFsl)
                    return nc
                half = NPL // 2

                @blk.vector
                def _(e):
                    zr, zi = Zsl[:, 0:half], Zsl[:, NPL:NPL + half]
                    kr, ki = KFsl[:, 0:half], KFsl[:, NPL:NPL + half]
                    tt = big32[:, 0:half]
                    e.tensor_tensor(tt, zi, ki, ALU.mult)
                    e.tensor_tensor(WmT[:, 0:half], zr, kr, ALU.mult)
                    e.drain()
                    e.tensor_tensor(WmT[:, 0:half], WmT[:, 0:half], tt, ALU.subtract)
                    e.drain()
                    e.tensor_tensor(tt, zi, kr, ALU.mult)
                    e.tensor_tensor(WmT[:, NPL:NPL + half], zr, ki, ALU.mult)
                    e.drain()
                    e.tensor_tensor(WmT[:, NPL:NPL + half], WmT[:, NPL:NPL + half],
                                    tt, ALU.add)
                    inc(e, pV, "pV")

                @blk.gpsimd
                def _(e):
                    zr, zi = Zsl[:, half:NPL], Zsl[:, NPL + half:2 * NPL]
                    kr, ki = KFsl[:, half:NPL], KFsl[:, NPL + half:2 * NPL]
                    tt = qre[:, 0:half]
                    e.tensor_tensor(tt, zi, ki, ALU.mult)
                    e.tensor_tensor(WmT[:, half:NPL], zr, kr, ALU.mult)
                    e.drain()
                    e.tensor_tensor(WmT[:, half:NPL], WmT[:, half:NPL], tt, ALU.subtract)
                    e.drain()
                    e.tensor_tensor(tt, zi, kr, ALU.mult)
                    e.tensor_tensor(WmT[:, NPL + half:2 * NPL], zr, ki, ALU.mult)
                    e.drain()
                    e.tensor_tensor(WmT[:, NPL + half:2 * NPL],
                                    WmT[:, NPL + half:2 * NPL], tt, ALU.add)
                    inc(e, pG2, "pG2")

                for tix in range(32):
                    @blk.gpsimd
                    def _(e, _ti=tix):
                        e.tensor_tensor(
                            xnds[:, _ti * CH_SL:(_ti + 1) * CH_SL],
                            TB1[:, _ti * 1024 + h0:_ti * 1024 + h0 + CH_SL],
                            dbc[:, h0:h0 + CH_SL], ALU.mult)
                        if _ti == 31:
                            inc(e, pG2, "pG2")
                bar(["vector", "gpsimd"])

                if PHASES == "B5":
                    finish_dump(WmT)
                    return nc
                @blk.tensor
                def _(e, _t=ph_ld):
                    e.wait_ge(G, _t)
                for m2 in range(M2):
                    bkr = banks[4 + (m2 % 2) * 2]
                    bki = banks[5 + (m2 % 2) * 2]

                    @blk.tensor
                    def _(e, _m2=m2, _r=bkr, _i=bki):
                        if _m2 >= 2:
                            e.wait_ge(pS, cnt["pS"] - 1)
                        wre = WmT[:, _m2 * CH_SL:(_m2 + 1) * CH_SL]
                        wim = WmT[:, NPL + _m2 * CH_SL:NPL + (_m2 + 1) * CH_SL]
                        e.matmul(_r[:, 0:CH_SL], om(0, _m2), wre, start=True, stop=False)
                        mm1 = e.matmul(_r[:, 0:CH_SL], om(2, _m2), wim, start=False, stop=True)
                        tinc(mm1, pP, "pP")
                        e.matmul(_i[:, 0:CH_SL], om(1, _m2), wre, start=True, stop=False)
                        mm2 = e.matmul(_i[:, 0:CH_SL], om(0, _m2), wim, start=False, stop=True)
                        tinc(mm2, pP, "pP")

                    @blk.vector
                    def _(e, _m2=m2, _r=bkr, _i=bki):
                        e.wait_ge(pP, cnt["pP"])
                        for c, bk in ((0, _r), (1, _i)):
                            dstv = Zsl.ap().rearrange(
                                "p (c hsg hssub m2) -> p c m2 hsg hssub",
                                c=2, hsg=NHSG, hssub=4, m2=M2)[:, c, _m2]
                            e.tensor_copy(dstv, bk[:, 0:CH_SL].rearrange(
                                "p (hsg hssub) -> p hsg hssub", hsg=NHSG, hssub=4))
                        inc(e, pS, "pS")
                bar(["vector"])

                if PHASES == "B6":
                    finish_dump(Zsl)
                    return nc
                ust_ready = []
                psy_done = []
                add_done = []
                gelu_done = []
                for hsg in range(NHSG):
                    bky = banks[2 + hsg % 2]
                    go = (hsg % 2) * 128
                    go2 = 256 + (hsg % 2) * 128

                    @blk.tensor
                    def _(e, _hsg=hsg):
                        if _hsg >= 1:
                            e.wait_ge(pS, ust_ready[_hsg - 1])
                        e.matmul(psA[:, 0:128], Zsl[:, _hsg * 128:(_hsg + 1) * 128],
                                 ident[:], start=True, stop=True)
                        mm = e.matmul(psA[:, 128:256],
                                 Zsl[:, NPL + _hsg * 128:NPL + (_hsg + 1) * 128],
                                 ident[:], start=True, stop=True)
                        tinc(mm, pP, "pP")

                    @blk.vector
                    def _(e):
                        e.wait_ge(pP, cnt["pP"])
                        e.tensor_copy(ust0[:], psA[:, 0:128])
                        e.tensor_copy(ust1[:], psA[:, 128:256])
                        inc(e, pS, "pS")
                    ust_ready.append(cnt["pS"])

                    @blk.tensor
                    def _(e, _y=bky, _hsg=hsg, _gv=ust_ready[-1]):
                        e.wait_ge(pS, _gv)
                        if _hsg >= 2:
                            e.wait_ge(pS, psy_done[_hsg - 2])
                        e.matmul(_y[:, 0:128], ust0[:], th0sb[:], start=True, stop=False)
                        mm = e.matmul(_y[:, 0:128], ust1[:], th1sb[:], start=False, stop=True)
                        tinc(mm, pP, "pP")

                    @blk.vector
                    def _(e, _hsg=hsg, _y=bky, _go2=go2):
                        e.wait_ge(pP, cnt["pP"])
                        if _hsg >= 2:
                            e.wait_ge(pV, add_done[_hsg - 2])
                        e.tensor_copy(gtmp[:, _go2:_go2 + 128], _y[:, 0:128])
                        inc(e, pS, "pS")
                    psy_done.append(cnt["pS"])

                    @blk.vector
                    def _(e, _hsg=hsg, _go=go, _go2=go2, _pd=psy_done[-1]):
                        e.wait_ge(pS, _pd)
                        av = xnds.ap().rearrange(
                            "p (c t2 hsg hssub) -> p hsg hssub t2 c",
                            c=2, t2=16, hsg=NHSG, hssub=4)[:, _hsg]
                        e.tensor_tensor(
                            gtmp[:, _go:_go + 128].rearrange(
                                "p (hssub t2 c) -> p hssub t2 c", hssub=4, t2=16, c=2),
                            gtmp[:, _go2:_go2 + 128].rearrange(
                                "p (hssub t2 c) -> p hssub t2 c", hssub=4, t2=16, c=2),
                            av, ALU.add)
                        inc(e, pV, "pV")
                    add_done.append(cnt["pV"])

                    @blk.scalar
                    def _(e, _hsg=hsg, _go=go, _sl=sl, _ad=add_done[-1]):
                        e.wait_ge(pV, _ad)
                        dstv = TB1.ap().rearrange(
                            "p (c t2 sl hsg hssub) -> p sl hsg hssub t2 c",
                            c=2, t2=16, sl=N_SL, hsg=NHSG, hssub=4)[:, _sl, _hsg]
                        e.activation(dstv,
                                     gtmp[:, _go:_go + 128].rearrange(
                                         "p (hssub t2 c) -> p hssub t2 c",
                                         hssub=4, t2=16, c=2),
                                     AF.Gelu_apprx_tanh)
                        inc(e, pS, "pS")
                    gelu_done.append(cnt["pS"])
                bar(["scalar"])
        if PHASES in ("B1", "B"):
            finish_dump()
            return nc
        if DEBUG:
            @blk.sync
            def _(e):
                dma(e, qS, "qS", dbg_y2, TB1[:])
                e.wait_ge(qS, cnt["qS"])
            bar(["sync"])

        NT_TG = TGT // 128

        def ln_pass(src_ap, stat, dst_fn, wait_pp=None):
            bn6, bn2, mrr = stat

            @blk.vector
            def _(e):
                e.bn_stats(bn6[:, 0:6], src_ap[:, 0:512])
                e.bn_stats(bn6[:, 6:12], src_ap[:, 512:1024])
                e.drain()
                e.bn_aggr(bn2[:], bn6[:])
                inc(e, pV, "pV")

            @blk.gpsimd
            def _(e):
                e.wait_ge(pV, cnt["pV"])
                e.tensor_scalar(mrr[:, 1:2], bn2[:, 1:2], 1e-5, None, ALU.add, ALU.bypass)
                e.drain()
                e.tensor_tensor(mrr[:, 0:1], mrr[:, 1:2], nhalf[:, 0:1], ALU.pow)
                inc(e, pG2, "pG2")

            @blk.vector
            def _(e):
                e.wait_ge(pG2, cnt["pG2"])
                if wait_pp is not None:
                    e.wait_ge(pP, wait_pp)
                dst_fn(e, bn2[:, 0:1], mrr[:, 0:1])
                inc(e, pV, "pV")

        with contextlib.ExitStack() as st_c:
            y2T = st_c.enter_context(nc.sbuf_tensor("y2T", [128, 8 * TGT], F16))
            attf = st_c.enter_context(nc.sbuf_tensor("attf", [128, 2 * TGT], F16))
            attm = st_c.enter_context(nc.sbuf_tensor("attm", [128, NT_TG * 1024], F16))
            wcA = st_c.enter_context(nc.sbuf_tensor("wcA", [128, 2048], F16))
            wcB = st_c.enter_context(nc.sbuf_tensor("wcB", [128, 2048], F16))
            xtl = st_c.enter_context(nc.sbuf_tensor("xtl", [128, H], F16))
            xtl2 = st_c.enter_context(nc.sbuf_tensor("xtl2", [128, H], F16))
            evt = st_c.enter_context(nc.sbuf_tensor("evt", [128, TGT], F16))
            evt2 = st_c.enter_context(nc.sbuf_tensor("evt2", [128, TGT], F16))
            cb6 = st_c.enter_context(nc.sbuf_tensor("cb6", [128, 12], F32))
            cb2 = st_c.enter_context(nc.sbuf_tensor("cb2", [128, 2], F32))
            cmr = st_c.enter_context(nc.sbuf_tensor("cmr", [128, 2], F32))
            ctt = st_c.enter_context(nc.sbuf_tensor("ctt", [128, H], F16))
            stat = (cb6, cb2, cmr)

            @blk.gpsimd
            def _(e):
                e.wait_ge(qG, cnt.get("wcast_qg", cnt["qG"]))
                inc(e, pG2, "pG2")
                for _nm in ("wout", "w1", "w2", "w3", "h1", "h2"):
                    wcast_tgt[_nm] = cnt["pG2"]

            @blk.sync
            def _(e):
                e.wait_ge(pG2, wcast_tgt["wout"])
            woutv = wout16.rearrange("(kc p) o -> p kc o", kc=8, p=128)
            for g in range(N_TG):
                for tt in range(NT_TG):
                    for hc in range(8):
                        bk = banks[(tt * 8 + hc) % 2]
                        tia = g * NT_TG + tt

                        @blk.tensor
                        def _(e, _t=tt, _hc=hc, _bk=bk, _tia=tia, _g=g):
                            if _t + _hc > 0 or _g > 0:
                                e.wait_ge(pS, cnt["pS"] - 1)
                            mm = e.matmul(_bk[:, 0:128],
                                     TB1[:, _tia * 1024 + _hc * 128:
                                         _tia * 1024 + (_hc + 1) * 128],
                                     ident[:], start=True, stop=True)
                            tinc(mm, pP, "pP")

                        @blk.vector
                        def _(e, _t=tt, _hc=hc, _bk=bk):
                            e.wait_ge(pP, cnt["pP"])
                            e.tensor_copy(y2T[:, _hc * TGT + _t * 128:
                                       _hc * TGT + (_t + 1) * 128], _bk[:, 0:128])
                            inc(e, pS, "pS")
                bar(["vector"])

                for oc in range(8):
                    wb = wcA if oc % 2 == 0 else wcB

                    wsem = wA if oc % 2 == 0 else wB
                    wnm = "wA" if oc % 2 == 0 else "wB"

                    @blk.sync
                    def _(e, _oc=oc, _wb=wb, _ws=wsem, _wn=wnm):
                        if _oc >= 2:
                            e.wait_ge(cW, cnt.get("cW", 0) - 1)
                        dma(e, _ws, _wn, _wb[:, 0:1024].rearrange(
                            "p (kc o) -> p kc o", kc=8, o=128),
                            woutv[:, :, _oc * 128:(_oc + 1) * 128])
                        dma(e, _ws, _wn, _wb[:, 1024:2048].rearrange(
                            "p (kc o) -> p kc o", kc=8, o=128),
                            woutv[:, :, (_oc + 8) * 128:(_oc + 9) * 128])
                    wt = cnt[wnm]

                    @blk.tensor
                    def _(e, _oc=oc, _wb=wb, _wt=wt, _ws=wsem):
                        e.wait_ge(_ws, _wt)
                        if _oc >= 1:
                            e.wait_ge(pV, cnt["pV"] - 0)
                        for kc in range(8):
                            mm = e.matmul(psE[:, 0:TGT], _wb[:, kc * 128:(kc + 1) * 128],
                                     y2T[:, kc * TGT:(kc + 1) * TGT],
                                     start=(kc == 0), stop=(kc == 7))
                        tinc(mm, pP, "pP")
                        for kc in range(8):
                            mm = e.matmul(psF[:, 0:TGT],
                                     _wb[:, 1024 + kc * 128:1024 + (kc + 1) * 128],
                                     y2T[:, kc * TGT:(kc + 1) * TGT],
                                     start=(kc == 0), stop=(kc == 7))
                        tinc(mm, pP, "pP")

                    @blk.scalar
                    def _(e, _oc=oc):
                        e.wait_ge(pP, cnt["pP"])
                        inc(e, cW, "cW")
                        e.activation(evt[:], psE[:, 0:TGT], AF.Identity,
                                     bias=bosb[:, _oc:_oc + 1])
                        e.activation(evt2[:], psF[:, 0:TGT], AF.Sigmoid,
                                     bias=bosb[:, 8 + _oc:9 + _oc])
                        inc(e, pS, "pS")

                    @blk.vector
                    def _(e, _oc=oc):
                        e.wait_ge(pS, cnt["pS"])
                        e.tensor_tensor(attf[:, 0:TGT], evt[:], evt2[:], ALU.mult)
                        inc(e, pV, "pV")
                    gva = cnt["pV"]

                    for tt in range(NT_TG):
                        bk = banks[2 + tt % 2]

                        @blk.tensor
                        def _(e, _oc=oc, _tt=tt, _bk=bk, _gv=gva):
                            e.wait_ge(pV, _gv)
                            if _oc + _tt > 0:
                                e.wait_ge(pS, cnt["pS"] - 1)
                            mm = e.matmul(_bk[:, 0:128], attf[:, _tt * 128:(_tt + 1) * 128],
                                     ident[:], start=True, stop=True)
                            tinc(mm, pP, "pP")

                        @blk.vector
                        def _(e, _oc=oc, _tt=tt, _bk=bk):
                            e.wait_ge(pP, cnt["pP"])
                            e.tensor_copy(attm[:, _tt * 1024 + _oc * 128:
                                        _tt * 1024 + (_oc + 1) * 128], _bk[:, 0:128])
                            inc(e, pS, "pS")
                bar(["vector"])

                if g == 0:
                    @blk.sync
                    def _(e):
                        e.wait_ge(pG2, cnt['xf16_ready'])
                for tt in range(NT_TG):
                    tia = g * NT_TG + tt
                    xb2 = xtl if tt % 2 == 0 else xtl2
                    xsem = xA if tt % 2 == 0 else xB
                    xnm = "xA" if tt % 2 == 0 else "xB"

                    @blk.sync
                    def _(e, _tia=tia, _xb=xb2, _tt=tt, _xs=xsem, _xn=xnm):
                        if _tt >= 2 or g > 0:
                            e.wait_ge(pV, cnt["pV"] - 2)
                        dma(e, _xs, _xn, _xb[:],
                            xf16.rearrange("(ti t1) h -> ti t1 h", t1=128)[_tia])
                    xt_t = cnt[xnm]

                    @blk.vector
                    def _(e, _t=xt_t, _xs=xsem):
                        e.wait_ge(_xs, _t)

                    src = attm[:, tt * 1024:(tt + 1) * 1024]

                    def dst_fn(e, m_ap, r_ap, _tia=tia, _src=src, _xb=xb2):
                        e.tensor_scalar(ctt[:], _src, m_ap, r_ap,
                                        ALU.subtract, ALU.mult)
                        e.drain()
                        e.tensor_tensor(TB1[:, _tia * 1024:(_tia + 1) * 1024],
                                        ctt[:], _xb[:], ALU.add)
                    ln_pass(src, stat, dst_fn)
                bar(["vector"])
        if DEBUG:
            @blk.sync
            def _(e):
                dma(e, qS, "qS", dbg_x1, TB1[:])
                e.wait_ge(qS, cnt["qS"])
            bar(["sync"])

        if PHASES == "Cpre":
            finish_dump()
            return nc

        with contextlib.ExitStack() as st_f:
            x1n = st_f.enter_context(nc.sbuf_tensor("x1n", [128, H], F16))
            x1nT = st_f.enter_context(nc.sbuf_tensor("x1nT", [128, 8 * TGT], F16))
            HID = st_f.enter_context(nc.sbuf_tensor("HID", [128, 32 * TGT], F16))
            HID2 = st_f.enter_context(nc.sbuf_tensor("HID2", [128, 32 * TGT], F16))
            f3m = st_f.enter_context(nc.sbuf_tensor("f3m", [128, NT_TG * 1024], F16))
            whA = st_f.enter_context(nc.sbuf_tensor("whA", [128, 4096], F16))
            whB = st_f.enter_context(nc.sbuf_tensor("whB", [128, 4096], F16))
            w12A = st_f.enter_context(nc.sbuf_tensor("w12A", [128, 2048], F16))
            w12B = st_f.enter_context(nc.sbuf_tensor("w12B", [128, 2048], F16))
            outb = st_f.enter_context(nc.sbuf_tensor("outb", [128, H], F32))
            fb6 = st_f.enter_context(nc.sbuf_tensor("fb6", [128, 12], F32))
            fb2 = st_f.enter_context(nc.sbuf_tensor("fb2", [128, 2], F32))
            fmr = st_f.enter_context(nc.sbuf_tensor("fmr", [128, 2], F32))
            stat2 = (fb6, fb2, fmr)
            F3 = x1nT
            ftt = x1n

            @blk.sync
            def _(e):
                e.wait_ge(pG2, wcast_tgt["h2"])
            w1v = w116.rearrange("(kc p) o -> p kc o", kc=8, p=128)
            w2v = w216.rearrange("(kc p) o -> p kc o", kc=8, p=128)
            h1v = h116.rearrange("(kc p) o -> p kc o", kc=32, p=128)
            h2v = h216.rearrange("(kc p) o -> p kc o", kc=32, p=128)
            w3v = w316.rearrange("(kc p) o -> p kc o", kc=32, p=128)
            out_t = 0

            for g in range(N_TG):
                for tt in range(NT_TG):
                    tia = g * NT_TG + tt
                    src = TB1[:, tia * 1024:(tia + 1) * 1024]

                    def dst_fn(e, m_ap, r_ap, _src=src):
                        e.tensor_scalar(x1n[:], _src, m_ap, r_ap,
                                        ALU.subtract, ALU.mult)
                    ln_pass(src, stat2, dst_fn,
                            wait_pp=(cnt["pP"] if (tt > 0 or g > 0) else None))
                    gvb = cnt["pV"]
                    for hc in range(8):
                        bk = banks[hc % 2]

                        @blk.tensor
                        def _(e, _tt=tt, _hc=hc, _bk=bk, _gv=gvb):
                            e.wait_ge(pV, _gv)
                            if _tt + _hc > 0 or g > 0:
                                e.wait_ge(pS, cnt["pS"] - 1)
                            mm = e.matmul(_bk[:, 0:128], x1n[:, _hc * 128:(_hc + 1) * 128],
                                     ident[:], start=True, stop=True)
                            tinc(mm, pP, "pP")

                        @blk.vector
                        def _(e, _tt=tt, _hc=hc, _bk=bk):
                            e.wait_ge(pP, cnt["pP"])
                            e.tensor_copy(x1nT[:, _hc * TGT + _tt * 128:
                                        _hc * TGT + (_tt + 1) * 128], _bk[:, 0:128])
                            inc(e, pS, "pS")
                bar(["vector"])

                if PHASES == "C45" and g == 0:
                    finish_dump(x1nT, 4096)
                    return nc
                for ht in range(32):
                    wb = w12A if ht % 2 == 0 else w12B

                    wsem = wA if ht % 2 == 0 else wB
                    wnm = "wA" if ht % 2 == 0 else "wB"

                    @blk.sync
                    def _(e, _ht=ht, _wb=wb, _ws=wsem, _wn=wnm):
                        if _ht >= 2:
                            e.wait_ge(cW, cnt["cW"] - 1)
                        dma(e, _ws, _wn, _wb[:, 0:1024].rearrange(
                            "p (kc o) -> p kc o", kc=8, o=128),
                            w1v[:, :, _ht * 128:(_ht + 1) * 128])
                        dma(e, _ws, _wn, _wb[:, 1024:2048].rearrange(
                            "p (kc o) -> p kc o", kc=8, o=128),
                            w2v[:, :, _ht * 128:(_ht + 1) * 128])
                    wt = cnt[wnm]

                    @blk.tensor
                    def _(e, _ht=ht, _wb=wb, _wt=wt, _ws=wsem):
                        e.wait_ge(_ws, _wt)
                        if _ht >= 1:
                            e.wait_ge(pV, cnt["pV"])
                        for kc in range(8):
                            mm = e.matmul(psE[:, 0:TGT], _wb[:, kc * 128:(kc + 1) * 128],
                                     x1nT[:, kc * TGT:(kc + 1) * TGT],
                                     start=(kc == 0), stop=(kc == 7))
                        tinc(mm, pP, "pP")
                        for kc in range(8):
                            mm = e.matmul(psF[:, 0:TGT],
                                     _wb[:, 1024 + kc * 128:1024 + (kc + 1) * 128],
                                     x1nT[:, kc * TGT:(kc + 1) * TGT],
                                     start=(kc == 0), stop=(kc == 7))
                        tinc(mm, pP, "pP")

                    @blk.scalar
                    def _(e, _ht=ht):
                        e.wait_ge(pP, cnt["pP"])
                        inc(e, cW, "cW")
                        e.activation(f3m[:, 0:512], psE[:, 0:TGT], AF.Silu,
                                     scale=s1sb[:, _ht:_ht + 1])
                        e.activation(f3m[:, 512:1024], psF[:, 0:TGT], AF.Identity,
                                     scale=s2sb[:, _ht:_ht + 1])
                        inc(e, pS, "pS")

                    @blk.vector
                    def _(e, _ht=ht):
                        e.wait_ge(pS, cnt["pS"])
                        e.tensor_tensor(HID[:, _ht * TGT:(_ht + 1) * TGT],
                                        f3m[:, 0:512], f3m[:, 512:1024], ALU.mult)
                        inc(e, pV, "pV")
                bar(["vector"])

                if PHASES == "C6a" and g == 0:
                    finish_dump(HID, 4096)
                    return nc
                for stagei, (wv, scl, srcbuf, dstbuf) in enumerate(
                        ((h1v, sh1sb, HID, HID2), (h2v, sh2sb, HID2, HID))):
                    for ot in range(32):
                        wb = whA if ot % 2 == 0 else whB

                        wsem = wA if ot % 2 == 0 else wB
                        wnm = "wA" if ot % 2 == 0 else "wB"

                        @blk.sync
                        def _(e, _ot=ot, _wb=wb, _wv=wv, _ws=wsem, _wn=wnm):
                            if _ot >= 2 or stagei > 0 or g > 0:
                                e.wait_ge(cW, cnt["cW"] - 1)
                            dma(e, _ws, _wn, _wb[:].rearrange(
                                "p (kc o) -> p kc o", kc=32, o=128),
                                _wv[:, :, _ot * 128:(_ot + 1) * 128])
                        wt = cnt[wnm]

                        pbk = psE if ot % 2 == 0 else psF

                        @blk.tensor
                        def _(e, _ot=ot, _wb=wb, _wt=wt, _src=srcbuf, _pb=pbk,
                              _ws=wsem):
                            e.wait_ge(_ws, _wt)
                            if _ot >= 2:
                                e.wait_ge(pS, cnt["pS"] - 1)
                            for kc in range(32):
                                mm = e.matmul(_pb[:, 0:TGT], _wb[:, kc * 128:(kc + 1) * 128],
                                         _src[:, kc * TGT:(kc + 1) * TGT],
                                         start=(kc == 0), stop=(kc == 31))
                            tinc(mm, pP, "pP")

                        @blk.scalar
                        def _(e, _ot=ot, _dst=dstbuf, _scl=scl, _pb=pbk):
                            e.wait_ge(pP, cnt["pP"])
                            inc(e, cW, "cW")
                            e.activation(_dst[:, _ot * TGT:(_ot + 1) * TGT],
                                         _pb[:, 0:TGT], AF.Gelu,
                                         scale=_scl[:, _ot:_ot + 1])
                            inc(e, pS, "pS")
                    bar(["scalar"])

                if PHASES == "C6b" and g == 0:
                    finish_dump(HID, 4096)
                    return nc
                for dc in range(8):
                    wb = whA if dc % 2 == 0 else whB

                    wsem = wA if dc % 2 == 0 else wB
                    wnm = "wA" if dc % 2 == 0 else "wB"

                    @blk.sync
                    def _(e, _dc=dc, _wb=wb, _ws=wsem, _wn=wnm):
                        e.wait_ge(cW, cnt["cW"] - 1)
                        dma(e, _ws, _wn, _wb[:].rearrange(
                            "p (kc o) -> p kc o", kc=32, o=128),
                            w3v[:, :, _dc * 128:(_dc + 1) * 128])
                    wt = cnt[wnm]

                    pbk = psE if dc % 2 == 0 else psF

                    @blk.tensor
                    def _(e, _dc=dc, _wb=wb, _wt=wt, _pb=pbk, _ws=wsem):
                        e.wait_ge(_ws, _wt)
                        if _dc >= 2:
                            e.wait_ge(pS, cnt["pS"] - 1)
                        for kc in range(32):
                            mm = e.matmul(_pb[:, 0:TGT], _wb[:, kc * 128:(kc + 1) * 128],
                                     HID[:, kc * TGT:(kc + 1) * TGT],
                                     start=(kc == 0), stop=(kc == 31))
                        tinc(mm, pP, "pP")

                    @blk.scalar
                    def _(e, _dc=dc, _pb=pbk):
                        e.wait_ge(pP, cnt["pP"])
                        inc(e, cW, "cW")
                        e.activation(F3[:, _dc * TGT:(_dc + 1) * TGT], _pb[:, 0:TGT],
                                     AF.Identity, scale=s3sb[:, _dc:_dc + 1])
                        inc(e, pS, "pS")
                bar(["scalar"])

                if PHASES == "C6c" and g == 0:
                    finish_dump(F3, 4096)
                    return nc
                for tt in range(NT_TG):
                    for dc in range(8):
                        bk = banks[dc % 2]

                        @blk.tensor
                        def _(e, _tt=tt, _dc=dc, _bk=bk):
                            if _tt + _dc > 0 or g > 0:
                                e.wait_ge(pS, cnt["pS"] - 1)
                            mm = e.matmul(_bk[:, 0:128],
                                     F3[:, _dc * TGT + _tt * 128:
                                        _dc * TGT + (_tt + 1) * 128],
                                     ident[:], start=True, stop=True)
                            tinc(mm, pP, "pP")

                        @blk.vector
                        def _(e, _tt=tt, _dc=dc, _bk=bk):
                            e.wait_ge(pP, cnt["pP"])
                            e.tensor_copy(f3m[:, _tt * 1024 + _dc * 128:
                                       _tt * 1024 + (_dc + 1) * 128], _bk[:, 0:128])
                            inc(e, pS, "pS")
                bar(["vector"])
                for tt in range(NT_TG):
                    tia = g * NT_TG + tt
                    src = f3m[:, tt * 1024:(tt + 1) * 1024]
                    prev_out = out_t

                    def dst_fn(e, m_ap, r_ap, _src=src, _tia=tia, _po=prev_out):
                        if _po > 0:
                            e.wait_ge(qS, _po)
                        e.tensor_scalar(ftt[:], _src, m_ap, r_ap,
                                        ALU.subtract, ALU.mult)
                        e.drain()
                        e.tensor_tensor(outb[:], ftt[:],
                                        TB1[:, _tia * 1024:(_tia + 1) * 1024], ALU.add)
                    ln_pass(src, stat2, dst_fn)
                    gvv = cnt["pV"]

                    @blk.sync
                    def _(e, _tia=tia, _gv=gvv):
                        e.wait_ge(pV, _gv)
                        dma(e, qS, "qS", y_out.rearrange(
                            "b (t2 t1) h -> (b t2) t1 h", t1=128)[_tia], outb[:])
                    out_t = cnt["qS"]
            @blk.sync
            def _(e):
                e.wait_ge(qS, cnt["qS"])
            bar(["sync"])
    return nc


_tables = _host_tables()
_nc_cache = {}


def kernel(L0_attn_x, L0_attn_log_dt, L0_attn_A_real_log, L0_attn_A_imag,
           L0_attn_C_real, L0_attn_C_imag, L0_attn_D, L0_attn_W_out,
           L0_attn_b_out, L0_ffn_w1_w, L0_ffn_w1_s, L0_ffn_w2_w, L0_ffn_w2_s,
           L0_ffn_h1_w, L0_ffn_h1_s, L0_ffn_h2_w, L0_ffn_h2_s, L0_ffn_w3_w,
           L0_ffn_w3_s):
    f32 = np.float32
    c = np.ascontiguousarray
    common = dict(
        log_dt=c(np.asarray(L0_attn_log_dt, f32).reshape(1, H)),
        arlT=c(np.asarray(L0_attn_A_real_log, f32).T),
        aimT=c(np.asarray(L0_attn_A_imag, f32).T),
        creT=c(np.asarray(L0_attn_C_real, f32).T),
        cimT=c(np.asarray(L0_attn_C_imag, f32).T),
        d_vec=c(np.asarray(L0_attn_D, f32).reshape(1, H)),
        wout=c(np.asarray(L0_attn_W_out, f32)),
        bout=c(np.asarray(L0_attn_b_out, f32).reshape(1, 2 * H)),
        w1T=c(np.asarray(L0_ffn_w1_w, f32).T),
        w2T=c(np.asarray(L0_ffn_w2_w, f32).T),
        h1T=c(np.asarray(L0_ffn_h1_w, f32).T),
        h2T=c(np.asarray(L0_ffn_h2_w, f32).T),
        w3T=c(np.asarray(L0_ffn_w3_w, f32).T),
        s1=c(np.asarray(L0_ffn_w1_s, f32).reshape(1, DH)),
        s2=c(np.asarray(L0_ffn_w2_s, f32).reshape(1, DH)),
        sh1=c(np.asarray(L0_ffn_h1_s, f32).reshape(1, DH)),
        sh2=c(np.asarray(L0_ffn_h2_s, f32).reshape(1, DH)),
        s3=c(np.asarray(L0_ffn_w3_s, f32).reshape(1, H)),
        **_tables,
    )
    x = np.asarray(L0_attn_x, f32)
    in_maps = []
    for core in range(8):
        m = dict(common)
        m["x"] = c(x[2 * core:2 * core + 2])
        in_maps.append(m)
    if "nc" not in _nc_cache:
        _nc_cache["nc"] = build_nc()
    res = run_bass_kernel_spmd(_nc_cache["nc"], in_maps, core_ids=list(range(8)))
    if DEBUG:
        _dbg.clear()
        _dbg.update({k: v for k, v in res.results[0].items() if k.startswith("dbg")})
    out = np.concatenate([res.results[i]["y"] for i in range(8)], axis=0)
    return out
